# version 06
# baseline (speedup 1.0000x reference)
import numpy as np
import ml_dtypes
from contextlib import ExitStack
import concourse.bass as bass
import concourse.mybir as mybir
from concourse.bass_utils import run_bass_kernel_spmd

F32 = mybir.dt.float32
BF16 = mybir.dt.bfloat16
I32 = mybir.dt.int32
AF = mybir.ActivationFunctionType
ALU = mybir.AluOpType
AX = mybir.AxisListType.X

P = 128
D = 2048
T = 2048
ND = D // P
NTT = T // P
TC = 512
NTC = T // TC
IFF = 8192
NIC = IFF // P
WIN = 512
NW = T // WIN
NH = 16
DH = 128
EPS_LN = 1e-5
EPS_RMS = 1e-6
SCALE_QK = 1.0 / (DH ** 0.5)


class Emit:

    def __init__(self, nc, stack):
        self.nc = nc
        self.stack = stack
        self.sem = {}
        self.val = {}

    def new_sem(self, name):
        h = self.stack.enter_context(self.nc.semaphore(name))
        self.sem[name] = h
        self.val[name] = 0
        return h

    def roll(self):
        pass

    def s(self, name):
        return self.sem[name]


class Ser:

    def __init__(self, em, tag):
        self.em = em
        self.sc = em.new_sem(tag + "_c")
        self.sd = em.new_sem(tag + "_d")
        self.cn = tag + "_c"
        self.dn = tag + "_d"
        self.ops = []

    def add(self, eng, fn, dma=False):
        self.ops.append((eng, dma, fn))

    def emit(self, blk, extra_wait=None):
        plan = {}
        ncc, ndd = 0, 0
        for eng, dma, fn in self.ops:
            plan.setdefault(eng, []).append((ncc, ndd, dma, fn))
            if dma:
                ndd += 1
            else:
                ncc += 1
        self.em.val[self.cn] += ncc
        self.em.val[self.dn] += 16 * ndd

        def mk(items):
            def body(eng):
                if extra_wait is not None:
                    eng.wait_ge(extra_wait[0], extra_wait[1])
                for (c_b, d_b, dma, fn) in items:
                    eng.wait_ge(self.sc, c_b)
                    eng.wait_ge(self.sd, 16 * d_b)
                    inst = fn(eng)
                    inst.then_inc(self.sd if dma else self.sc, 16 if dma else 1)
            return body

        for eng_name, items in plan.items():
            getattr(blk, eng_name)(mk(items))


def make_identities(em, st, nc, pfx=""):
    io_a = st.enter_context(nc.sbuf_tensor(pfx + "io_a", [P, P], I32))
    io_b = st.enter_context(nc.sbuf_tensor(pfx + "io_b", [P, P], I32))
    ident_f = st.enter_context(nc.sbuf_tensor(pfx + "ident_f", [P, P], F32))
    ident_b = st.enter_context(nc.sbuf_tensor(pfx + "ident_b", [P, P], BF16))
    ones1_f = st.enter_context(nc.sbuf_tensor(pfx + "ones1_f", [1, P], F32))
    ones_bf = st.enter_context(nc.sbuf_tensor(pfx + "ones_bf", [P, 1], BF16))
    cst = st.enter_context(nc.sbuf_tensor(pfx + "cst", [P, 4], F32))
    s_id = em.new_sem(pfx + "s_id")
    em.sem["s_id"] = s_id
    em.val["s_id"] = 0
    with nc.Block() as blk:
        @blk.gpsimd
        def _(g):
            g.iota(io_a[:], [[1, P]], channel_multiplier=0)
            g.iota(io_b[:], [[0, P]], channel_multiplier=1)
            g.memset(ones1_f[:], 1.0)
            g.memset(cst[:, 0:1], EPS_LN)
            g.memset(cst[:, 1:2], 0.25 * (1.0 + 1e-6))
            g.memset(cst[:, 2:3], EPS_RMS)
            g.memset(ones_bf[:], 1.0).then_inc(s_id, 1)

        @blk.vector
        def _(v):
            v.wait_ge(s_id, 1)
            v.tensor_tensor(ident_f[:], io_a[:], io_b[:], ALU.is_equal)
            v.tensor_copy(ident_b[:], ident_f[:]).then_inc(s_id, 1)
    em.val["s_id"] = 2
    return ident_f, ident_b, ones1_f, ones_bf, cst


def vec_table(ser, st, nc, name, src_ap, n, ident_f, pst, scale=None):
    raw = st.enter_context(nc.sbuf_tensor(name + "_r", [n, P], F32))
    tbl = st.enter_context(nc.sbuf_tensor(name, [P, n], F32))
    ser.add("sync", lambda e: e.dma_start(
        raw[:], src_ap.rearrange("(j p) -> j p", j=n)), dma=True)
    ser.add("tensor", lambda e: e.transpose(
        pst[:, 0:n], raw[:], ident_f[0:n, 0:n]))
    if scale is None:
        ser.add("vector", lambda e: e.tensor_copy(tbl[:], pst[:, 0:n]))
    else:
        ser.add("vector", lambda e: e.tensor_scalar(
            tbl[:], pst[:, 0:n], scale, None, ALU.mult))
    return tbl



def build_l0(dbg=False):
    nc = bass.Bass("TRN2", target_bir_lowering=False)

    x_in = nc.dram_tensor("x", [T, D], F32, kind="ExternalInput").ap()
    w_in = nc.dram_tensor("L0_attn_W_in", [D, D], F32, kind="ExternalInput").ap()
    b_in = nc.dram_tensor("L0_attn_b_in", [D], F32, kind="ExternalInput").ap()
    w_gate = nc.dram_tensor("L0_attn_W_gate", [D, D], F32, kind="ExternalInput").ap()
    b_gate = nc.dram_tensor("L0_attn_b_gate", [D], F32, kind="ExternalInput").ap()
    flam = nc.dram_tensor("L0_attn_forget_lambda", [D], F32, kind="ExternalInput").ap()
    gate_w = nc.dram_tensor("L0_ffn_gate_w", [IFF, D], I32, kind="ExternalInput").ap()
    gate_s = nc.dram_tensor("L0_ffn_gate_scale", [IFF], F32, kind="ExternalInput").ap()
    up_w = nc.dram_tensor("L0_ffn_up_w", [IFF, D], I32, kind="ExternalInput").ap()
    up_s = nc.dram_tensor("L0_ffn_up_scale", [IFF], F32, kind="ExternalInput").ap()
    down_w = nc.dram_tensor("L0_ffn_down_w", [D, IFF], I32, kind="ExternalInput").ap()
    down_s = nc.dram_tensor("L0_ffn_down_scale", [D], F32, kind="ExternalInput").ap()

    x1_img = nc.dram_tensor("x1_img", [P, ND * T], BF16, kind="ExternalOutput").ap()
    dk = "ExternalOutput" if dbg else "Internal"
    xn_img = nc.dram_tensor("xn_img", [P, ND * T], BF16, kind=dk).ap()
    st_img = nc.dram_tensor("st_img", [P, 2 * NTT], F32, kind=dk).ap()
    h_img = nc.dram_tensor("h_img", [ND, P, T], BF16, kind=dk).ap()
    cache_g = nc.dram_tensor("cache_g", [NIC, P, D], BF16).ap()
    cache_u = nc.dram_tensor("cache_u", [NIC, P, D], BF16).ap()
    cache_d = nc.dram_tensor("cache_d", [ND, P, IFF], BF16).ap()

    with ExitStack() as st:
        em = Emit(nc, st)
        ident_f, ident_b, ones1_f, ones_bf, cst = make_identities(em, st, nc)

        with ExitStack() as st0:
            with nc.Block() as blk:
                ser = Ser(em, "p0")
                pst = st0.enter_context(nc.psum_tensor("p0ps", [P, P], F32))
                bh_gate = vec_table(ser, st, nc, "bh_gate", b_gate, ND, ident_f, pst, 0.5)
                bh_in = vec_table(ser, st, nc, "bh_in", b_in, ND, ident_f, pst, 0.5)
                lam_t = vec_table(ser, st, nc, "lam_t", flam, ND, ident_f, pst)
                gsc = vec_table(ser, st, nc, "gsc", gate_s, NIC, ident_f, pst)
                usc = vec_table(ser, st, nc, "usc", up_s, NIC, ident_f, pst)
                dsc = vec_table(ser, st, nc, "dsc", down_s, ND, ident_f, pst, 0.5)
                gbias = st.enter_context(nc.sbuf_tensor("gbias", [P, NIC], F32))
                ubias = st.enter_context(nc.sbuf_tensor("ubias", [P, NIC], F32))
                dbias = st.enter_context(nc.sbuf_tensor("dbias", [P, ND], F32))
                chalf = st.enter_context(nc.sbuf_tensor("chalf", [P, ND], F32))
                tmp_a = st0.enter_context(nc.sbuf_tensor("tmp_a", [P, ND], F32))
                tmp_b = st0.enter_context(nc.sbuf_tensor("tmp_b", [P, ND], F32))
                ser.add("vector", lambda e: e.tensor_scalar(
                    gbias[:], gsc[:], -128.0, None, ALU.mult))
                ser.add("vector", lambda e: e.tensor_scalar(
                    ubias[:], usc[:], -128.0, None, ALU.mult))
                ser.add("vector", lambda e: e.tensor_scalar(
                    dbias[:], dsc[:], -128.0, None, ALU.mult))
                tmp_c = st0.enter_context(nc.sbuf_tensor("tmp_c", [P, ND], F32))
                ser.add("scalar", lambda e: e.activation(tmp_a[:], lam_t[:], AF.Abs))
                ser.add("scalar", lambda e: e.activation(tmp_b[:], tmp_a[:], AF.Exp, scale=-1.0))
                ser.add("scalar", lambda e: e.activation(tmp_c[:], tmp_b[:], AF.Ln, bias=1.0))
                ser.add("scalar", lambda e: e.activation(tmp_a[:], lam_t[:], AF.Relu))
                ser.add("vector", lambda e: e.tensor_tensor(
                    tmp_b[:], tmp_c[:], tmp_a[:], ALU.add))
                ser.add("vector", lambda e: e.tensor_scalar(
                    chalf[:], tmp_b[:], -4.0, None, ALU.mult))
                ser.emit(blk, extra_wait=(em.s("s_id"), 2))

        with ExitStack() as st_xn:
            xn_big = st_xn.enter_context(nc.sbuf_tensor("xn_big", [P, ND * T], BF16))
            xn3 = xn_big[:].rearrange("p (c t) -> p c t", c=ND)
            phase_p1(nc, em, x_in, xn_big, xn3, xn_img, ident_b, cst, st_img)
            phase_p2(nc, em, w_in, w_gate, h_img, xn3, ident_b,
                     bh_in, bh_gate, chalf, cst)
        phase_ffn_prep(nc, em, gate_w, up_w, down_w, cache_g, cache_u, cache_d,
                       gsc, gbias, usc, ubias, dsc, dbias, ident_b)
        phase_p3(nc, em, x_in, xn_img, h_img, x1_img,
                 cache_g, cache_u, cache_d, ident_b)

    return nc


def phase_p1(nc, em, x_in, xn_big, xn3, xn_img, ident_b, cst, st_img):
    st = em.stack
    em.roll()
    with ExitStack() as sp, nc.Block() as blk:
        for name in ["p1_ld0", "p1_ld1", "p1_sqr", "p1_var", "p1_ap", "p1_tp",
                     "p1_ev", "p1_st", "p1_rr", "p1_fs"]:
            em.new_sem(name)
        sb = lambda n_, s_, d_: sp.enter_context(nc.sbuf_tensor(n_, s_, d_))
        pt = lambda n_, s_, d_: sp.enter_context(nc.psum_tensor(n_, s_, d_))
        bx = [sb(f"bx{k}", [P, D], F32) for k in range(2)]
        sqb = [sb(f"sqb{k}", [P, D], BF16) for k in range(2)]
        xntm = [sb(f"xntm{k}", [P, D], BF16) for k in range(2)]
        tfb = sb("tfb", [P, D], F32)
        s1t = [sb(f"s1t{k}", [P, 1], F32) for k in range(2)]
        s2t = sb("s2t", [P, 1], F32)
        nmean = [sb(f"nmean{k}", [P, 1], F32) for k in range(2)]
        varb = [sb(f"varb{k}", [P, 1], F32) for k in range(2)]
        rstdb = sb("rstdb", [P, 1], F32)
        lnb = sb("lnb", [P, 1], F32)
        varc = sb("varc", [P, NTT], F32)
        rstdc = sb("rstdc", [P, NTT], F32)
        s2m = sb("s2m", [P, 1], F32)
        m2t = sb("m2t", [P, 1], F32)
        pstr = [pt(f"pstr{k}", [P, D], BF16) for k in range(2)]

        @blk.sync
        def _(e):
            for i in range(NTT):
                if i >= 2:
                    e.wait_ge(em.s("p1_ap"), i - 1)
                e.dma_start(bx[i % 2][:], x_in[i * P:(i + 1) * P, :]
                            ).then_inc(em.s(f"p1_ld{i % 2}"), 16)

        @blk.scalar
        def _(e):
            for i in range(NTT):
                e.wait_ge(em.s(f"p1_ld{i % 2}"), 16 * (i // 2 + 1))
                if i >= 2:
                    e.wait_ge(em.s("p1_var"), i - 1)
                e.activation(sqb[i % 2][:], bx[i % 2][:], AF.Square
                             ).then_inc(em.s("p1_sqr"), 1)
                e.wait_ge(em.s("p1_var"), i + 1)
                e.activation(lnb[:], varb[i % 2][:], AF.Ln, bias=cst[:, 0:1]
                             ).then_inc(em.s("p1_fs"), 1)
                e.wait_ge(em.s("p1_fs"), 2 * i + 1)
                e.activation(rstdb[:], lnb[:], AF.Exp, scale=-0.5
                             ).then_inc(em.s("p1_fs"), 1)
                e.wait_ge(em.s("p1_fs"), 2 * i + 2)
                e.activation(rstdc[:, i:i + 1], varb[i % 2][:], AF.Copy)
                e.activation(tfb[:], bx[i % 2][:], AF.Identity,
                             bias=nmean[i % 2][:, 0:1])
                if i >= 2:
                    e.wait_ge(em.s("p1_tp"), i - 1)
                e.activation(xntm[i % 2][:], tfb[:], AF.Identity,
                             scale=rstdb[:, 0:1]).then_inc(em.s("p1_ap"), 1)

        @blk.vector
        def _(e):
            for i in range(NTT):
                e.wait_ge(em.s(f"p1_ld{i % 2}"), 16 * (i // 2 + 1))
                if i >= 2:
                    e.wait_ge(em.s("p1_ap"), i - 1)
                fv = 6 * i
                e.reduce_sum(s1t[i % 2][:], bx[i % 2][:], axis=AX
                             ).then_inc(em.s("p1_rr"), 1)
                e.wait_ge(em.s("p1_rr"), fv + 1)
                e.tensor_scalar(nmean[i % 2][:], s1t[i % 2][:],
                                -1.0 / D, None, ALU.mult
                                ).then_inc(em.s("p1_rr"), 1)
                e.wait_ge(em.s("p1_sqr"), i + 1)
                e.reduce_sum(s2t[:], sqb[i % 2][:], axis=AX
                             ).then_inc(em.s("p1_rr"), 1)
                e.wait_ge(em.s("p1_rr"), fv + 3)
                e.tensor_scalar(s2m[:], s2t[:], 1.0 / D, None, ALU.mult
                                ).then_inc(em.s("p1_rr"), 1)
                e.tensor_tensor(m2t[:], nmean[i % 2][:],
                                nmean[i % 2][:], ALU.mult
                                ).then_inc(em.s("p1_rr"), 1)
                e.wait_ge(em.s("p1_rr"), fv + 5)
                e.tensor_tensor(varb[i % 2][:], s2m[:], m2t[:],
                                ALU.subtract).then_inc(em.s("p1_rr"), 1)
                e.wait_ge(em.s("p1_rr"), fv + 6)
                e.tensor_copy(varc[:, i:i + 1], varb[i % 2][:]
                              ).then_inc(em.s("p1_var"), 1)
                e.wait_ge(em.s("p1_tp"), i + 1)
                e.tensor_copy(xn3[:, :, i * P:(i + 1) * P],
                              pstr[i % 2][:].rearrange("p (c t) -> p c t", c=ND)
                              ).then_inc(em.s("p1_ev"), 1)

        @blk.tensor
        def _(e):
            for i in range(NTT):
                e.wait_ge(em.s("p1_ap"), i + 1)
                if i >= 2:
                    e.wait_ge(em.s("p1_ev"), i - 1)
                for dc in range(ND):
                    inst = e.transpose(pstr[i % 2][:, dc * P:(dc + 1) * P],
                                       xntm[i % 2][:, dc * P:(dc + 1) * P],
                                       ident_b[:])
                inst.then_inc(em.s("p1_tp"), 1)

        @blk.gpsimd
        def _(e):
            e.wait_ge(em.s("p1_ev"), NTT)
            e.dma_start(xn_img, xn_big[:]).then_inc(em.s("p1_st"), 16)
            e.dma_start(st_img[:, 0:NTT], varc[:]).then_inc(em.s("p1_st"), 16)
            e.dma_start(st_img[:, NTT:2 * NTT], rstdc[:]).then_inc(em.s("p1_st"), 16)
            e.wait_ge(em.s("p1_st"), 48)

        em.val["p1_ld0"] = 16 * (NTT // 2)
        em.val["p1_ld1"] = 16 * (NTT // 2)
        em.val["p1_sqr"] = NTT
        em.val["p1_rr"] = 6 * NTT
        em.val["p1_var"] = NTT
        em.val["p1_fs"] = 2 * NTT
        em.val["p1_ap"] = NTT
        em.val["p1_tp"] = NTT
        em.val["p1_ev"] = NTT
        em.val["p1_st"] = 48


def phase_p2(nc, em, w_in, w_gate, h_img, xn3, ident_b, bh_in, bh_gate, chalf, cst):
    em.roll()
    with ExitStack() as sp, nc.Block() as blk:
        for name in ["p2_ld0", "p2_ld1", "p2_tp", "p2_ev", "p2_mm", "p2_th",
                     "p2_al", "p2_a2", "p2_sq", "p2_xb", "p2_hst0", "p2_hst1"]:
            em.new_sem(name)
        sb = lambda n_, s_, d_: sp.enter_context(nc.sbuf_tensor(n_, s_, d_))
        pt = lambda n_, s_, d_: sp.enter_context(nc.psum_tensor(n_, s_, d_))
        sWn = [sb(f"sWn{k}", [P, 2 * D], BF16) for k in range(2)]
        sWT = [sb(f"sWT{k}", [P, 2 * D], BF16) for k in range(2)]
        pswt = [pt(f"pswt{k}", [P, D], BF16) for k in range(2)]
        psg = [pt(f"psg{k}", [P, TC], F32) for k in range(4)]
        th_r = [sb(f"th_r{k}", [P, T], F32) for k in range(2)]
        th_i = [sb(f"th_i{k}", [P, T], F32) for k in range(2)]
        al_b = [sb(f"al_b{k}", [P, T], F32) for k in range(2)]
        a2_b = [sb(f"a2_b{k}", [P, T], F32) for k in range(2)]
        u_b = sb("u_b", [P, T], F32)
        xb_b = sb("xb_b", [P, T], F32)
        hf_b = sb("hf_b", [P, T], F32)
        hb_b = [sb(f"hb_b{k}", [P, T], BF16) for k in range(2)]

        def mmc(e_, g_, tc_):
            return 8 * e_ + 4 * g_ + tc_ + 1

        @blk.gpsimd
        def _(e):
            for et in range(ND):
                if et >= 2:
                    e.wait_ge(em.s("p2_tp"), et - 1)
                e.dma_start(sWn[et % 2][:, 0:D], w_gate[et * P:(et + 1) * P, :]
                            ).then_inc(em.s(f"p2_ld{et % 2}"), 16)
                e.dma_start(sWn[et % 2][:, D:2 * D], w_in[et * P:(et + 1) * P, :]
                            ).then_inc(em.s(f"p2_ld{et % 2}"), 16)
                if et >= 1:
                    e.wait_ge(em.s("p2_xb"), et)
                    e.dma_start(h_img[et - 1], hb_b[(et - 1) % 2][:]
                                ).then_inc(em.s(f"p2_hst{(et - 1) % 2}"), 16)
            e.wait_ge(em.s("p2_xb"), ND)
            e.dma_start(h_img[ND - 1], hb_b[(ND - 1) % 2][:]
                        ).then_inc(em.s(f"p2_hst{(ND - 1) % 2}"), 16)
            e.wait_ge(em.s("p2_hst0"), 16 * (ND // 2))
            e.wait_ge(em.s("p2_hst1"), 16 * (ND // 2))

        @blk.tensor
        def _(e):
            for et in range(ND):
                e.wait_ge(em.s(f"p2_ld{et % 2}"), 32 * (et // 2 + 1))
                if et >= 1:
                    e.wait_ge(em.s("p2_ev"), et)
                for w in range(2):
                    for dc in range(ND):
                        inst = e.transpose(
                            pswt[w][:, dc * P:(dc + 1) * P],
                            sWn[et % 2][:, w * D + dc * P:w * D + (dc + 1) * P],
                            ident_b[:])
                inst.then_inc(em.s("p2_tp"), 1)
                e.wait_ge(em.s("p2_ev"), et + 1)
                for g in range(2):
                    for tc in range(NTC):
                        if tc >= 2:
                            e.wait_ge(em.s("p2_th"), mmc(et, g, tc - 2))
                        elif et >= 1:
                            e.wait_ge(em.s("p2_th"), mmc(et - 1, g, tc + 2))
                        slot = 2 * g + (tc % 2)
                        for dc in range(ND):
                            inst = e.matmul(
                                psg[slot][:],
                                sWT[et % 2][:, g * D + dc * P:g * D + (dc + 1) * P],
                                xn3[:, dc, tc * TC:(tc + 1) * TC],
                                start=(dc == 0), stop=(dc == ND - 1))
                        inst.then_inc(em.s("p2_mm"), 1)

        @blk.scalar
        def _(e):
            for et in range(ND):
                for g in range(2):
                    th = th_r if g == 0 else th_i
                    bh = bh_gate if g == 0 else bh_in
                    if et >= 2 and g == 0:
                        e.wait_ge(em.s("p2_xb"), et - 1)
                    for tc in range(NTC):
                        e.wait_ge(em.s("p2_mm"), mmc(et, g, tc))
                        slot = 2 * g + (tc % 2)
                        e.activation(th[et % 2][:, tc * TC:(tc + 1) * TC],
                                     psg[slot][:], AF.Tanh, scale=0.5,
                                     bias=bh[:, et:et + 1]
                                     ).then_inc(em.s("p2_th"), 1)
                e.activation(al_b[et % 2][:], th_r[et % 2][:], AF.Exp,
                             scale=chalf[:, et:et + 1],
                             bias=chalf[:, et:et + 1]).then_inc(em.s("p2_al"), 1)
                if et % 2 == 1:
                    for e2 in (et - 1, et):
                        e.wait_ge(em.s("p2_a2"), e2 + 1)
                        e.activation(th_r[e2 % 2][:], a2_b[e2 % 2][:], AF.Sqrt,
                                     scale=-0.25, bias=cst[:, 1:2]
                                     ).then_inc(em.s("p2_sq"), 1)

        def chain(e, et):
            k = et % 2
            e.wait_ge(em.s("p2_al"), et + 1)
            if et >= 2:
                e.wait_ge(em.s("p2_sq"), et - 1)
            e.tensor_tensor(a2_b[k][:], al_b[k][:], al_b[k][:], ALU.mult
                            ).then_inc(em.s("p2_a2"), 1)
            e.wait_ge(em.s("p2_sq"), et + 1)
            e.tensor_tensor(u_b[:], th_r[k][:], xn3[:, et, :], ALU.mult)
            e.scalar_tensor_tensor(xb_b[:], th_i[k][:], 1.0, u_b[:],
                                   ALU.add, ALU.mult)
            e.tensor_tensor_scan(hf_b[:], al_b[k][:], xb_b[:], 0.0,
                                 ALU.mult, ALU.add)
            if et >= 2:
                e.wait_ge(em.s(f"p2_hst{et % 2}"), 16 * (et // 2))
            e.tensor_copy(hb_b[k][:], hf_b[:]).then_inc(em.s("p2_xb"), 1)

        @blk.vector
        def _(e):
            for et in range(ND):
                e.wait_ge(em.s("p2_tp"), et + 1)
                if et >= 2:
                    e.wait_ge(em.s("p2_mm"), 8 * (et - 1))
                for w in range(2):
                    inst = e.tensor_copy(
                        sWT[et % 2][:, w * D:(w + 1) * D], pswt[w][:])
                inst.then_inc(em.s("p2_ev"), 1)
                if et >= 1:
                    chain(e, et - 1)
            chain(e, ND - 1)

        em.val["p2_ld0"] = 32 * (ND // 2)
        em.val["p2_ld1"] = 32 * (ND // 2)
        em.val["p2_tp"] = ND
        em.val["p2_ev"] = ND
        em.val["p2_mm"] = 8 * ND
        em.val["p2_th"] = 8 * ND
        em.val["p2_al"] = ND
        em.val["p2_a2"] = ND
        em.val["p2_sq"] = ND
        em.val["p2_xb"] = ND
        em.val["p2_hst0"] = 16 * (ND // 2)
        em.val["p2_hst1"] = 16 * (ND // 2)


def phase_ffn_prep(nc, em, gate_w, up_w, down_w, cache_g, cache_u, cache_d,
                   gsc, gbias, usc, ubias, dsc, dbias, ident_b):
    em.roll()
    with ExitStack() as sp, nc.Block() as blk:
        for name in (["fp_dq", "fp_tp", "fp_ev"]
                     + [f"fp_ld{k}" for k in range(4)]
                     + [f"fp_st{k}" for k in range(4)]):
            em.new_sem(name)
        sb = lambda n_, s_, d_: sp.enter_context(nc.sbuf_tensor(n_, s_, d_))
        pt = lambda n_, s_, d_: sp.enter_context(nc.psum_tensor(n_, s_, d_))
        raw = [sb(f"fpraw{k}", [P, D], I32) for k in range(4)]
        dq = [sb(f"fpdq{k}", [P, D], BF16) for k in range(4)]
        img = [sb(f"fpimg{k}", [P, D], BF16) for k in range(4)]
        pswt = [pt(f"fpps{k}", [P, D], BF16) for k in range(4)]

        jobs = []
        for ic in range(NIC):
            jobs.append(("g", gate_w[ic * P:(ic + 1) * P, :], gsc, gbias,
                         cache_g, ic))
            jobs.append(("u", up_w[ic * P:(ic + 1) * P, :], usc, ubias,
                         cache_u, ic))
        for hh in range(ND):
            for q in range(4):
                jobs.append(("d", down_w[hh * P:(hh + 1) * P,
                                         q * D:(q + 1) * D], dsc, dbias,
                             cache_d[hh].rearrange(
                                 "p (q f) -> q p f", q=4)[q], hh))
        NJ = len(jobs)

        @blk.sync
        def _(e):
            for j, (kind, src, s_t, b_t, dst, idx) in enumerate(jobs):
                if j >= 4:
                    e.wait_ge(em.s("fp_dq"), j - 3)
                e.dma_start(raw[j % 4][:], src).then_inc(em.s(f"fp_ld{j % 4}"), 16)

        @blk.scalar
        def _(e):
            for j, (kind, src, s_t, b_t, dst, idx) in enumerate(jobs):
                e.wait_ge(em.s(f"fp_ld{j % 4}"), 16 * (j // 4 + 1))
                if j >= 4:
                    e.wait_ge(em.s("fp_tp"), j - 3)
                e.activation(dq[j % 4][:], raw[j % 4][:], AF.Identity,
                             scale=s_t[:, idx:idx + 1],
                             bias=b_t[:, idx:idx + 1]).then_inc(em.s("fp_dq"), 1)

        @blk.tensor
        def _(e):
            for j in range(NJ):
                e.wait_ge(em.s("fp_dq"), j + 1)
                if j >= 4:
                    e.wait_ge(em.s("fp_ev"), j - 3)
                for dc in range(ND):
                    inst = e.transpose(pswt[j % 4][:, dc * P:(dc + 1) * P],
                                       dq[j % 4][:, dc * P:(dc + 1) * P],
                                       ident_b[:])
                inst.then_inc(em.s("fp_tp"), 1)

        @blk.vector
        def _(e):
            for j in range(NJ):
                e.wait_ge(em.s("fp_tp"), j + 1)
                if j >= 4:
                    e.wait_ge(em.s(f"fp_st{j % 4}"), 16 * (j // 4))
                e.tensor_copy(img[j % 4][:], pswt[j % 4][:]
                              ).then_inc(em.s("fp_ev"), 1)

        @blk.gpsimd
        def _(e):
            for j, (kind, src, s_t, b_t, dst, idx) in enumerate(jobs):
                e.wait_ge(em.s("fp_ev"), j + 1)
                e.dma_start(dst if kind == "d" else dst[idx], img[j % 4][:]
                            ).then_inc(em.s(f"fp_st{j % 4}"), 16)
            for k in range(4):
                e.wait_ge(em.s(f"fp_st{k}"), 16 * (NJ // 4 + (1 if NJ % 4 > k else 0)))

        for k in range(4):
            em.val[f"fp_ld{k}"] = 16 * (NJ // 4 + (1 if NJ % 4 > k else 0))
        em.val["fp_dq"] = NJ
        em.val["fp_tp"] = NJ
        em.val["fp_ev"] = NJ
        for k in range(4):
            em.val[f"fp_st{k}"] = 16 * (NJ // 4 + (1 if NJ % 4 > k else 0))


def phase_p3(nc, em, x_in, xn_img, h_img, x1_img, cache_g, cache_u, cache_d,
             ident_b):
    em.roll()
    with ExitStack() as sp, nc.Block() as blk:
        for name in ["p3_xld", "p3_wld0", "p3_wld1", "p3_mm", "p3_tg", "p3_hc",
                     "p3_dld0", "p3_dld1", "p3_dn", "p3_cb", "p3_st0", "p3_st1",
                     "p3_xtp", "p3_xev", "p3_hld", "p3_xw0", "p3_xw1"]:
            em.new_sem(name)
        sb = lambda n_, s_, d_: sp.enter_context(nc.sbuf_tensor(n_, s_, d_))
        pt = lambda n_, s_, d_: sp.enter_context(nc.psum_tensor(n_, s_, d_))
        xnw = sb("xnw", [P, ND * WIN], BF16)
        hact = sb("hact", [P, NIC * WIN], BF16)
        img_g = [sb(f"img_g{k}", [P, D], BF16) for k in range(2)]
        img_u = [sb(f"img_u{k}", [P, D], BF16) for k in range(2)]
        dimg = [sb(f"dimg{k}", [P, IFF], BF16) for k in range(2)]
        su_b = sb("su_b", [P, WIN], F32)
        tg_b = [sb(f"tg_b{k}", [P, WIN], F32) for k in range(2)]
        pg_b = sb("pg_b", [P, WIN], F32)
        sxtm = [sb(f"sxtm{k}", [P, D], BF16) for k in range(2)]
        xfm = sb("xfm", [P, ND * WIN], BF16)
        hwb = sb("hwb", [P, ND * WIN], BF16)
        t1b = sb("t1b", [P, WIN], F32)
        x1b = [sb(f"x1b{k}", [P, WIN], BF16) for k in range(2)]
        psG = [pt(f"psG{k}", [P, WIN], F32) for k in range(2)]
        psU = [pt(f"psU{k}", [P, WIN], F32) for k in range(2)]
        psD = [pt(f"psD{k}", [P, WIN], F32) for k in range(2)]
        psX = pt("psX", [P, D], BF16)
        xnw3 = xnw[:].rearrange("p (c t) -> p c t", c=ND)
        xfm3 = xfm[:].rearrange("p (c t) -> p c t", c=ND)
        hwb3 = hwb[:].rearrange("p (c t) -> p c t", c=ND)
        NSUB = WIN // P

        @blk.sync
        def _(e):
            for w in range(NW):
                if w >= 1:
                    e.wait_ge(em.s("p3_mm"), 2 * NIC * w)
                e.dma_start(
                    xnw3, xn_img.rearrange("p (c t) -> p c t", c=ND)
                    [:, :, w * WIN:(w + 1) * WIN]).then_inc(em.s("p3_xld"), 16)
                for ic in range(NIC):
                    n = w * NIC + ic
                    if n >= 2:
                        e.wait_ge(em.s("p3_hc"), n - 1)
                    e.dma_start(img_g[ic % 2][:], cache_g[ic]
                                ).then_inc(em.s(f"p3_wld{ic % 2}"), 16)
                    e.dma_start(img_u[ic % 2][:], cache_u[ic]
                                ).then_inc(em.s(f"p3_wld{ic % 2}"), 16)
                for hh in range(ND):
                    m = w * ND + hh
                    if m >= 2:
                        e.wait_ge(em.s("p3_dn"), m - 1)
                    e.dma_start(dimg[hh % 2][:], cache_d[hh]
                                ).then_inc(em.s(f"p3_dld{hh % 2}"), 16)

        @blk.gpsimd
        def _(e):
            for w in range(NW):
                for s in range(NSUB):
                    i = w * NSUB + s
                    if i >= 2:
                        e.wait_ge(em.s("p3_xtp"), i - 1)
                    e.dma_start(sxtm[i % 2][:],
                                x_in[(w * WIN + s * P):(w * WIN + (s + 1) * P), :]
                                ).then_inc(em.s(f"p3_xw{i % 2}"), 16)
                if w >= 1:
                    e.wait_ge(em.s("p3_cb"), w * ND)
                e.dma_start(hwb3,
                            h_img[:, :, w * WIN:(w + 1) * WIN]
                            .rearrange("c p t -> p c t")
                            ).then_inc(em.s("p3_hld"), 16)
                for hh in range(ND):
                    m = w * ND + hh
                    e.wait_ge(em.s("p3_cb"), m + 1)
                    e.dma_start(x1_img.rearrange("p (c t) -> p c t", c=ND)
                                [:, hh, w * WIN:(w + 1) * WIN],
                                x1b[hh % 2][:]).then_inc(em.s(f"p3_st{hh % 2}"), 16)
            e.wait_ge(em.s("p3_st0"), 16 * NW * ND // 2)
            e.wait_ge(em.s("p3_st1"), 16 * NW * ND // 2)

        @blk.tensor
        def _(e):
            for w in range(NW):
                for s in range(NSUB):
                    i = w * NSUB + s
                    e.wait_ge(em.s(f"p3_xw{i % 2}"), 16 * (i // 2 + 1))
                    if i >= 1:
                        e.wait_ge(em.s("p3_xev"), i)
                    for dc in range(ND):
                        inst = e.transpose(psX[:, dc * P:(dc + 1) * P],
                                           sxtm[i % 2][:, dc * P:(dc + 1) * P],
                                           ident_b[:])
                    inst.then_inc(em.s("p3_xtp"), 1)
                for ic in range(NIC):
                    n = w * NIC + ic
                    e.wait_ge(em.s(f"p3_wld{ic % 2}"), 32 * ((w * NIC + ic) // 2 + 1))
                    e.wait_ge(em.s("p3_xld"), 16 * (w + 1))
                    if n >= 2:
                        e.wait_ge(em.s("p3_hc"), n - 1)
                    for dc in range(ND):
                        inst = e.matmul(psG[ic % 2][:],
                                        img_g[ic % 2][:, dc * P:(dc + 1) * P],
                                        xnw3[:, dc, :],
                                        start=(dc == 0), stop=(dc == ND - 1))
                    inst.then_inc(em.s("p3_mm"), 1)
                    for dc in range(ND):
                        inst = e.matmul(psU[ic % 2][:],
                                        img_u[ic % 2][:, dc * P:(dc + 1) * P],
                                        xnw3[:, dc, :],
                                        start=(dc == 0), stop=(dc == ND - 1))
                    inst.then_inc(em.s("p3_mm"), 1)
                for hh in range(ND):
                    m = w * ND + hh
                    e.wait_ge(em.s(f"p3_dld{hh % 2}"), 16 * (m // 2 + 1))
                    e.wait_ge(em.s("p3_hc"), (w + 1) * NIC)
                    if m >= 2:
                        e.wait_ge(em.s("p3_cb"), m - 1)
                    for ic in range(NIC):
                        inst = e.matmul(psD[hh % 2][:],
                                        dimg[hh % 2][:, ic * P:(ic + 1) * P],
                                        hact[:, ic * WIN:(ic + 1) * WIN],
                                        start=(ic == 0), stop=(ic == NIC - 1))
                    inst.then_inc(em.s("p3_dn"), 1)

        @blk.scalar
        def _(e):
            for w in range(NW):
                for ic in range(NIC):
                    n = w * NIC + ic
                    e.wait_ge(em.s("p3_mm"), 2 * n + 1)
                    if n >= 2:
                        e.wait_ge(em.s("p3_hc"), n - 1)
                    e.activation(tg_b[ic % 2][:], psG[ic % 2][:], AF.Tanh,
                                 scale=0.5).then_inc(em.s("p3_tg"), 1)

        @blk.vector
        def _(e):
            for w in range(NW):
                for s in range(NSUB):
                    i = w * NSUB + s
                    e.wait_ge(em.s("p3_xtp"), i + 1)
                    if w >= 1 and s == 0:
                        e.wait_ge(em.s("p3_cb"), w * ND)
                    e.tensor_copy(xfm3[:, :, s * P:(s + 1) * P],
                                  psX[:].rearrange("p (c t) -> p c t", c=ND)
                                  ).then_inc(em.s("p3_xev"), 1)
                for ic in range(NIC):
                    n = w * NIC + ic
                    e.wait_ge(em.s("p3_mm"), 2 * n + 2)
                    e.tensor_copy(su_b[:], psU[ic % 2][:])
                    e.tensor_tensor(pg_b[:], psG[ic % 2][:], su_b[:], ALU.mult)
                    e.wait_ge(em.s("p3_tg"), n + 1)
                    if w >= 1:
                        e.wait_ge(em.s("p3_dn"), w * ND)
                    e.scalar_tensor_tensor(hact[:, ic * WIN:(ic + 1) * WIN],
                                           tg_b[ic % 2][:], 1.0, pg_b[:],
                                           ALU.add, ALU.mult
                                           ).then_inc(em.s("p3_hc"), 1)
                for hh in range(ND):
                    m = w * ND + hh
                    e.wait_ge(em.s("p3_dn"), m + 1)
                    if hh == 0:
                        e.wait_ge(em.s("p3_xev"), (w + 1) * NSUB)
                        e.wait_ge(em.s("p3_hld"), 16 * (w + 1))
                    if m >= 2:
                        e.wait_ge(em.s(f"p3_st{hh % 2}"), 16 * (m // 2))
                    e.tensor_tensor(t1b[:], psD[hh % 2][:], xfm3[:, hh, :],
                                    ALU.add)
                    e.tensor_tensor(x1b[hh % 2][:], t1b[:], hwb3[:, hh, :],
                                    ALU.add).then_inc(em.s("p3_cb"), 1)

        em.val["p3_xld"] = 16 * NW
        em.val["p3_wld0"] = 32 * NW * NIC // 2
        em.val["p3_wld1"] = 32 * NW * NIC // 2
        em.val["p3_dld0"] = 16 * NW * ND // 2
        em.val["p3_dld1"] = 16 * NW * ND // 2
        em.val["p3_mm"] = 2 * NW * NIC
        em.val["p3_tg"] = NW * NIC
        em.val["p3_hc"] = NW * NIC
        em.val["p3_dn"] = NW * ND
        em.val["p3_cb"] = NW * ND
        em.val["p3_st0"] = 16 * NW * ND // 2
        em.val["p3_st1"] = 16 * NW * ND // 2
        em.val["p3_xw0"] = 16 * NW * NSUB // 2
        em.val["p3_xw1"] = 16 * NW * NSUB // 2
        em.val["p3_xtp"] = NW * NSUB
        em.val["p3_xev"] = NW * NSUB
        em.val["p3_hld"] = 16 * NW




def build_l1(dbg=False):
    nc = bass.Bass("TRN2", target_bir_lowering=False)

    x1_img = nc.dram_tensor("x1_img", [P, ND * T], BF16, kind="ExternalInput").ap()
    wq = nc.dram_tensor("L1_attn_Wq", [D, 2 * DH * NH], F32, kind="ExternalInput").ap()
    wk = nc.dram_tensor("L1_attn_Wk", [D, 2 * DH * NH], F32, kind="ExternalInput").ap()
    wv = nc.dram_tensor("L1_attn_Wv", [D, DH * NH], F32, kind="ExternalInput").ap()
    wo = nc.dram_tensor("L1_attn_Wo", [DH * NH, D], F32, kind="ExternalInput").ap()
    nsc_in = nc.dram_tensor("L1_attn_norm_scale", [D], F32, kind="ExternalInput").ap()
    lam_in = nc.dram_tensor("L1_attn_lam", [NH], F32, kind="ExternalInput").ap()
    factors = nc.dram_tensor("L1_ffn_factors", [8, D, D], F32, kind="ExternalInput").ap()
    bias_in = nc.dram_tensor("L1_ffn_bias", [D], F32, kind="ExternalInput").ap()

    out = nc.dram_tensor("out", [T, D], F32, kind="ExternalOutput").ap()
    dk = "ExternalOutput" if dbg else "Internal"
    xn2_img = nc.dram_tensor("xn2_img", [P, ND * T], BF16, kind=dk).ap()
    oat_img = nc.dram_tensor("oat_img", [ND, P, T], BF16, kind=dk).ap()
    f_img = nc.dram_tensor("f_img", [ND, P, D], BF16).ap()

    with ExitStack() as st:
        em = Emit(nc, st)
        ident_f, ident_b, ones1_f, ones_bf, cst = make_identities(em, st, nc)

        with ExitStack() as st0:
            with nc.Block() as blk:
                ser = Ser(em, "q0")
                pst = st0.enter_context(nc.psum_tensor("q0ps", [P, P], F32))
                nsc = vec_table(ser, st, nc, "nsc", nsc_in, ND, ident_f, pst)
                bs_t = vec_table(ser, st, nc, "bs_t", bias_in, ND, ident_f, pst)
                lam_row = st.enter_context(nc.sbuf_tensor("lam_row", [1, NH], F32))
                lam_b = st.enter_context(nc.sbuf_tensor("lam_b", [P, NH], F32))
                om = st.enter_context(nc.sbuf_tensor("om", [1, 1], F32))
                t_ml = st0.enter_context(nc.sbuf_tensor("t_ml", [1, 1], F32))
                t_ms = st0.enter_context(nc.sbuf_tensor("t_ms", [1, 1], F32))
                ser.add("sync", lambda e: e.dma_start(lam_row[:], lam_in.rearrange("(a b) -> a b", a=1)), dma=True)
                ser.add("tensor", lambda e: e.matmul(pst[:, 0:NH], ones1_f[0:1, :],
                                                     lam_row[:], start=True, stop=True))
                ser.add("vector", lambda e: e.tensor_copy(lam_b[:], pst[:, 0:NH]))
                ser.add("vector", lambda e: e.reduce_sum(t_ml[:], lam_row[:], axis=AX))
                ser.add("vector", lambda e: e.tensor_scalar(t_ms[:], t_ml[:], 1.0 / NH, None, ALU.mult))
                ser.add("vector", lambda e: e.tensor_scalar(om[:], t_ms[:], -1.0, 1.0, ALU.mult, ALU.add))
                ser.emit(blk, extra_wait=(em.s("s_id"), 2))

        em.roll()
        with ExitStack() as sf, nc.Block() as blk:
            for k in range(8):
                em.new_sem(f"q0f_ld{k}")
            for name in ["q0f_sum", "q0f_st"]:
                em.new_sem(name)
            fb = [sf.enter_context(nc.sbuf_tensor(f"fb{k}", [P, D], F32))
                  for k in range(8)]
            fta = sf.enter_context(nc.sbuf_tensor("fta", [P, D], F32))
            ftb = sf.enter_context(nc.sbuf_tensor("ftb", [P, D], F32))
            sFb = sf.enter_context(nc.sbuf_tensor("sFb", [P, D], BF16))

            @blk.sync
            def _(e):
                for dc in range(ND):
                    for r in range(8):
                        if dc >= 1:
                            e.wait_ge(em.s("q0f_sum"), dc)
                        e.dma_start(fb[r][:], factors[r, dc * P:(dc + 1) * P, :]
                                    ).then_inc(em.s(f"q0f_ld{r}"), 16)

            @blk.vector
            def _(e):
                for dc in range(ND):
                    for r in range(8):
                        e.wait_ge(em.s(f"q0f_ld{r}"), 16 * (dc + 1))
                    e.tensor_tensor(fta[:], fb[0][:], fb[1][:], ALU.add)
                    e.tensor_tensor(ftb[:], fta[:], fb[2][:], ALU.add)
                    e.tensor_tensor(fta[:], ftb[:], fb[3][:], ALU.add)
                    e.tensor_tensor(ftb[:], fta[:], fb[4][:], ALU.add)
                    e.tensor_tensor(fta[:], ftb[:], fb[5][:], ALU.add)
                    e.tensor_tensor(ftb[:], fta[:], fb[6][:], ALU.add)
                    e.tensor_tensor(fta[:], ftb[:], fb[7][:], ALU.add)
                    if dc >= 1:
                        e.wait_ge(em.s("q0f_st"), 16 * dc)
                    e.tensor_scalar(sFb[:], fta[:], 8.0 ** -0.5, None, ALU.mult
                                    ).then_inc(em.s("q0f_sum"), 1)

            @blk.gpsimd
            def _(e):
                for dc in range(ND):
                    e.wait_ge(em.s("q0f_sum"), dc + 1)
                    e.dma_start(f_img[dc], sFb[:]).then_inc(em.s("q0f_st"), 16)
                e.wait_ge(em.s("q0f_st"), 16 * ND)

            em.val["q0f_sum"] = ND
            em.val["q0f_st"] = 16 * ND
            for r in range(8):
                em.val[f"q0f_ld{r}"] = 16 * ND

        xn2_big = st.enter_context(nc.sbuf_tensor("xn2_big", [P, ND * T], BF16))
        xn23 = xn2_big[:].rearrange("p (c t) -> p c t", c=ND)

        phase_q1(nc, em, x1_img, xn2_big, xn23, xn2_img, ones_bf, ones1_f, cst)
        phase_q2(nc, em, wq, wk, wv, xn23, oat_img, lam_b, ident_b)
        phase_q3(nc, em, oat_img, ones_bf, ones1_f, cst, nsc, om)
        phase_q4(nc, em, wo, f_img, bs_t, x1_img, xn2_img, oat_img, out,
                 ident_f)

    return nc


def phase_q1(nc, em, x1_img, xn2_big, xn23, xn2_img, ones_bf, ones1_f, cst):
    em.roll()
    with ExitStack() as sp, nc.Block() as blk:
        for name in ["q1_ld0", "q1_ld1", "q1_sq", "q1_mm", "q1_row", "q1_bc",
                     "q1_ap", "q1_st", "q1_fs", "q1_ld2", "q1_ld3"]:
            em.new_sem(name)
        sb = lambda n_, s_, d_: sp.enter_context(nc.sbuf_tensor(n_, s_, d_))
        x1t = [sb(f"q1x{k}", [P, D], BF16) for k in range(2)]
        sqt = [sb(f"q1s{k}", [P, D], BF16) for k in range(2)]
        rows_c = sb("rows_c", [1, D], F32)
        rows_d = sb("rows_d", [1, D], F32)
        mrow = sb("mrow", [1, D], F32)[:]
        s2row = sb("s2row", [1, D], F32)[:]
        m2row = sb("m2row", [1, D], F32)[:]
        varrow = sb("varrow", [1, D], F32)[:]
        lnrow = sb("lnrow", [1, D], F32)[:]
        arow = rows_c[:]
        brow = rows_d[:]
        tmp = sb("q1tmp", [P, D], F32)
        x13 = x1_img.rearrange("p (c t) -> p c t", c=ND)

        with ExitStack() as sps:
            ps_s1 = sps.enter_context(nc.psum_tensor("ps_s1", [1, D], F32))
            ps_s2 = sps.enter_context(nc.psum_tensor("ps_s2", [1, D], F32))

            @blk.sync
            def _(e):
                for et in range(ND):
                    if et >= 2:
                        e.wait_ge(em.s("q1_sq"), et - 1)
                        e.wait_ge(em.s("q1_mm"), 8 * (et - 1))
                    e.dma_start(x1t[et % 2][:], x13[:, et, :]
                                ).then_inc(em.s(f"q1_ld{et % 2}"), 16)

            @blk.scalar
            def _(e):
                for et in range(ND):
                    e.wait_ge(em.s(f"q1_ld{et % 2}"), 16 * (et // 2 + 1))
                    if et >= 2:
                        e.wait_ge(em.s("q1_mm"), 8 * (et - 1))
                    e.activation(sqt[et % 2][:], x1t[et % 2][:], AF.Square
                                 ).then_inc(em.s("q1_sq"), 1)
                e.wait_ge(em.s("q1_row"), 3)
                e.activation(lnrow, varrow, AF.Ln, bias=cst[0:1, 0:1]
                             ).then_inc(em.s("q1_fs"), 1)
                e.wait_ge(em.s("q1_fs"), 1)
                e.activation(arow, lnrow, AF.Exp, scale=-0.5
                             ).then_inc(em.s("q1_fs"), 1)
                e.wait_ge(em.s("q1_fs"), 2)
                e.nop().then_inc(em.s("q1_row"), 1)

            @blk.tensor
            def _(e):
                for et in range(ND):
                    e.wait_ge(em.s("q1_sq"), et + 1)
                    for tc in range(NTC):
                        e.matmul(ps_s1[:, tc * TC:(tc + 1) * TC], ones_bf[:],
                                 x1t[et % 2][:, tc * TC:(tc + 1) * TC],
                                 start=(et == 0), stop=(et == ND - 1))
                        inst = e.matmul(ps_s2[:, tc * TC:(tc + 1) * TC],
                                        ones_bf[:],
                                        sqt[et % 2][:, tc * TC:(tc + 1) * TC],
                                        start=(et == 0), stop=(et == ND - 1))
                    inst.then_inc(em.s("q1_mm"), 8)

            @blk.vector
            def _(e):
                e.wait_ge(em.s("q1_mm"), 8 * ND)
                e.tensor_scalar(mrow, ps_s1[:], 1.0 / D, None, ALU.mult)
                e.tensor_scalar(s2row, ps_s2[:], 1.0 / D, None, ALU.mult
                                ).then_inc(em.s("q1_row"), 1)
                e.wait_ge(em.s("q1_row"), 1)
                e.tensor_tensor(m2row, mrow, mrow, ALU.mult
                                ).then_inc(em.s("q1_row"), 1)
                e.wait_ge(em.s("q1_row"), 2)
                e.tensor_tensor(varrow, s2row, m2row, ALU.subtract
                                ).then_inc(em.s("q1_row"), 1)
                e.wait_ge(em.s("q1_row"), 4)
                e.scalar_tensor_tensor(brow, mrow, -1.0, arow,
                                       ALU.mult, ALU.mult
                                       ).then_inc(em.s("q1_row"), 1)

            em.val["q1_sq"] = ND
            em.val["q1_mm"] = 8 * ND
            em.val["q1_row"] = 5
            em.val["q1_fs"] = 2
            em.val["q1_ld0"] = 16 * (ND // 2)
            em.val["q1_ld1"] = 16 * (ND // 2)

        with ExitStack() as sps2:
            ps_A = sps2.enter_context(nc.psum_tensor("ps_A", [P, D], F32))
            ps_B = sps2.enter_context(nc.psum_tensor("ps_B", [P, D], F32))

            @blk.tensor
            def _(e):
                e.wait_ge(em.s("q1_row"), 5)
                for tc in range(NTC):
                    e.matmul(ps_A[:, tc * TC:(tc + 1) * TC], ones1_f[0:1, :],
                             rows_c[0:1, tc * TC:(tc + 1) * TC], start=True,
                             stop=True)
                    inst = e.matmul(ps_B[:, tc * TC:(tc + 1) * TC],
                                    ones1_f[0:1, :],
                                    rows_d[0:1, tc * TC:(tc + 1) * TC],
                                    start=True, stop=True)
                inst.then_inc(em.s("q1_bc"), 1)

            @blk.sync
            def _(e):
                e.wait_ge(em.s("q1_mm"), 8 * ND)
                for et in range(ND):
                    if et >= 2:
                        e.wait_ge(em.s("q1_ap"), et - 1)
                    e.dma_start(x1t[et % 2][:], x13[:, et, :]
                                ).then_inc(em.s(f"q1_ld{2 + et % 2}"), 16)

            @blk.vector
            def _(e):
                e.wait_ge(em.s("q1_bc"), 1)
                for et in range(ND):
                    e.wait_ge(em.s(f"q1_ld{2 + et % 2}"), 16 * (et // 2 + 1))
                    e.tensor_tensor(tmp[:], x1t[et % 2][:], ps_A[:], ALU.mult)
                    e.tensor_tensor(xn23[:, et, :], tmp[:], ps_B[:], ALU.add
                                    ).then_inc(em.s("q1_ap"), 1)

            @blk.gpsimd
            def _(e):
                e.wait_ge(em.s("q1_ap"), ND)
                e.dma_start(xn2_img, xn2_big[:]).then_inc(em.s("q1_st"), 16)
                e.wait_ge(em.s("q1_st"), 16)

            em.val["q1_bc"] = 1
            em.val["q1_ap"] = ND
            em.val["q1_st"] = 16
            em.val["q1_ld2"] = 16 * (ND // 2)
            em.val["q1_ld3"] = 16 * (ND // 2)


def phase_q2(nc, em, wq, wk, wv, xn23, oat_img, lam_b, ident_b):
    em.roll()
    with ExitStack() as sp, nc.Block() as blk:
        for name in ["q2_wl0", "q2_wl1", "q2_qg", "q2_qe", "q2_vt", "q2_ve",
                     "q2_sc", "q2_e", "q2_d", "q2_ed", "q2_aw", "q2_tp",
                     "q2_awe", "q2_av", "q2_oev", "q2_fv", "q2_ost0",
                     "q2_ost1"]:
            em.new_sem(name)
        sb = lambda n_, s_, d_: sp.enter_context(nc.sbuf_tensor(n_, s_, d_))
        wqh = [sb(f"wqh{k}", [P, ND * 256], BF16) for k in range(2)]
        wkh = [sb(f"wkh{k}", [P, ND * 256], BF16) for k in range(2)]
        wvh = [sb(f"wvh{k}", [P, ND * P], BF16) for k in range(2)]
        q1f = [sb(f"q1f{k}", [P, T], BF16) for k in range(2)]
        q2f = [sb(f"q2f{k}", [P, T], BF16) for k in range(2)]
        k1f = [sb(f"k1f{k}", [P, T], BF16) for k in range(2)]
        k2f = [sb(f"k2f{k}", [P, T], BF16) for k in range(2)]
        vf = [sb(f"vf{k}", [P, T], BF16) for k in range(2)]
        vt = sb("vt", [P, T], BF16)
        e1b = [sb(f"e1b{k}", [P, T], BF16) for k in range(2)]
        e2b = sb("e2b", [P, T], BF16)
        w2b = sb("w2b", [P, T], BF16)
        db = [sb(f"db{k}", [P, T], BF16) for k in range(2)]
        edb = [sb(f"edb{k}", [P, T], BF16) for k in range(2)]
        awb = [sb(f"awb{k}", [P, T], BF16) for k in range(2)]
        awsb = [sb(f"awsb{k}", [P, T], BF16) for k in range(2)]
        oath = sb("oath", [P, T], BF16)
        sum1 = [sb(f"sum1{k}", [P, 1], F32) for k in range(2)]
        sum2 = [sb(f"sum2{k}", [P, 1], F32) for k in range(2)]
        sumd = [sb(f"sumd{k}", [P, 1], F32) for k in range(2)]
        inv1 = sb("inv1", [P, 1], F32)
        inv2v = sb("inv2v", [P, 1], F32)
        invd = sb("invd", [P, 1], F32)
        l2s = sb("l2s", [P, 1], F32)
        ps_sc_t = sp.enter_context(nc.psum_tensor("ps_sc", [P, T], F32))
        ps_tp_t = sp.enter_context(nc.psum_tensor("ps_tp", [P, T // 2], F32))
        ps_av_t = sp.enter_context(nc.psum_tensor("ps_av", [P, P], F32))
        ps_qkv_t = sp.enter_context(nc.psum_tensor("ps_qkv", [P, TC], F32))
        ps_sc = ps_sc_t[:]
        ps_av = ps_av_t[:]
        ps_qkv = ps_qkv_t[:]
        pstp_bf = ps_tp_t.bitcast(BF16)
        wq3 = wq.rearrange("(c p) e -> p c e", p=P)
        wk3 = wk.rearrange("(c p) e -> p c e", p=P)
        wv3 = wv.rearrange("(c p) e -> p c e", p=P)

        def qkv_lhsT(h, oi, dc):
            if oi == 0:
                return wqh[h % 2][:, dc * 256:dc * 256 + 128]
            if oi == 1:
                return wqh[h % 2][:, dc * 256 + 128:dc * 256 + 256]
            if oi == 2:
                return wkh[h % 2][:, dc * 256:dc * 256 + 128]
            if oi == 3:
                return wkh[h % 2][:, dc * 256 + 128:dc * 256 + 256]
            return wvh[h % 2][:, dc * 128:dc * 128 + 128]

        def qkv_dst(h, oi):
            return [q1f, q2f, k1f, k2f, vf][oi][h % 2]

        PREF = {qt: ([2 * qt, 2 * qt + 1] if qt < 4 else [qt + 4])
                for qt in range(NTT)}

        @blk.gpsimd
        def _(e):
            def load(h):
                if h >= 2:
                    e.wait_ge(em.s("q2_qg"), 20 * (h - 1))
                e.dma_start(wqh[h % 2][:].rearrange("p (c e) -> p c e", e=256),
                            wq3[:, :, h * 256:(h + 1) * 256]
                            ).then_inc(em.s(f"q2_wl{h % 2}"), 16)
                e.dma_start(wkh[h % 2][:].rearrange("p (c e) -> p c e", e=256),
                            wk3[:, :, h * 256:(h + 1) * 256]
                            ).then_inc(em.s(f"q2_wl{h % 2}"), 16)
                e.dma_start(wvh[h % 2][:].rearrange("p (c e) -> p c e", e=128),
                            wv3[:, :, h * 128:(h + 1) * 128]
                            ).then_inc(em.s(f"q2_wl{h % 2}"), 16)

            load(0)
            load(1)
            for h in range(NH):
                if h + 2 < NH:
                    load(h + 2)
                e.wait_ge(em.s("q2_oev"), NH * (h + 1))
                e.dma_start(oat_img[h], oath[:]).then_inc(em.s("q2_ost0"), 16)
            e.wait_ge(em.s("q2_ost0"), 16 * NH)

        def pe_group(e, h, g):
            oi, tc = divmod((g - 1) % 20, NTC)
            if g >= 2:
                e.wait_ge(em.s("q2_qe"), g - 1)
            for dc in range(ND):
                inst = e.matmul(ps_qkv, qkv_lhsT(h, oi, dc),
                                xn23[:, dc, tc * TC:(tc + 1) * TC],
                                start=(dc == 0), stop=(dc == ND - 1))
            inst.then_inc(em.s("q2_qg"), 1)

        def pe_tp_av(e, m):
            hh, qq = divmod(m, NTT)
            e.wait_ge(em.s("q2_aw"), m + 1)
            e.wait_ge(em.s("q2_awe"), m)
            if qq == 0:
                e.wait_ge(em.s("q2_ve"), hh + 1)
            for kt in range(NTT):
                inst = e.transpose(pstp_bf[:, kt * P:(kt + 1) * P],
                                   awb[m % 2][:, kt * P:(kt + 1) * P],
                                   ident_b[:])
            inst.then_inc(em.s("q2_tp"), 1)
            e.wait_ge(em.s("q2_awe"), m + 1)
            e.wait_ge(em.s("q2_oev"), m)
            for kt in range(NTT):
                inst = e.matmul(ps_av, vt[:, kt * P:(kt + 1) * P],
                                awsb[m % 2][:, kt * P:(kt + 1) * P],
                                start=(kt == 0), stop=(kt == NTT - 1))
            inst.then_inc(em.s("q2_av"), 1)

        @blk.tensor
        def _(e):
            e.wait_ge(em.s("q2_wl0"), 48)
            for g in range(1, 21):
                pe_group(e, 0, g)
            e.wait_ge(em.s("q2_qe"), 20)
            for kt in range(NTT):
                inst = e.transpose(pstp_bf[:, kt * P:(kt + 1) * P],
                                   vf[0][:, kt * P:(kt + 1) * P], ident_b[:])
            inst.then_inc(em.s("q2_vt"), 1)
            for n in range(NH * NTT):
                h, qt = divmod(n, NTT)
                pre = PREF[qt] if h < NH - 1 else []
                for j in pre:
                    if j == 0:
                        e.wait_ge(em.s(f"q2_wl{(h + 1) % 2}"),
                                  48 * ((h + 1) // 2 + 1))
                    pe_group(e, h + 1, 20 * (h + 1) + j + 1)
                if n >= 1:
                    e.wait_ge(em.s("q2_e"), 2 * n)
                for tc in range(NTC):
                    inst = e.matmul(ps_sc[:, tc * TC:(tc + 1) * TC],
                                    q1f[h % 2][:, qt * P:(qt + 1) * P],
                                    k1f[h % 2][:, tc * TC:(tc + 1) * TC],
                                    start=True, stop=True)
                inst.then_inc(em.s("q2_sc"), 1)
                e.wait_ge(em.s("q2_e"), 2 * n + 1)
                for tc in range(NTC):
                    inst = e.matmul(ps_sc[:, tc * TC:(tc + 1) * TC],
                                    q2f[h % 2][:, qt * P:(qt + 1) * P],
                                    k2f[h % 2][:, tc * TC:(tc + 1) * TC],
                                    start=True, stop=True)
                inst.then_inc(em.s("q2_sc"), 1)
                if n >= 1:
                    pe_tp_av(e, n - 1)
                    hh, qq = divmod(n - 1, NTT)
                    if qq == NTT - 1 and hh < NH - 1:
                        e.wait_ge(em.s("q2_qe"), 20 * (hh + 2))
                        for kt in range(NTT):
                            inst = e.transpose(
                                pstp_bf[:, kt * P:(kt + 1) * P],
                                vf[(hh + 1) % 2][:, kt * P:(kt + 1) * P],
                                ident_b[:])
                        inst.then_inc(em.s("q2_vt"), 1)
            pe_tp_av(e, NH * NTT - 1)

        @blk.scalar
        def _(e):
            for h in range(NH):
                for qt in range(NTT):
                    n = NH * h + qt
                    e.wait_ge(em.s("q2_sc"), 2 * n + 1)
                    if n >= 2:
                        e.wait_ge(em.s("q2_d"), n - 1)
                    e.activation(e1b[n % 2][:], ps_sc, AF.Exp, scale=SCALE_QK,
                                 accum_out=sum1[n % 2][:]).then_inc(em.s("q2_e"), 1)
                    e.wait_ge(em.s("q2_sc"), 2 * n + 2)
                    if n >= 1:
                        e.wait_ge(em.s("q2_d"), n)
                    e.activation(e2b[:], ps_sc, AF.Exp, scale=SCALE_QK,
                                 accum_out=sum2[n % 2][:]).then_inc(em.s("q2_e"), 1)
                    e.wait_ge(em.s("q2_d"), n + 1)
                    if n >= 2:
                        e.wait_ge(em.s("q2_aw"), n - 1)
                    e.activation(edb[n % 2][:], db[n % 2][:], AF.Exp,
                                 accum_out=sumd[n % 2][:]).then_inc(em.s("q2_ed"), 1)

        @blk.vector
        def _(e):
            fv = [0]

            def fence(inst):
                fv[0] += 1
                inst.then_inc(em.s("q2_fv"), 1)
                e.wait_ge(em.s("q2_fv"), fv[0])

            def qkv_evac(h, g):
                oi, tc = divmod((g - 1) % 20, NTC)
                e.wait_ge(em.s("q2_qg"), g)
                e.tensor_copy(qkv_dst(h, oi)[:, tc * TC:(tc + 1) * TC],
                              ps_qkv).then_inc(em.s("q2_qe"), 1)

            def pref_evacs(n):
                h, qt = divmod(n, NTT)
                if h >= NH - 1:
                    return
                if qt == 0:
                    e.wait_ge(em.s("q2_sc"), 32 * h)
                for j in PREF[qt]:
                    qkv_evac(h + 1, 20 * (h + 1) + j + 1)

            for g in range(1, 21):
                qkv_evac(0, g)
            e.wait_ge(em.s("q2_vt"), 1)
            e.tensor_copy(vt[:], pstp_bf[:, 0:T]).then_inc(em.s("q2_ve"), 1)
            pref_evacs(0)
            for h in range(NH):
                for qt in range(NTT):
                    n = NH * h + qt
                    e.wait_ge(em.s("q2_e"), 2 * n + 2)
                    fence(e.reciprocal(inv1[:], sum1[n % 2][:]))
                    fence(e.reciprocal(inv2v[:], sum2[n % 2][:]))
                    fence(e.tensor_scalar(l2s[:], inv2v[:], lam_b[:, h:h + 1],
                                          None, ALU.mult))
                    e.tensor_scalar(w2b[:], e2b[:], l2s[:, 0:1], None,
                                    ALU.mult)
                    if n >= 2:
                        e.wait_ge(em.s("q2_ed"), n - 1)
                    e.scalar_tensor_tensor(db[n % 2][:], e1b[n % 2][:],
                                           inv1[:, 0:1], w2b[:], ALU.mult,
                                           ALU.subtract).then_inc(em.s("q2_d"), 1)
                    e.wait_ge(em.s("q2_ed"), n + 1)
                    fence(e.reciprocal(invd[:], sumd[n % 2][:]))
                    if n >= 2:
                        e.wait_ge(em.s("q2_tp"), n - 1)
                    e.tensor_scalar(awb[n % 2][:], edb[n % 2][:], invd[:, 0:1],
                                    None, ALU.mult).then_inc(em.s("q2_aw"), 1)
                    pref_evacs(n + 1)
                    e.wait_ge(em.s("q2_tp"), n + 1)
                    if n >= 2:
                        e.wait_ge(em.s("q2_av"), n - 1)
                    e.tensor_copy(awsb[n % 2][:], pstp_bf[:, 0:T]
                                  ).then_inc(em.s("q2_awe"), 1)
                    e.wait_ge(em.s("q2_av"), n + 1)
                    if qt == 0 and h >= 1:
                        e.wait_ge(em.s("q2_ost0"), 16 * h)
                    e.tensor_copy(oath[:, qt * P:(qt + 1) * P],
                                  ps_av).then_inc(em.s("q2_oev"), 1)
                    if qt == NTT - 1 and h < NH - 1:
                        e.wait_ge(em.s("q2_vt"), h + 2)
                        e.tensor_copy(vt[:], pstp_bf[:, 0:T]
                                      ).then_inc(em.s("q2_ve"), 1)


        em.val["q2_wl0"] = 48 * (NH // 2)
        em.val["q2_wl1"] = 48 * (NH // 2)
        em.val["q2_qg"] = 20 * NH
        em.val["q2_qe"] = 20 * NH
        em.val["q2_vt"] = NH
        em.val["q2_ve"] = NH
        em.val["q2_sc"] = 2 * NH * NTT
        em.val["q2_e"] = 2 * NH * NTT
        em.val["q2_d"] = NH * NTT
        em.val["q2_ed"] = NH * NTT
        em.val["q2_aw"] = NH * NTT
        em.val["q2_tp"] = NH * NTT
        em.val["q2_awe"] = NH * NTT
        em.val["q2_av"] = NH * NTT
        em.val["q2_oev"] = NH * NTT
        em.val["q2_fv"] = 4 * NH * NTT
        em.val["q2_ost0"] = 16 * NH
        em.val["q2_ost1"] = 0


def phase_q3(nc, em, oat_img, ones_bf, ones1_f, cst, nsc, om):
    em.roll()
    with ExitStack() as sp, nc.Block() as blk:
        for name in ["q3_ld0", "q3_ld1", "q3_ld2", "q3_ld3", "q3_sq", "q3_mm",
                     "q3_row", "q3_fs", "q3_bc", "q3_t", "q3_ap", "q3_st0",
                     "q3_st1"]:
            em.new_sem(name)
        sb = lambda n_, s_, d_: sp.enter_context(nc.sbuf_tensor(n_, s_, d_))
        oatt = [sb(f"q3o{k}", [P, T], BF16) for k in range(2)]
        sqa = [sb(f"q3sq{k}", [P, T], BF16) for k in range(2)]
        nrm = [sb(f"q3n{k}", [P, T], BF16) for k in range(2)]
        msrow = sb("msrow", [1, D], F32)[:]
        lnr = sb("lnr", [1, D], F32)[:]
        rstdr = sb("rstdr", [1, D], F32)[:]
        rows2 = sb("q3rows2", [1, D], F32)
        rstdr2 = rows2[:]
        tmp = [sb(f"q3tmp{k}", [P, T], F32) for k in range(2)]

        with ExitStack() as sps:
            ps_ms = sps.enter_context(nc.psum_tensor("ps_ms", [1, D], F32))

            @blk.sync
            def _(e):
                for ft in range(ND):
                    if ft >= 2:
                        e.wait_ge(em.s("q3_sq"), ft - 1)
                    e.dma_start(oatt[ft % 2][:], oat_img[ft]
                                ).then_inc(em.s(f"q3_ld{ft % 2}"), 16)

            @blk.scalar
            def _(e):
                for ft in range(ND):
                    e.wait_ge(em.s(f"q3_ld{ft % 2}"), 16 * (ft // 2 + 1))
                    if ft >= 2:
                        e.wait_ge(em.s("q3_mm"), ft - 1)
                    e.activation(sqa[ft % 2][:], oatt[ft % 2][:], AF.Square
                                 ).then_inc(em.s("q3_sq"), 1)
                e.wait_ge(em.s("q3_row"), 1)
                e.activation(lnr, msrow, AF.Ln, bias=cst[0:1, 2:3]
                             ).then_inc(em.s("q3_fs"), 1)
                e.wait_ge(em.s("q3_fs"), 1)
                e.activation(rstdr, lnr, AF.Exp, scale=-0.5
                             ).then_inc(em.s("q3_fs"), 1)
                e.wait_ge(em.s("q3_fs"), 2)
                e.nop().then_inc(em.s("q3_row"), 1)

            @blk.tensor
            def _(e):
                for ft in range(ND):
                    e.wait_ge(em.s("q3_sq"), ft + 1)
                    for tc in range(NTC):
                        inst = e.matmul(ps_ms[:, tc * TC:(tc + 1) * TC],
                                        ones_bf[:],
                                        sqa[ft % 2][:, tc * TC:(tc + 1) * TC],
                                        start=(ft == 0), stop=(ft == ND - 1))
                    inst.then_inc(em.s("q3_mm"), 1)

            @blk.vector
            def _(e):
                e.wait_ge(em.s("q3_mm"), ND)
                e.tensor_scalar(msrow, ps_ms[:], 1.0 / D, None, ALU.mult
                                ).then_inc(em.s("q3_row"), 1)
                e.wait_ge(em.s("q3_row"), 2)
                e.tensor_scalar(rstdr2, rstdr, om[0:1, 0:1], None,
                                ALU.mult).then_inc(em.s("q3_row"), 1)

            em.val["q3_sq"] = ND
            em.val["q3_mm"] = ND
            em.val["q3_row"] = 3
            em.val["q3_fs"] = 2
            em.val["q3_ld0"] = 16 * (ND // 2)
            em.val["q3_ld1"] = 16 * (ND // 2)

        with ExitStack() as sps2:
            ps_R = sps2.enter_context(nc.psum_tensor("ps_R", [P, D], F32))

            @blk.tensor
            def _(e):
                e.wait_ge(em.s("q3_row"), 3)
                for tc in range(NTC):
                    inst = e.matmul(ps_R[:, tc * TC:(tc + 1) * TC],
                                    ones1_f[0:1, :],
                                    rows2[0:1, tc * TC:(tc + 1) * TC],
                                    start=True, stop=True)
                inst.then_inc(em.s("q3_bc"), 1)

            @blk.sync
            def _(e):
                e.wait_ge(em.s("q3_mm"), ND)
                for ft in range(ND):
                    if ft >= 2:
                        e.wait_ge(em.s("q3_t"), ft - 1)
                    e.dma_start(oatt[ft % 2][:], oat_img[ft]
                                ).then_inc(em.s(f"q3_ld{2 + ft % 2}"), 16)

            @blk.vector
            def _(e):
                e.wait_ge(em.s("q3_bc"), 1)
                for ft in range(ND):
                    e.wait_ge(em.s(f"q3_ld{2 + ft % 2}"), 16 * (ft // 2 + 1))
                    if ft >= 2:
                        e.wait_ge(em.s("q3_ap"), ft - 1)
                    e.tensor_tensor(tmp[ft % 2][:], oatt[ft % 2][:], ps_R[:],
                                    ALU.mult).then_inc(em.s("q3_t"), 1)

            @blk.scalar
            def _(e):
                for ft in range(ND):
                    e.wait_ge(em.s("q3_t"), ft + 1)
                    if ft >= 2:
                        e.wait_ge(em.s(f"q3_st{ft % 2}"), 16 * (ft // 2))
                    e.activation(nrm[ft % 2][:], tmp[ft % 2][:], AF.Identity,
                                 scale=nsc[:, ft:ft + 1]).then_inc(em.s("q3_ap"), 1)

            @blk.gpsimd
            def _(e):
                for ft in range(ND):
                    e.wait_ge(em.s("q3_ap"), ft + 1)
                    e.dma_start(oat_img[ft], nrm[ft % 2][:]
                                ).then_inc(em.s(f"q3_st{ft % 2}"), 16)
                e.wait_ge(em.s("q3_st0"), 16 * (ND // 2))
                e.wait_ge(em.s("q3_st1"), 16 * (ND // 2))

            em.val["q3_bc"] = 1
            em.val["q3_t"] = ND
            em.val["q3_ap"] = ND
            em.val["q3_ld2"] = 16 * (ND // 2)
            em.val["q3_ld3"] = 16 * (ND // 2)
            em.val["q3_st0"] = 16 * (ND // 2)
            em.val["q3_st1"] = 16 * (ND // 2)


def phase_q4(nc, em, wo, f_img, bs_t, x1_img, xn2_img, oat_img, out, ident_f):
    em.roll()
    with ExitStack() as sp, nc.Block() as blk:
        for name in ["q4_xn", "q4_wl0", "q4_wl1", "q4_fl0", "q4_fl1",
                     "q4_x10", "q4_x11", "q4_mm", "q4_mx", "q4_cb", "q4_tp",
                     "q4_ev", "q4_st"]:
            em.new_sem(name)
        sb = lambda n_, s_, d_: sp.enter_context(nc.sbuf_tensor(n_, s_, d_))
        xn2w = sb("xn2w", [P, ND * TC], BF16)
        nrmw = sb("nrmw", [P, ND * TC], BF16)
        woi = [sb(f"woi{k}", [P, D], BF16) for k in range(2)]
        fii = [sb(f"fii{k}", [P, D], BF16) for k in range(2)]
        sx1 = [sb(f"sx1{k}", [P, TC], BF16) for k in range(2)]
        mxb = [sb(f"mxb{k}", [P, TC], F32) for k in range(2)]
        t4 = sb("t4", [P, TC], F32)
        ofb = [sb(f"ofb{k}", [P, TC], F32) for k in range(2)]
        stg = [sb(f"stg{k}", [P, D], F32) for k in range(4)]
        ps_wo = [sp.enter_context(nc.psum_tensor(f"ps_wo{k}", [P, TC], F32))
                 for k in range(2)]
        ps_F = [sp.enter_context(nc.psum_tensor(f"ps_F{k}", [P, TC], F32))
                for k in range(2)]
        ps_tr = sp.enter_context(nc.psum_tensor("ps_tr", [P, TC], F32))
        oat3 = nrmw[:].rearrange("p (c t) -> p c t", c=ND)
        xn2w3 = xn2w[:].rearrange("p (c t) -> p c t", c=ND)
        wo3 = wo.rearrange("(c p) e -> p c e", p=P)
        NSUB = TC // P

        @blk.sync
        def _(e):
            for tc in range(NTC):
                if tc >= 1:
                    e.wait_ge(em.s("q4_mm"), 2 * ND * tc)
                e.dma_start(xn2w3,
                            xn2_img.rearrange("p (c t) -> p c t", c=ND)
                            [:, :, tc * TC:(tc + 1) * TC]
                            ).then_inc(em.s("q4_xn"), 16)
                e.dma_start(oat3,
                            oat_img[:, :, tc * TC:(tc + 1) * TC]
                            .rearrange("c p t -> p c t")
                            ).then_inc(em.s("q4_xn"), 16)
                for et in range(ND):
                    m = tc * ND + et
                    if m >= 2:
                        e.wait_ge(em.s("q4_mm"), 2 * (m - 1))
                    e.dma_start(fii[et % 2][:].rearrange("p (c e) -> p c e", e=P),
                                f_img[:, :, et * P:(et + 1) * P]
                                .rearrange("c p e -> p c e")
                                ).then_inc(em.s(f"q4_fl{et % 2}"), 16)
                    if m >= 2:
                        e.wait_ge(em.s("q4_cb"), m - 1)
                    e.dma_start(sx1[et % 2][:],
                                x1_img.rearrange("p (c t) -> p c t", c=ND)
                                [:, et, tc * TC:(tc + 1) * TC]
                                ).then_inc(em.s(f"q4_x1{et % 2}"), 16)

        @blk.gpsimd
        def _(e):
            for tc in range(NTC):
                for et in range(ND):
                    m = tc * ND + et
                    if m >= 2:
                        e.wait_ge(em.s("q4_mm"), 2 * (m - 1))
                    e.dma_start(woi[et % 2][:].rearrange("p (c e) -> p c e", e=P),
                                wo3[:, :, et * P:(et + 1) * P]
                                ).then_inc(em.s(f"q4_wl{et % 2}"), 16)
                e.wait_ge(em.s("q4_ev"), (tc + 1) * ND * NSUB)
                for su in range(NSUB):
                    e.dma_start(out[(tc * NSUB + su) * P:(tc * NSUB + su + 1) * P, :],
                                stg[su][:]).then_inc(em.s("q4_st"), 16)
            e.wait_ge(em.s("q4_st"), 16 * NTC * NSUB)

        @blk.tensor
        def _(e):
            for tc in range(NTC):
                e.wait_ge(em.s("q4_xn"), 32 * (tc + 1))
                for et in range(ND):
                    m = tc * ND + et
                    if m >= 1:
                        mm = m - 1
                        e.wait_ge(em.s("q4_cb"), mm + 1)
                        if mm >= 1:
                            e.wait_ge(em.s("q4_ev"), mm * NSUB)
                        for su in range(NSUB):
                            inst = e.transpose(ps_tr[:, su * P:(su + 1) * P],
                                               ofb[mm % 2][:, su * P:(su + 1) * P],
                                               ident_f[:])
                        inst.then_inc(em.s("q4_tp"), 1)
                    e.wait_ge(em.s(f"q4_wl{et % 2}"), 16 * (m // 2 + 1))
                    e.wait_ge(em.s(f"q4_fl{et % 2}"), 16 * (m // 2 + 1))
                    if m >= 2:
                        e.wait_ge(em.s("q4_cb"), m - 1)
                    for fc in range(ND):
                        inst = e.matmul(ps_wo[et % 2][:],
                                        woi[et % 2][:, fc * P:(fc + 1) * P],
                                        oat3[:, fc, :],
                                        start=(fc == 0), stop=(fc == ND - 1))
                    inst.then_inc(em.s("q4_mm"), 1)
                    for dc in range(ND):
                        inst = e.matmul(ps_F[et % 2][:],
                                        fii[et % 2][:, dc * P:(dc + 1) * P],
                                        xn2w3[:, dc, :],
                                        start=(dc == 0), stop=(dc == ND - 1))
                    inst.then_inc(em.s("q4_mm"), 1)
            mm = NTC * ND - 1
            e.wait_ge(em.s("q4_cb"), mm + 1)
            e.wait_ge(em.s("q4_ev"), mm * NSUB)
            for su in range(NSUB):
                inst = e.transpose(ps_tr[:, su * P:(su + 1) * P],
                                   ofb[mm % 2][:, su * P:(su + 1) * P],
                                   ident_f[:])
            inst.then_inc(em.s("q4_tp"), 1)

        @blk.scalar
        def _(e):
            for tc in range(NTC):
                for et in range(ND):
                    m = tc * ND + et
                    e.wait_ge(em.s("q4_mm"), 2 * m + 2)
                    if m >= 2:
                        e.wait_ge(em.s("q4_cb"), m - 1)
                    e.activation(mxb[et % 2][:], ps_F[et % 2][:], AF.Identity,
                                 bias=bs_t[:, et:et + 1]).then_inc(em.s("q4_mx"), 1)

        @blk.vector
        def _(e):
            for tc in range(NTC):
                for et in range(ND):
                    m = tc * ND + et
                    if m >= 1:
                        mm = m - 1
                        tcc, ett = divmod(mm, ND)
                        e.wait_ge(em.s("q4_tp"), mm + 1)
                        if tcc >= 1 and ett == 0:
                            e.wait_ge(em.s("q4_st"), 16 * NSUB * tcc)
                        for su in range(NSUB):
                            inst = e.tensor_copy(
                                stg[su][:, ett * P:(ett + 1) * P],
                                ps_tr[:, su * P:(su + 1) * P])
                        inst.then_inc(em.s("q4_ev"), NSUB)
                    e.wait_ge(em.s("q4_mx"), m + 1)
                    e.wait_ge(em.s(f"q4_x1{et % 2}"), 16 * (m // 2 + 1))
                    if m >= 2:
                        e.wait_ge(em.s("q4_tp"), m - 1)
                    e.tensor_tensor(t4[:], ps_wo[et % 2][:], mxb[et % 2][:],
                                    ALU.add)
                    e.tensor_tensor(ofb[m % 2][:], t4[:], sx1[et % 2][:],
                                    ALU.add).then_inc(em.s("q4_cb"), 1)
            mm = NTC * ND - 1
            tcc, ett = divmod(mm, ND)
            e.wait_ge(em.s("q4_tp"), mm + 1)
            if ett == 0:
                e.wait_ge(em.s("q4_st"), 16 * NSUB * tcc)
            for su in range(NSUB):
                inst = e.tensor_copy(stg[su][:, ett * P:(ett + 1) * P],
                                     ps_tr[:, su * P:(su + 1) * P])
            inst.then_inc(em.s("q4_ev"), NSUB)

        em.val["q4_xn"] = 32 * NTC
        em.val["q4_wl0"] = 16 * NTC * ND // 2
        em.val["q4_wl1"] = 16 * NTC * ND // 2
        em.val["q4_fl0"] = 16 * NTC * ND // 2
        em.val["q4_fl1"] = 16 * NTC * ND // 2
        em.val["q4_x10"] = 16 * NTC * ND // 2
        em.val["q4_x11"] = 16 * NTC * ND // 2
        em.val["q4_mm"] = 2 * NTC * ND
        em.val["q4_mx"] = NTC * ND
        em.val["q4_cb"] = NTC * ND
        em.val["q4_tp"] = NTC * ND
        em.val["q4_ev"] = NSUB * NTC * ND
        em.val["q4_st"] = 16 * NTC * NSUB




def build_fused():
    nc = bass.Bass("TRN2", target_bir_lowering=False)

    x_in = nc.dram_tensor("x", [T, D], F32, kind="ExternalInput").ap()
    w_in = nc.dram_tensor("L0_attn_W_in", [D, D], F32, kind="ExternalInput").ap()
    b_in = nc.dram_tensor("L0_attn_b_in", [D], F32, kind="ExternalInput").ap()
    w_gate = nc.dram_tensor("L0_attn_W_gate", [D, D], F32, kind="ExternalInput").ap()
    b_gate = nc.dram_tensor("L0_attn_b_gate", [D], F32, kind="ExternalInput").ap()
    flam = nc.dram_tensor("L0_attn_forget_lambda", [D], F32, kind="ExternalInput").ap()
    gate_w = nc.dram_tensor("L0_ffn_gate_w", [IFF, D], I32, kind="ExternalInput").ap()
    gate_s = nc.dram_tensor("L0_ffn_gate_scale", [IFF], F32, kind="ExternalInput").ap()
    up_w = nc.dram_tensor("L0_ffn_up_w", [IFF, D], I32, kind="ExternalInput").ap()
    up_s = nc.dram_tensor("L0_ffn_up_scale", [IFF], F32, kind="ExternalInput").ap()
    down_w = nc.dram_tensor("L0_ffn_down_w", [D, IFF], I32, kind="ExternalInput").ap()
    down_s = nc.dram_tensor("L0_ffn_down_scale", [D], F32, kind="ExternalInput").ap()
    wq = nc.dram_tensor("L1_attn_Wq", [D, 2 * DH * NH], F32, kind="ExternalInput").ap()
    wk = nc.dram_tensor("L1_attn_Wk", [D, 2 * DH * NH], F32, kind="ExternalInput").ap()
    wv = nc.dram_tensor("L1_attn_Wv", [D, DH * NH], F32, kind="ExternalInput").ap()
    wo = nc.dram_tensor("L1_attn_Wo", [DH * NH, D], F32, kind="ExternalInput").ap()
    nsc_in = nc.dram_tensor("L1_attn_norm_scale", [D], F32, kind="ExternalInput").ap()
    lam_in = nc.dram_tensor("L1_attn_lam", [NH], F32, kind="ExternalInput").ap()
    factors = nc.dram_tensor("L1_ffn_factors", [8, D, D], F32, kind="ExternalInput").ap()
    bias_in = nc.dram_tensor("L1_ffn_bias", [D], F32, kind="ExternalInput").ap()

    out = nc.dram_tensor("out", [T, D], F32, kind="ExternalOutput").ap()
    x1_img = nc.dram_tensor("x1_img", [P, ND * T], BF16).ap()
    xn_img = nc.dram_tensor("xn_img", [P, ND * T], BF16).ap()
    st_img = nc.dram_tensor("st_img", [P, 2 * NTT], F32).ap()
    h_img = nc.dram_tensor("h_img", [ND, P, T], BF16).ap()
    cache_g = nc.dram_tensor("cache_g", [NIC, P, D], BF16).ap()
    cache_u = nc.dram_tensor("cache_u", [NIC, P, D], BF16).ap()
    cache_d = nc.dram_tensor("cache_d", [ND, P, IFF], BF16).ap()
    xn2_img = nc.dram_tensor("xn2_img", [P, ND * T], BF16).ap()
    oat_img = nc.dram_tensor("oat_img", [ND, P, T], BF16).ap()
    f_img = nc.dram_tensor("f_img", [ND, P, D], BF16).ap()

    with ExitStack() as st:
        em = Emit(nc, st)
        ident_f, ident_b, ones1_f, ones_bf, cst = make_identities(em, st, nc)

        with ExitStack() as st0:
            with nc.Block() as blk:
                ser = Ser(em, "p0")
                pst = st0.enter_context(nc.psum_tensor("p0ps", [P, P], F32))
                bh_gate = vec_table(ser, st, nc, "bh_gate", b_gate, ND, ident_f, pst, 0.5)
                bh_in = vec_table(ser, st, nc, "bh_in", b_in, ND, ident_f, pst, 0.5)
                lam_t = vec_table(ser, st, nc, "lam_t", flam, ND, ident_f, pst)
                gsc = vec_table(ser, st, nc, "gsc", gate_s, NIC, ident_f, pst)
                usc = vec_table(ser, st, nc, "usc", up_s, NIC, ident_f, pst)
                dsc = vec_table(ser, st, nc, "dsc", down_s, ND, ident_f, pst, 0.5)
                gbias = st.enter_context(nc.sbuf_tensor("gbias", [P, NIC], F32))
                ubias = st.enter_context(nc.sbuf_tensor("ubias", [P, NIC], F32))
                dbias = st.enter_context(nc.sbuf_tensor("dbias", [P, ND], F32))
                chalf = st.enter_context(nc.sbuf_tensor("chalf", [P, ND], F32))
                tmp_a = st0.enter_context(nc.sbuf_tensor("tmp_a", [P, ND], F32))
                tmp_b = st0.enter_context(nc.sbuf_tensor("tmp_b", [P, ND], F32))
                tmp_c = st0.enter_context(nc.sbuf_tensor("tmp_c", [P, ND], F32))
                ser.add("vector", lambda e: e.tensor_scalar(
                    gbias[:], gsc[:], -128.0, None, ALU.mult))
                ser.add("vector", lambda e: e.tensor_scalar(
                    ubias[:], usc[:], -128.0, None, ALU.mult))
                ser.add("vector", lambda e: e.tensor_scalar(
                    dbias[:], dsc[:], -128.0, None, ALU.mult))
                ser.add("scalar", lambda e: e.activation(tmp_a[:], lam_t[:], AF.Abs))
                ser.add("scalar", lambda e: e.activation(tmp_b[:], tmp_a[:], AF.Exp, scale=-1.0))
                ser.add("scalar", lambda e: e.activation(tmp_c[:], tmp_b[:], AF.Ln, bias=1.0))
                ser.add("scalar", lambda e: e.activation(tmp_a[:], lam_t[:], AF.Relu))
                ser.add("vector", lambda e: e.tensor_tensor(
                    tmp_b[:], tmp_c[:], tmp_a[:], ALU.add))
                ser.add("vector", lambda e: e.tensor_scalar(
                    chalf[:], tmp_b[:], -4.0, None, ALU.mult))
                ser.emit(blk, extra_wait=(em.s("s_id"), 2))

        with ExitStack() as st_xn:
            xn_big = st_xn.enter_context(nc.sbuf_tensor("xn_big", [P, ND * T], BF16))
            xn3 = xn_big[:].rearrange("p (c t) -> p c t", c=ND)
            with nc.named_scope("p1"):
                phase_p1(nc, em, x_in, xn_big, xn3, xn_img, ident_b, cst, st_img)
            with nc.named_scope("p2"):
                phase_p2(nc, em, w_in, w_gate, h_img, xn3, ident_b,
                         bh_in, bh_gate, chalf, cst)
        with nc.named_scope("prep"):
            phase_ffn_prep(nc, em, gate_w, up_w, down_w, cache_g, cache_u, cache_d,
                           gsc, gbias, usc, ubias, dsc, dbias, ident_b)
        with nc.named_scope("p3"):
            phase_p3(nc, em, x_in, xn_img, h_img, x1_img,
                     cache_g, cache_u, cache_d, ident_b)

    nc.reset()

    with ExitStack() as st:
        em = Emit(nc, st)
        ident_f, ident_b, ones1_f, ones_bf, cst = make_identities(em, st, nc, pfx="l1_")

        with ExitStack() as st0:
            with nc.Block() as blk:
                ser = Ser(em, "q0")
                pst = st0.enter_context(nc.psum_tensor("q0ps", [P, P], F32))
                nsc = vec_table(ser, st, nc, "nsc", nsc_in, ND, ident_f, pst)
                bs_t = vec_table(ser, st, nc, "bs_t", bias_in, ND, ident_f, pst)
                lam_row = st.enter_context(nc.sbuf_tensor("lam_row", [1, NH], F32))
                lam_b = st.enter_context(nc.sbuf_tensor("lam_b", [P, NH], F32))
                om = st.enter_context(nc.sbuf_tensor("om", [1, 1], F32))
                t_ml = st0.enter_context(nc.sbuf_tensor("t_ml", [1, 1], F32))
                t_ms = st0.enter_context(nc.sbuf_tensor("t_ms", [1, 1], F32))
                ser.add("sync", lambda e: e.dma_start(lam_row[:], lam_in.rearrange("(a b) -> a b", a=1)), dma=True)
                ser.add("tensor", lambda e: e.matmul(pst[:, 0:NH], ones1_f[0:1, :],
                                                     lam_row[:], start=True, stop=True))
                ser.add("vector", lambda e: e.tensor_copy(lam_b[:], pst[:, 0:NH]))
                ser.add("vector", lambda e: e.reduce_sum(t_ml[:], lam_row[:], axis=AX))
                ser.add("vector", lambda e: e.tensor_scalar(t_ms[:], t_ml[:], 1.0 / NH, None, ALU.mult))
                ser.add("vector", lambda e: e.tensor_scalar(om[:], t_ms[:], -1.0, 1.0, ALU.mult, ALU.add))
                ser.emit(blk, extra_wait=(em.s("s_id"), 2))

        em.roll()
        with ExitStack() as sf, nc.Block() as blk:
            for k in range(16):
                em.new_sem(f"q0f_ld{k}")
            for name in ["q0f_sum", "q0f_st"]:
                em.new_sem(name)
            fb = [sf.enter_context(nc.sbuf_tensor(f"fb{k}", [P, D], F32))
                  for k in range(16)]
            fta = sf.enter_context(nc.sbuf_tensor("fta", [P, D], F32))
            ftb = sf.enter_context(nc.sbuf_tensor("ftb", [P, D], F32))
            sFb = [sf.enter_context(nc.sbuf_tensor(f"sFb{k}", [P, D], BF16))
                   for k in range(2)]

            @blk.sync
            def _(e):
                for dc in range(ND):
                    for r in range(8):
                        k = (dc % 2) * 8 + r
                        if dc >= 2:
                            e.wait_ge(em.s("q0f_sum"), dc - 1)
                        e.dma_start(fb[k][:], factors[r, dc * P:(dc + 1) * P, :]
                                    ).then_inc(em.s(f"q0f_ld{k}"), 16)

            @blk.vector
            def _(e):
                for dc in range(ND):
                    ks = (dc % 2) * 8
                    for r in range(8):
                        e.wait_ge(em.s(f"q0f_ld{ks + r}"), 16 * (dc // 2 + 1))
                    e.tensor_tensor(fta[:], fb[ks][:], fb[ks + 1][:], ALU.add)
                    e.tensor_tensor(ftb[:], fta[:], fb[ks + 2][:], ALU.add)
                    e.tensor_tensor(fta[:], ftb[:], fb[ks + 3][:], ALU.add)
                    e.tensor_tensor(ftb[:], fta[:], fb[ks + 4][:], ALU.add)
                    e.tensor_tensor(fta[:], ftb[:], fb[ks + 5][:], ALU.add)
                    e.tensor_tensor(ftb[:], fta[:], fb[ks + 6][:], ALU.add)
                    e.tensor_tensor(fta[:], ftb[:], fb[ks + 7][:], ALU.add)
                    if dc >= 2:
                        e.wait_ge(em.s("q0f_st"), 16 * (dc - 1))
                    e.tensor_scalar(sFb[dc % 2][:], fta[:], 8.0 ** -0.5, None,
                                    ALU.mult).then_inc(em.s("q0f_sum"), 1)

            @blk.gpsimd
            def _(e):
                for dc in range(ND):
                    e.wait_ge(em.s("q0f_sum"), dc + 1)
                    e.dma_start(f_img[dc], sFb[dc % 2][:]
                                ).then_inc(em.s("q0f_st"), 16)
                e.wait_ge(em.s("q0f_st"), 16 * ND)

            em.val["q0f_sum"] = ND
            em.val["q0f_st"] = 16 * ND
            for k in range(16):
                em.val[f"q0f_ld{k}"] = 16 * (ND // 2)

        xn2_big = st.enter_context(nc.sbuf_tensor("xn2_big", [P, ND * T], BF16))
        xn23 = xn2_big[:].rearrange("p (c t) -> p c t", c=ND)
        with nc.named_scope("q1"):
            phase_q1(nc, em, x1_img, xn2_big, xn23, xn2_img, ones_bf, ones1_f, cst)
        with nc.named_scope("q2"):
            phase_q2(nc, em, wq, wk, wv, xn23, oat_img, lam_b, ident_b)
        with nc.named_scope("q3"):
            phase_q3(nc, em, oat_img, ones_bf, ones1_f, cst, nsc, om)
        with nc.named_scope("q4"):
            phase_q4(nc, em, wo, f_img, bs_t, x1_img, xn2_img, oat_img, out,
                     ident_f)

    return nc

def _split_inputs(inputs):
    keys = ["L0_attn_W_in", "L0_attn_b_in", "L0_attn_W_gate", "L0_attn_b_gate",
            "L0_attn_forget_lambda", "L0_ffn_gate_w", "L0_ffn_gate_scale",
            "L0_ffn_up_w", "L0_ffn_up_scale", "L0_ffn_down_w",
            "L0_ffn_down_scale"]
    shared = {k: np.ascontiguousarray(inputs[k]) for k in keys}
    x = inputs["L0_attn_x"]
    return [dict(shared, x=np.ascontiguousarray(x[b])) for b in range(8)]


def kernel(**inputs):
    nc = build_fused()
    keys = ["L0_attn_W_in", "L0_attn_b_in", "L0_attn_W_gate", "L0_attn_b_gate",
            "L0_attn_forget_lambda", "L0_ffn_gate_w", "L0_ffn_gate_scale",
            "L0_ffn_up_w", "L0_ffn_up_scale", "L0_ffn_down_w",
            "L0_ffn_down_scale", "L1_attn_Wq", "L1_attn_Wk", "L1_attn_Wv",
            "L1_attn_Wo", "L1_attn_norm_scale", "L1_attn_lam",
            "L1_ffn_factors", "L1_ffn_bias"]
    shared = {k: np.ascontiguousarray(inputs[k]) for k in keys}
    x = inputs["L0_attn_x"]
    in_maps = [dict(shared, x=np.ascontiguousarray(x[b])) for b in range(8)]
    res = run_bass_kernel_spmd(nc, in_maps, core_ids=list(range(8)))
    return np.stack([np.asarray(res.results[b]["out"]) for b in range(8)],
                    axis=0)


# version 07
# speedup vs baseline: 1.0498x; 1.0498x over previous
import numpy as np
import ml_dtypes
from contextlib import ExitStack
import concourse.bass as bass
import concourse.mybir as mybir
from concourse.bass_utils import run_bass_kernel_spmd

F32 = mybir.dt.float32
BF16 = mybir.dt.bfloat16
I32 = mybir.dt.int32
AF = mybir.ActivationFunctionType
ALU = mybir.AluOpType
AX = mybir.AxisListType.X

P = 128
D = 2048
T = 2048
ND = D // P
NTT = T // P
TC = 512
NTC = T // TC
IFF = 8192
NIC = IFF // P
WIN = 512
NW = T // WIN
NH = 16
DH = 128
EPS_LN = 1e-5
EPS_RMS = 1e-6
SCALE_QK = 1.0 / (DH ** 0.5)


class Emit:

    def __init__(self, nc, stack):
        self.nc = nc
        self.stack = stack
        self.sem = {}
        self.val = {}

    def new_sem(self, name):
        h = self.stack.enter_context(self.nc.semaphore(name))
        self.sem[name] = h
        self.val[name] = 0
        return h

    def roll(self):
        pass

    def s(self, name):
        return self.sem[name]


class Ser:

    def __init__(self, em, tag):
        self.em = em
        self.sc = em.new_sem(tag + "_c")
        self.sd = em.new_sem(tag + "_d")
        self.cn = tag + "_c"
        self.dn = tag + "_d"
        self.ops = []

    def add(self, eng, fn, dma=False):
        self.ops.append((eng, dma, fn))

    def emit(self, blk, extra_wait=None):
        plan = {}
        ncc, ndd = 0, 0
        for eng, dma, fn in self.ops:
            plan.setdefault(eng, []).append((ncc, ndd, dma, fn))
            if dma:
                ndd += 1
            else:
                ncc += 1
        self.em.val[self.cn] += ncc
        self.em.val[self.dn] += 16 * ndd

        def mk(items):
            def body(eng):
                if extra_wait is not None:
                    eng.wait_ge(extra_wait[0], extra_wait[1])
                for (c_b, d_b, dma, fn) in items:
                    eng.wait_ge(self.sc, c_b)
                    eng.wait_ge(self.sd, 16 * d_b)
                    inst = fn(eng)
                    inst.then_inc(self.sd if dma else self.sc, 16 if dma else 1)
            return body

        for eng_name, items in plan.items():
            getattr(blk, eng_name)(mk(items))


def make_identities(em, st, nc, pfx=""):
    io_a = st.enter_context(nc.sbuf_tensor(pfx + "io_a", [P, P], I32))
    io_b = st.enter_context(nc.sbuf_tensor(pfx + "io_b", [P, P], I32))
    ident_f = st.enter_context(nc.sbuf_tensor(pfx + "ident_f", [P, P], F32))
    ident_b = st.enter_context(nc.sbuf_tensor(pfx + "ident_b", [P, P], BF16))
    ones1_f = st.enter_context(nc.sbuf_tensor(pfx + "ones1_f", [1, P], F32))
    ones_bf = st.enter_context(nc.sbuf_tensor(pfx + "ones_bf", [P, 1], BF16))
    cst = st.enter_context(nc.sbuf_tensor(pfx + "cst", [P, 4], F32))
    s_id = em.new_sem(pfx + "s_id")
    em.sem["s_id"] = s_id
    em.val["s_id"] = 0
    with nc.Block() as blk:
        @blk.gpsimd
        def _(g):
            g.iota(io_a[:], [[1, P]], channel_multiplier=0)
            g.iota(io_b[:], [[0, P]], channel_multiplier=1)
            g.memset(ones1_f[:], 1.0)
            g.memset(cst[:, 0:1], EPS_LN)
            g.memset(cst[:, 1:2], 0.25 * (1.0 + 1e-6))
            g.memset(cst[:, 2:3], EPS_RMS)
            g.memset(ones_bf[:], 1.0).then_inc(s_id, 1)

        @blk.vector
        def _(v):
            v.wait_ge(s_id, 1)
            v.tensor_tensor(ident_f[:], io_a[:], io_b[:], ALU.is_equal)
            v.tensor_copy(ident_b[:], ident_f[:]).then_inc(s_id, 1)
    em.val["s_id"] = 2
    return ident_f, ident_b, ones1_f, ones_bf, cst


def vec_table(ser, st, nc, name, src_ap, n, ident_f, pst, scale=None):
    raw = st.enter_context(nc.sbuf_tensor(name + "_r", [n, P], F32))
    tbl = st.enter_context(nc.sbuf_tensor(name, [P, n], F32))
    ser.add("sync", lambda e: e.dma_start(
        raw[:], src_ap.rearrange("(j p) -> j p", j=n)), dma=True)
    ser.add("tensor", lambda e: e.transpose(
        pst[:, 0:n], raw[:], ident_f[0:n, 0:n]))
    if scale is None:
        ser.add("vector", lambda e: e.tensor_copy(tbl[:], pst[:, 0:n]))
    else:
        ser.add("vector", lambda e: e.tensor_scalar(
            tbl[:], pst[:, 0:n], scale, None, ALU.mult))
    return tbl



def build_l0(dbg=False):
    nc = bass.Bass("TRN2", target_bir_lowering=False)

    x_in = nc.dram_tensor("x", [T, D], F32, kind="ExternalInput").ap()
    w_in = nc.dram_tensor("L0_attn_W_in", [D, D], F32, kind="ExternalInput").ap()
    b_in = nc.dram_tensor("L0_attn_b_in", [D], F32, kind="ExternalInput").ap()
    w_gate = nc.dram_tensor("L0_attn_W_gate", [D, D], F32, kind="ExternalInput").ap()
    b_gate = nc.dram_tensor("L0_attn_b_gate", [D], F32, kind="ExternalInput").ap()
    flam = nc.dram_tensor("L0_attn_forget_lambda", [D], F32, kind="ExternalInput").ap()
    gate_w = nc.dram_tensor("L0_ffn_gate_w", [IFF, D], I32, kind="ExternalInput").ap()
    gate_s = nc.dram_tensor("L0_ffn_gate_scale", [IFF], F32, kind="ExternalInput").ap()
    up_w = nc.dram_tensor("L0_ffn_up_w", [IFF, D], I32, kind="ExternalInput").ap()
    up_s = nc.dram_tensor("L0_ffn_up_scale", [IFF], F32, kind="ExternalInput").ap()
    down_w = nc.dram_tensor("L0_ffn_down_w", [D, IFF], I32, kind="ExternalInput").ap()
    down_s = nc.dram_tensor("L0_ffn_down_scale", [D], F32, kind="ExternalInput").ap()

    x1_img = nc.dram_tensor("x1_img", [P, ND * T], BF16, kind="ExternalOutput").ap()
    dk = "ExternalOutput" if dbg else "Internal"
    xn_img = nc.dram_tensor("xn_img", [P, ND * T], BF16, kind=dk).ap()
    st_img = nc.dram_tensor("st_img", [P, 2 * NTT], F32, kind=dk).ap()
    h_img = nc.dram_tensor("h_img", [ND, P, T], BF16, kind=dk).ap()
    cache_g = nc.dram_tensor("cache_g", [NIC, P, D], BF16).ap()
    cache_u = nc.dram_tensor("cache_u", [NIC, P, D], BF16).ap()
    cache_d = nc.dram_tensor("cache_d", [ND, P, IFF], BF16).ap()

    with ExitStack() as st:
        em = Emit(nc, st)
        ident_f, ident_b, ones1_f, ones_bf, cst = make_identities(em, st, nc)

        with ExitStack() as st0:
            with nc.Block() as blk:
                ser = Ser(em, "p0")
                pst = st0.enter_context(nc.psum_tensor("p0ps", [P, P], F32))
                bh_gate = vec_table(ser, st, nc, "bh_gate", b_gate, ND, ident_f, pst, 0.5)
                bh_in = vec_table(ser, st, nc, "bh_in", b_in, ND, ident_f, pst, 0.5)
                lam_t = vec_table(ser, st, nc, "lam_t", flam, ND, ident_f, pst)
                gsc = vec_table(ser, st, nc, "gsc", gate_s, NIC, ident_f, pst)
                usc = vec_table(ser, st, nc, "usc", up_s, NIC, ident_f, pst)
                dsc = vec_table(ser, st, nc, "dsc", down_s, ND, ident_f, pst, 0.5)
                gbias = st.enter_context(nc.sbuf_tensor("gbias", [P, NIC], F32))
                ubias = st.enter_context(nc.sbuf_tensor("ubias", [P, NIC], F32))
                dbias = st.enter_context(nc.sbuf_tensor("dbias", [P, ND], F32))
                chalf = st.enter_context(nc.sbuf_tensor("chalf", [P, ND], F32))
                tmp_a = st0.enter_context(nc.sbuf_tensor("tmp_a", [P, ND], F32))
                tmp_b = st0.enter_context(nc.sbuf_tensor("tmp_b", [P, ND], F32))
                ser.add("vector", lambda e: e.tensor_scalar(
                    gbias[:], gsc[:], -128.0, None, ALU.mult))
                ser.add("vector", lambda e: e.tensor_scalar(
                    ubias[:], usc[:], -128.0, None, ALU.mult))
                ser.add("vector", lambda e: e.tensor_scalar(
                    dbias[:], dsc[:], -128.0, None, ALU.mult))
                tmp_c = st0.enter_context(nc.sbuf_tensor("tmp_c", [P, ND], F32))
                ser.add("scalar", lambda e: e.activation(tmp_a[:], lam_t[:], AF.Abs))
                ser.add("scalar", lambda e: e.activation(tmp_b[:], tmp_a[:], AF.Exp, scale=-1.0))
                ser.add("scalar", lambda e: e.activation(tmp_c[:], tmp_b[:], AF.Ln, bias=1.0))
                ser.add("scalar", lambda e: e.activation(tmp_a[:], lam_t[:], AF.Relu))
                ser.add("vector", lambda e: e.tensor_tensor(
                    tmp_b[:], tmp_c[:], tmp_a[:], ALU.add))
                ser.add("vector", lambda e: e.tensor_scalar(
                    chalf[:], tmp_b[:], -4.0, None, ALU.mult))
                ser.emit(blk, extra_wait=(em.s("s_id"), 2))

        with ExitStack() as st_xn:
            xn_big = st_xn.enter_context(nc.sbuf_tensor("xn_big", [P, ND * T], BF16))
            xn3 = xn_big[:].rearrange("p (c t) -> p c t", c=ND)
            phase_p1(nc, em, x_in, xn_big, xn3, xn_img, ident_b, cst, st_img)
            phase_p2(nc, em, w_in, w_gate, h_img, xn3, ident_b,
                     bh_in, bh_gate, chalf, cst)
        phase_ffn_prep(nc, em, gate_w, up_w, down_w, cache_g, cache_u, cache_d,
                       gsc, gbias, usc, ubias, dsc, dbias, ident_b)
        phase_p3(nc, em, x_in, xn_img, h_img, x1_img,
                 cache_g, cache_u, cache_d, ident_b)

    return nc


def phase_p1(nc, em, x_in, xn_big, xn3, xn_img, ident_b, cst, st_img):
    st = em.stack
    em.roll()
    with ExitStack() as sp, nc.Block() as blk:
        for name in ["p1_ld0", "p1_ld1", "p1_sqr", "p1_var", "p1_ap", "p1_tp",
                     "p1_ev", "p1_st", "p1_rr", "p1_fs"]:
            em.new_sem(name)
        sb = lambda n_, s_, d_: sp.enter_context(nc.sbuf_tensor(n_, s_, d_))
        pt = lambda n_, s_, d_: sp.enter_context(nc.psum_tensor(n_, s_, d_))
        bx = [sb(f"bx{k}", [P, D], F32) for k in range(2)]
        sqb = [sb(f"sqb{k}", [P, D], BF16) for k in range(2)]
        xntm = [sb(f"xntm{k}", [P, D], BF16) for k in range(2)]
        tfb = sb("tfb", [P, D], F32)
        s1t = [sb(f"s1t{k}", [P, 1], F32) for k in range(2)]
        s2t = sb("s2t", [P, 1], F32)
        nmean = [sb(f"nmean{k}", [P, 1], F32) for k in range(2)]
        varb = [sb(f"varb{k}", [P, 1], F32) for k in range(2)]
        rstdb = sb("rstdb", [P, 1], F32)
        lnb = sb("lnb", [P, 1], F32)
        varc = sb("varc", [P, NTT], F32)
        rstdc = sb("rstdc", [P, NTT], F32)
        s2m = sb("s2m", [P, 1], F32)
        m2t = sb("m2t", [P, 1], F32)
        pstr = [pt(f"pstr{k}", [P, D], BF16) for k in range(2)]

        @blk.sync
        def _(e):
            for i in range(NTT):
                if i >= 2:
                    e.wait_ge(em.s("p1_ap"), i - 1)
                e.dma_start(bx[i % 2][:], x_in[i * P:(i + 1) * P, :]
                            ).then_inc(em.s(f"p1_ld{i % 2}"), 16)

        @blk.scalar
        def _(e):
            for i in range(NTT):
                e.wait_ge(em.s(f"p1_ld{i % 2}"), 16 * (i // 2 + 1))
                if i >= 2:
                    e.wait_ge(em.s("p1_var"), i - 1)
                e.activation(sqb[i % 2][:], bx[i % 2][:], AF.Square
                             ).then_inc(em.s("p1_sqr"), 1)
                e.wait_ge(em.s("p1_var"), i + 1)
                e.activation(lnb[:], varb[i % 2][:], AF.Ln, bias=cst[:, 0:1]
                             ).then_inc(em.s("p1_fs"), 1)
                e.wait_ge(em.s("p1_fs"), 2 * i + 1)
                e.activation(rstdb[:], lnb[:], AF.Exp, scale=-0.5
                             ).then_inc(em.s("p1_fs"), 1)
                e.wait_ge(em.s("p1_fs"), 2 * i + 2)
                e.activation(rstdc[:, i:i + 1], varb[i % 2][:], AF.Copy)
                e.activation(tfb[:], bx[i % 2][:], AF.Identity,
                             bias=nmean[i % 2][:, 0:1])
                if i >= 2:
                    e.wait_ge(em.s("p1_tp"), i - 1)
                e.activation(xntm[i % 2][:], tfb[:], AF.Identity,
                             scale=rstdb[:, 0:1]).then_inc(em.s("p1_ap"), 1)

        @blk.vector
        def _(e):
            for i in range(NTT):
                e.wait_ge(em.s(f"p1_ld{i % 2}"), 16 * (i // 2 + 1))
                if i >= 2:
                    e.wait_ge(em.s("p1_ap"), i - 1)
                fv = 6 * i
                e.reduce_sum(s1t[i % 2][:], bx[i % 2][:], axis=AX
                             ).then_inc(em.s("p1_rr"), 1)
                e.wait_ge(em.s("p1_rr"), fv + 1)
                e.tensor_scalar(nmean[i % 2][:], s1t[i % 2][:],
                                -1.0 / D, None, ALU.mult
                                ).then_inc(em.s("p1_rr"), 1)
                e.wait_ge(em.s("p1_sqr"), i + 1)
                e.reduce_sum(s2t[:], sqb[i % 2][:], axis=AX
                             ).then_inc(em.s("p1_rr"), 1)
                e.wait_ge(em.s("p1_rr"), fv + 3)
                e.tensor_scalar(s2m[:], s2t[:], 1.0 / D, None, ALU.mult
                                ).then_inc(em.s("p1_rr"), 1)
                e.tensor_tensor(m2t[:], nmean[i % 2][:],
                                nmean[i % 2][:], ALU.mult
                                ).then_inc(em.s("p1_rr"), 1)
                e.wait_ge(em.s("p1_rr"), fv + 5)
                e.tensor_tensor(varb[i % 2][:], s2m[:], m2t[:],
                                ALU.subtract).then_inc(em.s("p1_rr"), 1)
                e.wait_ge(em.s("p1_rr"), fv + 6)
                e.tensor_copy(varc[:, i:i + 1], varb[i % 2][:]
                              ).then_inc(em.s("p1_var"), 1)
                e.wait_ge(em.s("p1_tp"), i + 1)
                e.tensor_copy(xn3[:, :, i * P:(i + 1) * P],
                              pstr[i % 2][:].rearrange("p (c t) -> p c t", c=ND)
                              ).then_inc(em.s("p1_ev"), 1)

        @blk.tensor
        def _(e):
            for i in range(NTT):
                e.wait_ge(em.s("p1_ap"), i + 1)
                if i >= 2:
                    e.wait_ge(em.s("p1_ev"), i - 1)
                for dc in range(ND):
                    inst = e.transpose(pstr[i % 2][:, dc * P:(dc + 1) * P],
                                       xntm[i % 2][:, dc * P:(dc + 1) * P],
                                       ident_b[:])
                inst.then_inc(em.s("p1_tp"), 1)

        @blk.gpsimd
        def _(e):
            e.wait_ge(em.s("p1_ev"), NTT)
            e.dma_start(xn_img, xn_big[:]).then_inc(em.s("p1_st"), 16)
            e.dma_start(st_img[:, 0:NTT], varc[:]).then_inc(em.s("p1_st"), 16)
            e.dma_start(st_img[:, NTT:2 * NTT], rstdc[:]).then_inc(em.s("p1_st"), 16)
            e.wait_ge(em.s("p1_st"), 48)

        em.val["p1_ld0"] = 16 * (NTT // 2)
        em.val["p1_ld1"] = 16 * (NTT // 2)
        em.val["p1_sqr"] = NTT
        em.val["p1_rr"] = 6 * NTT
        em.val["p1_var"] = NTT
        em.val["p1_fs"] = 2 * NTT
        em.val["p1_ap"] = NTT
        em.val["p1_tp"] = NTT
        em.val["p1_ev"] = NTT
        em.val["p1_st"] = 48


def phase_p2(nc, em, w_in, w_gate, h_img, xn3, ident_b, bh_in, bh_gate, chalf, cst):
    em.roll()
    with ExitStack() as sp, nc.Block() as blk:
        for name in ["p2_ld0", "p2_ld1", "p2_tp", "p2_ev", "p2_mm", "p2_th",
                     "p2_al", "p2_a2", "p2_sq", "p2_xb", "p2_hst0", "p2_hst1"]:
            em.new_sem(name)
        sb = lambda n_, s_, d_: sp.enter_context(nc.sbuf_tensor(n_, s_, d_))
        pt = lambda n_, s_, d_: sp.enter_context(nc.psum_tensor(n_, s_, d_))
        sWn = [sb(f"sWn{k}", [P, 2 * D], BF16) for k in range(2)]
        sWT = [sb(f"sWT{k}", [P, 2 * D], BF16) for k in range(2)]
        pswt = [pt(f"pswt{k}", [P, D], BF16) for k in range(2)]
        psg = [pt(f"psg{k}", [P, TC], F32) for k in range(4)]
        th_r = [sb(f"th_r{k}", [P, T], F32) for k in range(2)]
        th_i = [sb(f"th_i{k}", [P, T], F32) for k in range(2)]
        al_b = [sb(f"al_b{k}", [P, T], F32) for k in range(2)]
        a2_b = [sb(f"a2_b{k}", [P, T], F32) for k in range(2)]
        u_b = sb("u_b", [P, T], F32)
        xb_b = sb("xb_b", [P, T], F32)
        hf_b = sb("hf_b", [P, T], F32)
        hb_b = [sb(f"hb_b{k}", [P, T], BF16) for k in range(2)]

        def mmc(e_, g_, tc_):
            return 8 * e_ + 4 * g_ + tc_ + 1

        @blk.gpsimd
        def _(e):
            for et in range(ND):
                if et >= 2:
                    e.wait_ge(em.s("p2_tp"), et - 1)
                e.dma_start(sWn[et % 2][:, 0:D], w_gate[et * P:(et + 1) * P, :]
                            ).then_inc(em.s(f"p2_ld{et % 2}"), 16)
                e.dma_start(sWn[et % 2][:, D:2 * D], w_in[et * P:(et + 1) * P, :]
                            ).then_inc(em.s(f"p2_ld{et % 2}"), 16)
                if et >= 1:
                    e.wait_ge(em.s("p2_xb"), et)
                    e.dma_start(h_img[et - 1], hb_b[(et - 1) % 2][:]
                                ).then_inc(em.s(f"p2_hst{(et - 1) % 2}"), 16)
            e.wait_ge(em.s("p2_xb"), ND)
            e.dma_start(h_img[ND - 1], hb_b[(ND - 1) % 2][:]
                        ).then_inc(em.s(f"p2_hst{(ND - 1) % 2}"), 16)
            e.wait_ge(em.s("p2_hst0"), 16 * (ND // 2))
            e.wait_ge(em.s("p2_hst1"), 16 * (ND // 2))

        @blk.tensor
        def _(e):
            for et in range(ND):
                e.wait_ge(em.s(f"p2_ld{et % 2}"), 32 * (et // 2 + 1))
                if et >= 1:
                    e.wait_ge(em.s("p2_ev"), et)
                for w in range(2):
                    for dc in range(ND):
                        inst = e.transpose(
                            pswt[w][:, dc * P:(dc + 1) * P],
                            sWn[et % 2][:, w * D + dc * P:w * D + (dc + 1) * P],
                            ident_b[:])
                inst.then_inc(em.s("p2_tp"), 1)
                e.wait_ge(em.s("p2_ev"), et + 1)
                for g in range(2):
                    for tc in range(NTC):
                        if tc >= 2:
                            e.wait_ge(em.s("p2_th"), mmc(et, g, tc - 2))
                        elif et >= 1:
                            e.wait_ge(em.s("p2_th"), mmc(et - 1, g, tc + 2))
                        slot = 2 * g + (tc % 2)
                        for dc in range(ND):
                            inst = e.matmul(
                                psg[slot][:],
                                sWT[et % 2][:, g * D + dc * P:g * D + (dc + 1) * P],
                                xn3[:, dc, tc * TC:(tc + 1) * TC],
                                start=(dc == 0), stop=(dc == ND - 1))
                        inst.then_inc(em.s("p2_mm"), 1)

        @blk.scalar
        def _(e):
            for et in range(ND):
                for g in range(2):
                    th = th_r if g == 0 else th_i
                    bh = bh_gate if g == 0 else bh_in
                    if et >= 2 and g == 0:
                        e.wait_ge(em.s("p2_xb"), et - 1)
                    for tc in range(NTC):
                        e.wait_ge(em.s("p2_mm"), mmc(et, g, tc))
                        slot = 2 * g + (tc % 2)
                        e.activation(th[et % 2][:, tc * TC:(tc + 1) * TC],
                                     psg[slot][:], AF.Tanh, scale=0.5,
                                     bias=bh[:, et:et + 1]
                                     ).then_inc(em.s("p2_th"), 1)
                e.activation(al_b[et % 2][:], th_r[et % 2][:], AF.Exp,
                             scale=chalf[:, et:et + 1],
                             bias=chalf[:, et:et + 1]).then_inc(em.s("p2_al"), 1)
                if et % 2 == 1:
                    for e2 in (et - 1, et):
                        e.wait_ge(em.s("p2_a2"), e2 + 1)
                        e.activation(th_r[e2 % 2][:], a2_b[e2 % 2][:], AF.Sqrt,
                                     scale=-0.25, bias=cst[:, 1:2]
                                     ).then_inc(em.s("p2_sq"), 1)

        def chain(e, et):
            k = et % 2
            e.wait_ge(em.s("p2_al"), et + 1)
            if et >= 2:
                e.wait_ge(em.s("p2_sq"), et - 1)
            e.tensor_tensor(a2_b[k][:], al_b[k][:], al_b[k][:], ALU.mult
                            ).then_inc(em.s("p2_a2"), 1)
            e.wait_ge(em.s("p2_sq"), et + 1)
            e.tensor_tensor(u_b[:], th_r[k][:], xn3[:, et, :], ALU.mult)
            e.scalar_tensor_tensor(xb_b[:], th_i[k][:], 1.0, u_b[:],
                                   ALU.add, ALU.mult)
            e.tensor_tensor_scan(hf_b[:], al_b[k][:], xb_b[:], 0.0,
                                 ALU.mult, ALU.add)
            if et >= 2:
                e.wait_ge(em.s(f"p2_hst{et % 2}"), 16 * (et // 2))
            e.tensor_copy(hb_b[k][:], hf_b[:]).then_inc(em.s("p2_xb"), 1)

        @blk.vector
        def _(e):
            for et in range(ND):
                e.wait_ge(em.s("p2_tp"), et + 1)
                if et >= 2:
                    e.wait_ge(em.s("p2_mm"), 8 * (et - 1))
                for w in range(2):
                    inst = e.tensor_copy(
                        sWT[et % 2][:, w * D:(w + 1) * D], pswt[w][:])
                inst.then_inc(em.s("p2_ev"), 1)
                if et >= 1:
                    chain(e, et - 1)
            chain(e, ND - 1)

        em.val["p2_ld0"] = 32 * (ND // 2)
        em.val["p2_ld1"] = 32 * (ND // 2)
        em.val["p2_tp"] = ND
        em.val["p2_ev"] = ND
        em.val["p2_mm"] = 8 * ND
        em.val["p2_th"] = 8 * ND
        em.val["p2_al"] = ND
        em.val["p2_a2"] = ND
        em.val["p2_sq"] = ND
        em.val["p2_xb"] = ND
        em.val["p2_hst0"] = 16 * (ND // 2)
        em.val["p2_hst1"] = 16 * (ND // 2)


def phase_ffn_prep(nc, em, gate_w, up_w, down_w, cache_g, cache_u, cache_d,
                   gsc, gbias, usc, ubias, dsc, dbias, ident_b):
    em.roll()
    with ExitStack() as sp, nc.Block() as blk:
        for name in (["fp_dq", "fp_tp", "fp_ev"]
                     + [f"fp_ld{k}" for k in range(4)]
                     + [f"fp_st{k}" for k in range(4)]):
            em.new_sem(name)
        sb = lambda n_, s_, d_: sp.enter_context(nc.sbuf_tensor(n_, s_, d_))
        pt = lambda n_, s_, d_: sp.enter_context(nc.psum_tensor(n_, s_, d_))
        raw = [sb(f"fpraw{k}", [P, D], I32) for k in range(4)]
        dq = [sb(f"fpdq{k}", [P, D], BF16) for k in range(4)]
        img = [sb(f"fpimg{k}", [P, D], BF16) for k in range(4)]
        pswt = [pt(f"fpps{k}", [P, D], BF16) for k in range(4)]

        jobs = []
        for ic in range(NIC):
            jobs.append(("g", gate_w[ic * P:(ic + 1) * P, :], gsc, gbias,
                         cache_g, ic))
            jobs.append(("u", up_w[ic * P:(ic + 1) * P, :], usc, ubias,
                         cache_u, ic))
        for hh in range(ND):
            for q in range(4):
                jobs.append(("d", down_w[hh * P:(hh + 1) * P,
                                         q * D:(q + 1) * D], dsc, dbias,
                             cache_d[hh].rearrange(
                                 "p (q f) -> q p f", q=4)[q], hh))
        NJ = len(jobs)

        @blk.sync
        def _(e):
            for j, (kind, src, s_t, b_t, dst, idx) in enumerate(jobs):
                if j >= 4:
                    e.wait_ge(em.s("fp_dq"), j - 3)
                e.dma_start(raw[j % 4][:], src).then_inc(em.s(f"fp_ld{j % 4}"), 16)

        @blk.scalar
        def _(e):
            for j, (kind, src, s_t, b_t, dst, idx) in enumerate(jobs):
                e.wait_ge(em.s(f"fp_ld{j % 4}"), 16 * (j // 4 + 1))
                if j >= 4:
                    e.wait_ge(em.s("fp_tp"), j - 3)
                e.activation(dq[j % 4][:], raw[j % 4][:], AF.Identity,
                             scale=s_t[:, idx:idx + 1],
                             bias=b_t[:, idx:idx + 1]).then_inc(em.s("fp_dq"), 1)

        @blk.tensor
        def _(e):
            for j in range(NJ):
                e.wait_ge(em.s("fp_dq"), j + 1)
                if j >= 4:
                    e.wait_ge(em.s("fp_ev"), j - 3)
                for dc in range(ND):
                    inst = e.transpose(pswt[j % 4][:, dc * P:(dc + 1) * P],
                                       dq[j % 4][:, dc * P:(dc + 1) * P],
                                       ident_b[:])
                inst.then_inc(em.s("fp_tp"), 1)

        @blk.vector
        def _(e):
            for j in range(NJ):
                e.wait_ge(em.s("fp_tp"), j + 1)
                if j >= 4:
                    e.wait_ge(em.s(f"fp_st{j % 4}"), 16 * (j // 4))
                e.tensor_copy(img[j % 4][:], pswt[j % 4][:]
                              ).then_inc(em.s("fp_ev"), 1)

        @blk.gpsimd
        def _(e):
            for j, (kind, src, s_t, b_t, dst, idx) in enumerate(jobs):
                e.wait_ge(em.s("fp_ev"), j + 1)
                e.dma_start(dst if kind == "d" else dst[idx], img[j % 4][:]
                            ).then_inc(em.s(f"fp_st{j % 4}"), 16)
            for k in range(4):
                e.wait_ge(em.s(f"fp_st{k}"), 16 * (NJ // 4 + (1 if NJ % 4 > k else 0)))

        for k in range(4):
            em.val[f"fp_ld{k}"] = 16 * (NJ // 4 + (1 if NJ % 4 > k else 0))
        em.val["fp_dq"] = NJ
        em.val["fp_tp"] = NJ
        em.val["fp_ev"] = NJ
        for k in range(4):
            em.val[f"fp_st{k}"] = 16 * (NJ // 4 + (1 if NJ % 4 > k else 0))


def phase_p3(nc, em, x_in, xn_img, h_img, x1_img, cache_g, cache_u, cache_d,
             ident_b):
    em.roll()
    with ExitStack() as sp, nc.Block() as blk:
        for name in ["p3_xld", "p3_wld0", "p3_wld1", "p3_mm", "p3_tg", "p3_hc",
                     "p3_dld0", "p3_dld1", "p3_dn", "p3_cb", "p3_st0", "p3_st1",
                     "p3_xtp", "p3_xev", "p3_hld", "p3_xw0", "p3_xw1"]:
            em.new_sem(name)
        sb = lambda n_, s_, d_: sp.enter_context(nc.sbuf_tensor(n_, s_, d_))
        pt = lambda n_, s_, d_: sp.enter_context(nc.psum_tensor(n_, s_, d_))
        xnw = sb("xnw", [P, ND * WIN], BF16)
        hact = sb("hact", [P, NIC * WIN], BF16)
        img_g = [sb(f"img_g{k}", [P, D], BF16) for k in range(2)]
        img_u = [sb(f"img_u{k}", [P, D], BF16) for k in range(2)]
        dimg = [sb(f"dimg{k}", [P, IFF], BF16) for k in range(2)]
        su_b = sb("su_b", [P, WIN], F32)
        tg_b = [sb(f"tg_b{k}", [P, WIN], F32) for k in range(2)]
        pg_b = sb("pg_b", [P, WIN], F32)
        sxtm = [sb(f"sxtm{k}", [P, D], BF16) for k in range(2)]
        xfm = sb("xfm", [P, ND * WIN], BF16)
        hwb = sb("hwb", [P, ND * WIN], BF16)
        t1b = sb("t1b", [P, WIN], F32)
        x1b = [sb(f"x1b{k}", [P, WIN], BF16) for k in range(2)]
        psG = [pt(f"psG{k}", [P, WIN], F32) for k in range(2)]
        psU = [pt(f"psU{k}", [P, WIN], F32) for k in range(2)]
        psD = [pt(f"psD{k}", [P, WIN], F32) for k in range(2)]
        psX = pt("psX", [P, D], BF16)
        xnw3 = xnw[:].rearrange("p (c t) -> p c t", c=ND)
        xfm3 = xfm[:].rearrange("p (c t) -> p c t", c=ND)
        hwb3 = hwb[:].rearrange("p (c t) -> p c t", c=ND)
        NSUB = WIN // P

        @blk.sync
        def _(e):
            for w in range(NW):
                if w >= 1:
                    e.wait_ge(em.s("p3_mm"), 2 * NIC * w)
                e.dma_start(
                    xnw3, xn_img.rearrange("p (c t) -> p c t", c=ND)
                    [:, :, w * WIN:(w + 1) * WIN]).then_inc(em.s("p3_xld"), 16)
                for ic in range(NIC):
                    n = w * NIC + ic
                    if n >= 2:
                        e.wait_ge(em.s("p3_hc"), n - 1)
                    e.dma_start(img_g[ic % 2][:], cache_g[ic]
                                ).then_inc(em.s(f"p3_wld{ic % 2}"), 16)
                    e.dma_start(img_u[ic % 2][:], cache_u[ic]
                                ).then_inc(em.s(f"p3_wld{ic % 2}"), 16)
                for hh in range(ND):
                    m = w * ND + hh
                    if m >= 2:
                        e.wait_ge(em.s("p3_dn"), m - 1)
                    e.dma_start(dimg[hh % 2][:], cache_d[hh]
                                ).then_inc(em.s(f"p3_dld{hh % 2}"), 16)

        @blk.gpsimd
        def _(e):
            for w in range(NW):
                for s in range(NSUB):
                    i = w * NSUB + s
                    if i >= 2:
                        e.wait_ge(em.s("p3_xtp"), i - 1)
                    e.dma_start(sxtm[i % 2][:],
                                x_in[(w * WIN + s * P):(w * WIN + (s + 1) * P), :]
                                ).then_inc(em.s(f"p3_xw{i % 2}"), 16)
                if w >= 1:
                    e.wait_ge(em.s("p3_cb"), w * ND)
                e.dma_start(hwb3,
                            h_img[:, :, w * WIN:(w + 1) * WIN]
                            .rearrange("c p t -> p c t")
                            ).then_inc(em.s("p3_hld"), 16)
                for hh in range(ND):
                    m = w * ND + hh
                    e.wait_ge(em.s("p3_cb"), m + 1)
                    e.dma_start(x1_img.rearrange("p (c t) -> p c t", c=ND)
                                [:, hh, w * WIN:(w + 1) * WIN],
                                x1b[hh % 2][:]).then_inc(em.s(f"p3_st{hh % 2}"), 16)
            e.wait_ge(em.s("p3_st0"), 16 * NW * ND // 2)
            e.wait_ge(em.s("p3_st1"), 16 * NW * ND // 2)

        @blk.tensor
        def _(e):
            for w in range(NW):
                for s in range(NSUB):
                    i = w * NSUB + s
                    e.wait_ge(em.s(f"p3_xw{i % 2}"), 16 * (i // 2 + 1))
                    if i >= 1:
                        e.wait_ge(em.s("p3_xev"), i)
                    for dc in range(ND):
                        inst = e.transpose(psX[:, dc * P:(dc + 1) * P],
                                           sxtm[i % 2][:, dc * P:(dc + 1) * P],
                                           ident_b[:])
                    inst.then_inc(em.s("p3_xtp"), 1)
                for ic in range(NIC):
                    n = w * NIC + ic
                    e.wait_ge(em.s(f"p3_wld{ic % 2}"), 32 * ((w * NIC + ic) // 2 + 1))
                    e.wait_ge(em.s("p3_xld"), 16 * (w + 1))
                    if n >= 2:
                        e.wait_ge(em.s("p3_hc"), n - 1)
                    for dc in range(ND):
                        inst = e.matmul(psG[ic % 2][:],
                                        img_g[ic % 2][:, dc * P:(dc + 1) * P],
                                        xnw3[:, dc, :],
                                        start=(dc == 0), stop=(dc == ND - 1))
                    inst.then_inc(em.s("p3_mm"), 1)
                    for dc in range(ND):
                        inst = e.matmul(psU[ic % 2][:],
                                        img_u[ic % 2][:, dc * P:(dc + 1) * P],
                                        xnw3[:, dc, :],
                                        start=(dc == 0), stop=(dc == ND - 1))
                    inst.then_inc(em.s("p3_mm"), 1)
                for hh in range(ND):
                    m = w * ND + hh
                    e.wait_ge(em.s(f"p3_dld{hh % 2}"), 16 * (m // 2 + 1))
                    e.wait_ge(em.s("p3_hc"), (w + 1) * NIC)
                    if m >= 2:
                        e.wait_ge(em.s("p3_cb"), m - 1)
                    for ic in range(NIC):
                        inst = e.matmul(psD[hh % 2][:],
                                        dimg[hh % 2][:, ic * P:(ic + 1) * P],
                                        hact[:, ic * WIN:(ic + 1) * WIN],
                                        start=(ic == 0), stop=(ic == NIC - 1))
                    inst.then_inc(em.s("p3_dn"), 1)

        @blk.scalar
        def _(e):
            for w in range(NW):
                for ic in range(NIC):
                    n = w * NIC + ic
                    e.wait_ge(em.s("p3_mm"), 2 * n + 1)
                    if n >= 2:
                        e.wait_ge(em.s("p3_hc"), n - 1)
                    e.activation(tg_b[ic % 2][:], psG[ic % 2][:], AF.Tanh,
                                 scale=0.5).then_inc(em.s("p3_tg"), 1)

        @blk.vector
        def _(e):
            for w in range(NW):
                for s in range(NSUB):
                    i = w * NSUB + s
                    e.wait_ge(em.s("p3_xtp"), i + 1)
                    if w >= 1 and s == 0:
                        e.wait_ge(em.s("p3_cb"), w * ND)
                    e.tensor_copy(xfm3[:, :, s * P:(s + 1) * P],
                                  psX[:].rearrange("p (c t) -> p c t", c=ND)
                                  ).then_inc(em.s("p3_xev"), 1)
                for ic in range(NIC):
                    n = w * NIC + ic
                    e.wait_ge(em.s("p3_mm"), 2 * n + 2)
                    e.tensor_copy(su_b[:], psU[ic % 2][:])
                    e.tensor_tensor(pg_b[:], psG[ic % 2][:], su_b[:], ALU.mult)
                    e.wait_ge(em.s("p3_tg"), n + 1)
                    if w >= 1:
                        e.wait_ge(em.s("p3_dn"), w * ND)
                    e.scalar_tensor_tensor(hact[:, ic * WIN:(ic + 1) * WIN],
                                           tg_b[ic % 2][:], 1.0, pg_b[:],
                                           ALU.add, ALU.mult
                                           ).then_inc(em.s("p3_hc"), 1)
                for hh in range(ND):
                    m = w * ND + hh
                    e.wait_ge(em.s("p3_dn"), m + 1)
                    if hh == 0:
                        e.wait_ge(em.s("p3_xev"), (w + 1) * NSUB)
                        e.wait_ge(em.s("p3_hld"), 16 * (w + 1))
                    if m >= 2:
                        e.wait_ge(em.s(f"p3_st{hh % 2}"), 16 * (m // 2))
                    e.tensor_tensor(t1b[:], psD[hh % 2][:], xfm3[:, hh, :],
                                    ALU.add)
                    e.tensor_tensor(x1b[hh % 2][:], t1b[:], hwb3[:, hh, :],
                                    ALU.add).then_inc(em.s("p3_cb"), 1)

        em.val["p3_xld"] = 16 * NW
        em.val["p3_wld0"] = 32 * NW * NIC // 2
        em.val["p3_wld1"] = 32 * NW * NIC // 2
        em.val["p3_dld0"] = 16 * NW * ND // 2
        em.val["p3_dld1"] = 16 * NW * ND // 2
        em.val["p3_mm"] = 2 * NW * NIC
        em.val["p3_tg"] = NW * NIC
        em.val["p3_hc"] = NW * NIC
        em.val["p3_dn"] = NW * ND
        em.val["p3_cb"] = NW * ND
        em.val["p3_st0"] = 16 * NW * ND // 2
        em.val["p3_st1"] = 16 * NW * ND // 2
        em.val["p3_xw0"] = 16 * NW * NSUB // 2
        em.val["p3_xw1"] = 16 * NW * NSUB // 2
        em.val["p3_xtp"] = NW * NSUB
        em.val["p3_xev"] = NW * NSUB
        em.val["p3_hld"] = 16 * NW




def build_l1(dbg=False):
    nc = bass.Bass("TRN2", target_bir_lowering=False)

    x1_img = nc.dram_tensor("x1_img", [P, ND * T], BF16, kind="ExternalInput").ap()
    wq = nc.dram_tensor("L1_attn_Wq", [D, 2 * DH * NH], F32, kind="ExternalInput").ap()
    wk = nc.dram_tensor("L1_attn_Wk", [D, 2 * DH * NH], F32, kind="ExternalInput").ap()
    wv = nc.dram_tensor("L1_attn_Wv", [D, DH * NH], F32, kind="ExternalInput").ap()
    wo = nc.dram_tensor("L1_attn_Wo", [DH * NH, D], F32, kind="ExternalInput").ap()
    nsc_in = nc.dram_tensor("L1_attn_norm_scale", [D], F32, kind="ExternalInput").ap()
    lam_in = nc.dram_tensor("L1_attn_lam", [NH], F32, kind="ExternalInput").ap()
    factors = nc.dram_tensor("L1_ffn_factors", [8, D, D], F32, kind="ExternalInput").ap()
    bias_in = nc.dram_tensor("L1_ffn_bias", [D], F32, kind="ExternalInput").ap()

    out = nc.dram_tensor("out", [T, D], F32, kind="ExternalOutput").ap()
    dk = "ExternalOutput" if dbg else "Internal"
    xn2_img = nc.dram_tensor("xn2_img", [P, ND * T], BF16, kind=dk).ap()
    oat_img = nc.dram_tensor("oat_img", [ND, P, T], BF16, kind=dk).ap()
    f_img = nc.dram_tensor("f_img", [ND, P, D], BF16).ap()

    with ExitStack() as st:
        em = Emit(nc, st)
        ident_f, ident_b, ones1_f, ones_bf, cst = make_identities(em, st, nc)

        with ExitStack() as st0:
            with nc.Block() as blk:
                ser = Ser(em, "q0")
                pst = st0.enter_context(nc.psum_tensor("q0ps", [P, P], F32))
                nsc = vec_table(ser, st, nc, "nsc", nsc_in, ND, ident_f, pst)
                bs_t = vec_table(ser, st, nc, "bs_t", bias_in, ND, ident_f, pst)
                lam_row = st.enter_context(nc.sbuf_tensor("lam_row", [1, NH], F32))
                lam_b = st.enter_context(nc.sbuf_tensor("lam_b", [P, NH], F32))
                om = st.enter_context(nc.sbuf_tensor("om", [1, 1], F32))
                t_ml = st0.enter_context(nc.sbuf_tensor("t_ml", [1, 1], F32))
                t_ms = st0.enter_context(nc.sbuf_tensor("t_ms", [1, 1], F32))
                ser.add("sync", lambda e: e.dma_start(lam_row[:], lam_in.rearrange("(a b) -> a b", a=1)), dma=True)
                ser.add("tensor", lambda e: e.matmul(pst[:, 0:NH], ones1_f[0:1, :],
                                                     lam_row[:], start=True, stop=True))
                ser.add("vector", lambda e: e.tensor_copy(lam_b[:], pst[:, 0:NH]))
                ser.add("vector", lambda e: e.reduce_sum(t_ml[:], lam_row[:], axis=AX))
                ser.add("vector", lambda e: e.tensor_scalar(t_ms[:], t_ml[:], 1.0 / NH, None, ALU.mult))
                ser.add("vector", lambda e: e.tensor_scalar(om[:], t_ms[:], -1.0, 1.0, ALU.mult, ALU.add))
                ser.emit(blk, extra_wait=(em.s("s_id"), 2))

        em.roll()
        with ExitStack() as sf, nc.Block() as blk:
            for k in range(8):
                em.new_sem(f"q0f_ld{k}")
            for name in ["q0f_sum", "q0f_st"]:
                em.new_sem(name)
            fb = [sf.enter_context(nc.sbuf_tensor(f"fb{k}", [P, D], F32))
                  for k in range(8)]
            fta = sf.enter_context(nc.sbuf_tensor("fta", [P, D], F32))
            ftb = sf.enter_context(nc.sbuf_tensor("ftb", [P, D], F32))
            sFb = sf.enter_context(nc.sbuf_tensor("sFb", [P, D], BF16))

            @blk.sync
            def _(e):
                for dc in range(ND):
                    for r in range(8):
                        if dc >= 1:
                            e.wait_ge(em.s("q0f_sum"), dc)
                        e.dma_start(fb[r][:], factors[r, dc * P:(dc + 1) * P, :]
                                    ).then_inc(em.s(f"q0f_ld{r}"), 16)

            @blk.vector
            def _(e):
                for dc in range(ND):
                    for r in range(8):
                        e.wait_ge(em.s(f"q0f_ld{r}"), 16 * (dc + 1))
                    e.tensor_tensor(fta[:], fb[0][:], fb[1][:], ALU.add)
                    e.tensor_tensor(ftb[:], fta[:], fb[2][:], ALU.add)
                    e.tensor_tensor(fta[:], ftb[:], fb[3][:], ALU.add)
                    e.tensor_tensor(ftb[:], fta[:], fb[4][:], ALU.add)
                    e.tensor_tensor(fta[:], ftb[:], fb[5][:], ALU.add)
                    e.tensor_tensor(ftb[:], fta[:], fb[6][:], ALU.add)
                    e.tensor_tensor(fta[:], ftb[:], fb[7][:], ALU.add)
                    if dc >= 1:
                        e.wait_ge(em.s("q0f_st"), 16 * dc)
                    e.tensor_scalar(sFb[:], fta[:], 8.0 ** -0.5, None, ALU.mult
                                    ).then_inc(em.s("q0f_sum"), 1)

            @blk.gpsimd
            def _(e):
                for dc in range(ND):
                    e.wait_ge(em.s("q0f_sum"), dc + 1)
                    e.dma_start(f_img[dc], sFb[:]).then_inc(em.s("q0f_st"), 16)
                e.wait_ge(em.s("q0f_st"), 16 * ND)

            em.val["q0f_sum"] = ND
            em.val["q0f_st"] = 16 * ND
            for r in range(8):
                em.val[f"q0f_ld{r}"] = 16 * ND

        xn2_big = st.enter_context(nc.sbuf_tensor("xn2_big", [P, ND * T], BF16))
        xn23 = xn2_big[:].rearrange("p (c t) -> p c t", c=ND)

        phase_q1(nc, em, x1_img, xn2_big, xn23, xn2_img, ones_bf, ones1_f, cst)
        phase_q2(nc, em, wq, wk, wv, xn23, oat_img, lam_b, ident_b)
        phase_q3(nc, em, oat_img, ones_bf, ones1_f, cst, nsc, om)
        phase_q4(nc, em, wo, f_img, bs_t, x1_img, xn2_img, oat_img, out,
                 ident_f)

    return nc


def phase_q1(nc, em, x1_img, xn2_big, xn23, xn2_img, ones_bf, ones1_f, cst):
    em.roll()
    with ExitStack() as sp, nc.Block() as blk:
        for name in ["q1_ld0", "q1_ld1", "q1_sq", "q1_mm", "q1_row", "q1_bc",
                     "q1_ap", "q1_st", "q1_fs", "q1_ld2", "q1_ld3"]:
            em.new_sem(name)
        sb = lambda n_, s_, d_: sp.enter_context(nc.sbuf_tensor(n_, s_, d_))
        x1t = [sb(f"q1x{k}", [P, D], BF16) for k in range(2)]
        sqt = [sb(f"q1s{k}", [P, D], BF16) for k in range(2)]
        rows_c = sb("rows_c", [1, D], F32)
        rows_d = sb("rows_d", [1, D], F32)
        mrow = sb("mrow", [1, D], F32)[:]
        s2row = sb("s2row", [1, D], F32)[:]
        m2row = sb("m2row", [1, D], F32)[:]
        varrow = sb("varrow", [1, D], F32)[:]
        lnrow = sb("lnrow", [1, D], F32)[:]
        arow = rows_c[:]
        brow = rows_d[:]
        tmp = sb("q1tmp", [P, D], F32)
        x13 = x1_img.rearrange("p (c t) -> p c t", c=ND)

        with ExitStack() as sps:
            ps_s1 = sps.enter_context(nc.psum_tensor("ps_s1", [1, D], F32))
            ps_s2 = sps.enter_context(nc.psum_tensor("ps_s2", [1, D], F32))

            @blk.sync
            def _(e):
                for et in range(ND):
                    if et >= 2:
                        e.wait_ge(em.s("q1_sq"), et - 1)
                        e.wait_ge(em.s("q1_mm"), 8 * (et - 1))
                    e.dma_start(x1t[et % 2][:], x13[:, et, :]
                                ).then_inc(em.s(f"q1_ld{et % 2}"), 16)

            @blk.scalar
            def _(e):
                for et in range(ND):
                    e.wait_ge(em.s(f"q1_ld{et % 2}"), 16 * (et // 2 + 1))
                    if et >= 2:
                        e.wait_ge(em.s("q1_mm"), 8 * (et - 1))
                    e.activation(sqt[et % 2][:], x1t[et % 2][:], AF.Square
                                 ).then_inc(em.s("q1_sq"), 1)
                e.wait_ge(em.s("q1_row"), 3)
                e.activation(lnrow, varrow, AF.Ln, bias=cst[0:1, 0:1]
                             ).then_inc(em.s("q1_fs"), 1)
                e.wait_ge(em.s("q1_fs"), 1)
                e.activation(arow, lnrow, AF.Exp, scale=-0.5
                             ).then_inc(em.s("q1_fs"), 1)
                e.wait_ge(em.s("q1_fs"), 2)
                e.nop().then_inc(em.s("q1_row"), 1)

            @blk.tensor
            def _(e):
                for et in range(ND):
                    e.wait_ge(em.s("q1_sq"), et + 1)
                    for tc in range(NTC):
                        e.matmul(ps_s1[:, tc * TC:(tc + 1) * TC], ones_bf[:],
                                 x1t[et % 2][:, tc * TC:(tc + 1) * TC],
                                 start=(et == 0), stop=(et == ND - 1))
                        inst = e.matmul(ps_s2[:, tc * TC:(tc + 1) * TC],
                                        ones_bf[:],
                                        sqt[et % 2][:, tc * TC:(tc + 1) * TC],
                                        start=(et == 0), stop=(et == ND - 1))
                    inst.then_inc(em.s("q1_mm"), 8)

            @blk.vector
            def _(e):
                e.wait_ge(em.s("q1_mm"), 8 * ND)
                e.tensor_scalar(mrow, ps_s1[:], 1.0 / D, None, ALU.mult)
                e.tensor_scalar(s2row, ps_s2[:], 1.0 / D, None, ALU.mult
                                ).then_inc(em.s("q1_row"), 1)
                e.wait_ge(em.s("q1_row"), 1)
                e.tensor_tensor(m2row, mrow, mrow, ALU.mult
                                ).then_inc(em.s("q1_row"), 1)
                e.wait_ge(em.s("q1_row"), 2)
                e.tensor_tensor(varrow, s2row, m2row, ALU.subtract
                                ).then_inc(em.s("q1_row"), 1)
                e.wait_ge(em.s("q1_row"), 4)
                e.scalar_tensor_tensor(brow, mrow, -1.0, arow,
                                       ALU.mult, ALU.mult
                                       ).then_inc(em.s("q1_row"), 1)

            em.val["q1_sq"] = ND
            em.val["q1_mm"] = 8 * ND
            em.val["q1_row"] = 5
            em.val["q1_fs"] = 2
            em.val["q1_ld0"] = 16 * (ND // 2)
            em.val["q1_ld1"] = 16 * (ND // 2)

        with ExitStack() as sps2:
            ps_A = sps2.enter_context(nc.psum_tensor("ps_A", [P, D], F32))
            ps_B = sps2.enter_context(nc.psum_tensor("ps_B", [P, D], F32))

            @blk.tensor
            def _(e):
                e.wait_ge(em.s("q1_row"), 5)
                for tc in range(NTC):
                    e.matmul(ps_A[:, tc * TC:(tc + 1) * TC], ones1_f[0:1, :],
                             rows_c[0:1, tc * TC:(tc + 1) * TC], start=True,
                             stop=True)
                    inst = e.matmul(ps_B[:, tc * TC:(tc + 1) * TC],
                                    ones1_f[0:1, :],
                                    rows_d[0:1, tc * TC:(tc + 1) * TC],
                                    start=True, stop=True)
                inst.then_inc(em.s("q1_bc"), 1)

            @blk.sync
            def _(e):
                e.wait_ge(em.s("q1_mm"), 8 * ND)
                for et in range(ND):
                    if et >= 2:
                        e.wait_ge(em.s("q1_ap"), et - 1)
                    e.dma_start(x1t[et % 2][:], x13[:, et, :]
                                ).then_inc(em.s(f"q1_ld{2 + et % 2}"), 16)

            @blk.vector
            def _(e):
                e.wait_ge(em.s("q1_bc"), 1)
                for et in range(ND):
                    e.wait_ge(em.s(f"q1_ld{2 + et % 2}"), 16 * (et // 2 + 1))
                    e.tensor_tensor(tmp[:], x1t[et % 2][:], ps_A[:], ALU.mult)
                    e.tensor_tensor(xn23[:, et, :], tmp[:], ps_B[:], ALU.add
                                    ).then_inc(em.s("q1_ap"), 1)

            @blk.gpsimd
            def _(e):
                e.wait_ge(em.s("q1_ap"), ND)
                e.dma_start(xn2_img, xn2_big[:]).then_inc(em.s("q1_st"), 16)
                e.wait_ge(em.s("q1_st"), 16)

            em.val["q1_bc"] = 1
            em.val["q1_ap"] = ND
            em.val["q1_st"] = 16
            em.val["q1_ld2"] = 16 * (ND // 2)
            em.val["q1_ld3"] = 16 * (ND // 2)


def phase_q2(nc, em, wq, wk, wv, xn23, oat_img, lam_b, ident_b):
    em.roll()
    with ExitStack() as sp, nc.Block() as blk:
        for name in ["q2_wl0", "q2_wl1", "q2_qg", "q2_qe", "q2_vt", "q2_ve",
                     "q2_sc", "q2_e", "q2_d", "q2_ed", "q2_aw", "q2_tp",
                     "q2_awe", "q2_av", "q2_oev", "q2_fv", "q2_ost0",
                     "q2_ost1"]:
            em.new_sem(name)
        sb = lambda n_, s_, d_: sp.enter_context(nc.sbuf_tensor(n_, s_, d_))
        wqh = [sb(f"wqh{k}", [P, ND * 256], BF16) for k in range(2)]
        wkh = [sb(f"wkh{k}", [P, ND * 256], BF16) for k in range(2)]
        wvh = [sb(f"wvh{k}", [P, ND * P], BF16) for k in range(2)]
        q1f = [sb(f"q1f{k}", [P, T], BF16) for k in range(2)]
        q2f = [sb(f"q2f{k}", [P, T], BF16) for k in range(2)]
        k1f = [sb(f"k1f{k}", [P, T], BF16) for k in range(2)]
        k2f = [sb(f"k2f{k}", [P, T], BF16) for k in range(2)]
        vf = [sb(f"vf{k}", [P, T], BF16) for k in range(2)]
        vt = sb("vt", [P, T], BF16)
        e1b = [sb(f"e1b{k}", [P, T], BF16) for k in range(2)]
        e2b = sb("e2b", [P, T], BF16)
        w2b = sb("w2b", [P, T], BF16)
        db = [sb(f"db{k}", [P, T], BF16) for k in range(2)]
        edb = [sb(f"edb{k}", [P, T], BF16) for k in range(2)]
        awb = [sb(f"awb{k}", [P, T], BF16) for k in range(2)]
        awsb = [sb(f"awsb{k}", [P, T], BF16) for k in range(2)]
        oath = sb("oath", [P, T], BF16)
        sum1 = [sb(f"sum1{k}", [P, 1], F32) for k in range(2)]
        sum2 = [sb(f"sum2{k}", [P, 1], F32) for k in range(2)]
        sumd = [sb(f"sumd{k}", [P, 1], F32) for k in range(2)]
        inv1 = sb("inv1", [P, 1], F32)
        inv2v = sb("inv2v", [P, 1], F32)
        invd = sb("invd", [P, 1], F32)
        l2s = sb("l2s", [P, 1], F32)
        ps_sc_t = sp.enter_context(nc.psum_tensor("ps_sc", [P, T], F32))
        ps_tp_t = sp.enter_context(nc.psum_tensor("ps_tp", [P, T // 2], F32))
        ps_av_t = sp.enter_context(nc.psum_tensor("ps_av", [P, P], F32))
        ps_qkv_t = sp.enter_context(nc.psum_tensor("ps_qkv", [P, TC], F32))
        ps_sc = ps_sc_t[:]
        ps_av = ps_av_t[:]
        ps_qkv = ps_qkv_t[:]
        pstp_bf = ps_tp_t.bitcast(BF16)
        wq3 = wq.rearrange("(c p) e -> p c e", p=P)
        wk3 = wk.rearrange("(c p) e -> p c e", p=P)
        wv3 = wv.rearrange("(c p) e -> p c e", p=P)

        def qkv_lhsT(h, oi, dc):
            if oi == 0:
                return wqh[h % 2][:, dc * 256:dc * 256 + 128]
            if oi == 1:
                return wqh[h % 2][:, dc * 256 + 128:dc * 256 + 256]
            if oi == 2:
                return wkh[h % 2][:, dc * 256:dc * 256 + 128]
            if oi == 3:
                return wkh[h % 2][:, dc * 256 + 128:dc * 256 + 256]
            return wvh[h % 2][:, dc * 128:dc * 128 + 128]

        def qkv_dst(h, oi):
            return [q1f, q2f, k1f, k2f, vf][oi][h % 2]

        PREF = {qt: ([2 * qt, 2 * qt + 1] if qt < 4 else [qt + 4])
                for qt in range(NTT)}

        @blk.gpsimd
        def _(e):
            def load(h):
                if h >= 2:
                    e.wait_ge(em.s("q2_qg"), 20 * (h - 1))
                e.dma_start(wqh[h % 2][:].rearrange("p (c e) -> p c e", e=256),
                            wq3[:, :, h * 256:(h + 1) * 256]
                            ).then_inc(em.s(f"q2_wl{h % 2}"), 16)
                e.dma_start(wkh[h % 2][:].rearrange("p (c e) -> p c e", e=256),
                            wk3[:, :, h * 256:(h + 1) * 256]
                            ).then_inc(em.s(f"q2_wl{h % 2}"), 16)
                e.dma_start(wvh[h % 2][:].rearrange("p (c e) -> p c e", e=128),
                            wv3[:, :, h * 128:(h + 1) * 128]
                            ).then_inc(em.s(f"q2_wl{h % 2}"), 16)

            load(0)
            load(1)
            for h in range(NH):
                if h + 2 < NH:
                    load(h + 2)
                e.wait_ge(em.s("q2_oev"), NH * (h + 1))
                e.dma_start(oat_img[h], oath[:]).then_inc(em.s("q2_ost0"), 16)
            e.wait_ge(em.s("q2_ost0"), 16 * NH)

        def pe_group(e, h, g):
            oi, tc = divmod((g - 1) % 20, NTC)
            if g >= 2:
                e.wait_ge(em.s("q2_qe"), g - 1)
            for dc in range(ND):
                inst = e.matmul(ps_qkv, qkv_lhsT(h, oi, dc),
                                xn23[:, dc, tc * TC:(tc + 1) * TC],
                                start=(dc == 0), stop=(dc == ND - 1))
            inst.then_inc(em.s("q2_qg"), 1)

        def pe_tp_av(e, m):
            hh, qq = divmod(m, NTT)
            e.wait_ge(em.s("q2_aw"), m + 1)
            e.wait_ge(em.s("q2_awe"), m)
            if qq == 0:
                e.wait_ge(em.s("q2_ve"), hh + 1)
            for kt in range(NTT):
                inst = e.transpose(pstp_bf[:, kt * P:(kt + 1) * P],
                                   awb[m % 2][:, kt * P:(kt + 1) * P],
                                   ident_b[:])
            inst.then_inc(em.s("q2_tp"), 1)
            e.wait_ge(em.s("q2_awe"), m + 1)
            e.wait_ge(em.s("q2_oev"), m)
            for kt in range(NTT):
                inst = e.matmul(ps_av, vt[:, kt * P:(kt + 1) * P],
                                awsb[m % 2][:, kt * P:(kt + 1) * P],
                                start=(kt == 0), stop=(kt == NTT - 1))
            inst.then_inc(em.s("q2_av"), 1)

        @blk.tensor
        def _(e):
            e.wait_ge(em.s("q2_wl0"), 48)
            for g in range(1, 21):
                pe_group(e, 0, g)
            e.wait_ge(em.s("q2_qe"), 20)
            for kt in range(NTT):
                inst = e.transpose(pstp_bf[:, kt * P:(kt + 1) * P],
                                   vf[0][:, kt * P:(kt + 1) * P], ident_b[:])
            inst.then_inc(em.s("q2_vt"), 1)
            for n in range(NH * NTT):
                h, qt = divmod(n, NTT)
                pre = PREF[qt] if h < NH - 1 else []
                for j in pre:
                    if j == 0:
                        e.wait_ge(em.s(f"q2_wl{(h + 1) % 2}"),
                                  48 * ((h + 1) // 2 + 1))
                    pe_group(e, h + 1, 20 * (h + 1) + j + 1)
                if n >= 1:
                    e.wait_ge(em.s("q2_e"), 2 * n)
                for tc in range(NTC):
                    inst = e.matmul(ps_sc[:, tc * TC:(tc + 1) * TC],
                                    q1f[h % 2][:, qt * P:(qt + 1) * P],
                                    k1f[h % 2][:, tc * TC:(tc + 1) * TC],
                                    start=True, stop=True)
                inst.then_inc(em.s("q2_sc"), 1)
                e.wait_ge(em.s("q2_e"), 2 * n + 1)
                for tc in range(NTC):
                    inst = e.matmul(ps_sc[:, tc * TC:(tc + 1) * TC],
                                    q2f[h % 2][:, qt * P:(qt + 1) * P],
                                    k2f[h % 2][:, tc * TC:(tc + 1) * TC],
                                    start=True, stop=True)
                inst.then_inc(em.s("q2_sc"), 1)
                if n >= 1:
                    pe_tp_av(e, n - 1)
                    hh, qq = divmod(n - 1, NTT)
                    if qq == NTT - 1 and hh < NH - 1:
                        e.wait_ge(em.s("q2_qe"), 20 * (hh + 2))
                        for kt in range(NTT):
                            inst = e.transpose(
                                pstp_bf[:, kt * P:(kt + 1) * P],
                                vf[(hh + 1) % 2][:, kt * P:(kt + 1) * P],
                                ident_b[:])
                        inst.then_inc(em.s("q2_vt"), 1)
            pe_tp_av(e, NH * NTT - 1)

        def sc_ed(e, m):
            e.wait_ge(em.s("q2_d"), m + 1)
            if m >= 2:
                e.wait_ge(em.s("q2_aw"), m - 1)
            e.activation(edb[m % 2][:], db[m % 2][:], AF.Exp,
                         accum_out=sumd[m % 2][:]).then_inc(em.s("q2_ed"), 1)

        @blk.scalar
        def _(e):
            for h in range(NH):
                for qt in range(NTT):
                    n = NH * h + qt
                    e.wait_ge(em.s("q2_sc"), 2 * n + 1)
                    if n >= 2:
                        e.wait_ge(em.s("q2_d"), n - 1)
                    e.activation(e1b[n % 2][:], ps_sc, AF.Exp, scale=SCALE_QK,
                                 accum_out=sum1[n % 2][:]).then_inc(em.s("q2_e"), 1)
                    e.wait_ge(em.s("q2_sc"), 2 * n + 2)
                    if n >= 1:
                        e.wait_ge(em.s("q2_d"), n)
                    e.activation(e2b[:], ps_sc, AF.Exp, scale=SCALE_QK,
                                 accum_out=sum2[n % 2][:]).then_inc(em.s("q2_e"), 1)
                    if n >= 1:
                        sc_ed(e, n - 1)
            sc_ed(e, NH * NTT - 1)

        @blk.vector
        def _(e):
            fv = [0]

            def fence(inst):
                fv[0] += 1
                inst.then_inc(em.s("q2_fv"), 1)
                e.wait_ge(em.s("q2_fv"), fv[0])

            def qkv_evac(h, g):
                oi, tc = divmod((g - 1) % 20, NTC)
                e.wait_ge(em.s("q2_qg"), g)
                e.tensor_copy(qkv_dst(h, oi)[:, tc * TC:(tc + 1) * TC],
                              ps_qkv).then_inc(em.s("q2_qe"), 1)

            def pref_evacs(n):
                h, qt = divmod(n, NTT)
                if h >= NH - 1:
                    return
                if qt == 0:
                    e.wait_ge(em.s("q2_sc"), 32 * h)
                for j in PREF[qt]:
                    qkv_evac(h + 1, 20 * (h + 1) + j + 1)

            for g in range(1, 21):
                qkv_evac(0, g)
            e.wait_ge(em.s("q2_vt"), 1)
            e.tensor_copy(vt[:], pstp_bf[:, 0:T]).then_inc(em.s("q2_ve"), 1)
            pref_evacs(0)
            for h in range(NH):
                for qt in range(NTT):
                    n = NH * h + qt
                    e.wait_ge(em.s("q2_e"), 2 * n + 2)
                    fence(e.reciprocal(inv1[:], sum1[n % 2][:]))
                    fence(e.reciprocal(inv2v[:], sum2[n % 2][:]))
                    fence(e.tensor_scalar(l2s[:], inv2v[:], lam_b[:, h:h + 1],
                                          None, ALU.mult))
                    e.tensor_scalar(w2b[:], e2b[:], l2s[:, 0:1], None,
                                    ALU.mult)
                    if n >= 2:
                        e.wait_ge(em.s("q2_ed"), n - 1)
                    e.scalar_tensor_tensor(db[n % 2][:], e1b[n % 2][:],
                                           inv1[:, 0:1], w2b[:], ALU.mult,
                                           ALU.subtract).then_inc(em.s("q2_d"), 1)
                    pref_evacs(n + 1)
                    e.wait_ge(em.s("q2_ed"), n + 1)
                    fence(e.reciprocal(invd[:], sumd[n % 2][:]))
                    if n >= 2:
                        e.wait_ge(em.s("q2_tp"), n - 1)
                    e.tensor_scalar(awb[n % 2][:], edb[n % 2][:], invd[:, 0:1],
                                    None, ALU.mult).then_inc(em.s("q2_aw"), 1)
                    e.wait_ge(em.s("q2_tp"), n + 1)
                    if n >= 2:
                        e.wait_ge(em.s("q2_av"), n - 1)
                    e.tensor_copy(awsb[n % 2][:], pstp_bf[:, 0:T]
                                  ).then_inc(em.s("q2_awe"), 1)
                    e.wait_ge(em.s("q2_av"), n + 1)
                    if qt == 0 and h >= 1:
                        e.wait_ge(em.s("q2_ost0"), 16 * h)
                    e.tensor_copy(oath[:, qt * P:(qt + 1) * P],
                                  ps_av).then_inc(em.s("q2_oev"), 1)
                    if qt == NTT - 1 and h < NH - 1:
                        e.wait_ge(em.s("q2_vt"), h + 2)
                        e.tensor_copy(vt[:], pstp_bf[:, 0:T]
                                      ).then_inc(em.s("q2_ve"), 1)


        em.val["q2_wl0"] = 48 * (NH // 2)
        em.val["q2_wl1"] = 48 * (NH // 2)
        em.val["q2_qg"] = 20 * NH
        em.val["q2_qe"] = 20 * NH
        em.val["q2_vt"] = NH
        em.val["q2_ve"] = NH
        em.val["q2_sc"] = 2 * NH * NTT
        em.val["q2_e"] = 2 * NH * NTT
        em.val["q2_d"] = NH * NTT
        em.val["q2_ed"] = NH * NTT
        em.val["q2_aw"] = NH * NTT
        em.val["q2_tp"] = NH * NTT
        em.val["q2_awe"] = NH * NTT
        em.val["q2_av"] = NH * NTT
        em.val["q2_oev"] = NH * NTT
        em.val["q2_fv"] = 4 * NH * NTT
        em.val["q2_ost0"] = 16 * NH
        em.val["q2_ost1"] = 0


def phase_q3(nc, em, oat_img, ones_bf, ones1_f, cst, nsc, om):
    em.roll()
    with ExitStack() as sp, nc.Block() as blk:
        for name in ["q3_ld0", "q3_ld1", "q3_ld2", "q3_ld3", "q3_sq", "q3_mm",
                     "q3_row", "q3_fs", "q3_bc", "q3_t", "q3_ap", "q3_st0",
                     "q3_st1"]:
            em.new_sem(name)
        sb = lambda n_, s_, d_: sp.enter_context(nc.sbuf_tensor(n_, s_, d_))
        oatt = [sb(f"q3o{k}", [P, T], BF16) for k in range(2)]
        sqa = [sb(f"q3sq{k}", [P, T], BF16) for k in range(2)]
        nrm = [sb(f"q3n{k}", [P, T], BF16) for k in range(2)]
        msrow = sb("msrow", [1, D], F32)[:]
        lnr = sb("lnr", [1, D], F32)[:]
        rstdr = sb("rstdr", [1, D], F32)[:]
        rows2 = sb("q3rows2", [1, D], F32)
        rstdr2 = rows2[:]
        tmp = [sb(f"q3tmp{k}", [P, T], F32) for k in range(2)]

        with ExitStack() as sps:
            ps_ms = sps.enter_context(nc.psum_tensor("ps_ms", [1, D], F32))

            @blk.sync
            def _(e):
                for ft in range(ND):
                    if ft >= 2:
                        e.wait_ge(em.s("q3_sq"), ft - 1)
                    e.dma_start(oatt[ft % 2][:], oat_img[ft]
                                ).then_inc(em.s(f"q3_ld{ft % 2}"), 16)

            @blk.scalar
            def _(e):
                for ft in range(ND):
                    e.wait_ge(em.s(f"q3_ld{ft % 2}"), 16 * (ft // 2 + 1))
                    if ft >= 2:
                        e.wait_ge(em.s("q3_mm"), ft - 1)
                    e.activation(sqa[ft % 2][:], oatt[ft % 2][:], AF.Square
                                 ).then_inc(em.s("q3_sq"), 1)
                e.wait_ge(em.s("q3_row"), 1)
                e.activation(lnr, msrow, AF.Ln, bias=cst[0:1, 2:3]
                             ).then_inc(em.s("q3_fs"), 1)
                e.wait_ge(em.s("q3_fs"), 1)
                e.activation(rstdr, lnr, AF.Exp, scale=-0.5
                             ).then_inc(em.s("q3_fs"), 1)
                e.wait_ge(em.s("q3_fs"), 2)
                e.nop().then_inc(em.s("q3_row"), 1)

            @blk.tensor
            def _(e):
                for ft in range(ND):
                    e.wait_ge(em.s("q3_sq"), ft + 1)
                    for tc in range(NTC):
                        inst = e.matmul(ps_ms[:, tc * TC:(tc + 1) * TC],
                                        ones_bf[:],
                                        sqa[ft % 2][:, tc * TC:(tc + 1) * TC],
                                        start=(ft == 0), stop=(ft == ND - 1))
                    inst.then_inc(em.s("q3_mm"), 1)

            @blk.vector
            def _(e):
                e.wait_ge(em.s("q3_mm"), ND)
                e.tensor_scalar(msrow, ps_ms[:], 1.0 / D, None, ALU.mult
                                ).then_inc(em.s("q3_row"), 1)
                e.wait_ge(em.s("q3_row"), 2)
                e.tensor_scalar(rstdr2, rstdr, om[0:1, 0:1], None,
                                ALU.mult).then_inc(em.s("q3_row"), 1)

            em.val["q3_sq"] = ND
            em.val["q3_mm"] = ND
            em.val["q3_row"] = 3
            em.val["q3_fs"] = 2
            em.val["q3_ld0"] = 16 * (ND // 2)
            em.val["q3_ld1"] = 16 * (ND // 2)

        with ExitStack() as sps2:
            ps_R = sps2.enter_context(nc.psum_tensor("ps_R", [P, D], F32))

            @blk.tensor
            def _(e):
                e.wait_ge(em.s("q3_row"), 3)
                for tc in range(NTC):
                    inst = e.matmul(ps_R[:, tc * TC:(tc + 1) * TC],
                                    ones1_f[0:1, :],
                                    rows2[0:1, tc * TC:(tc + 1) * TC],
                                    start=True, stop=True)
                inst.then_inc(em.s("q3_bc"), 1)

            @blk.sync
            def _(e):
                e.wait_ge(em.s("q3_mm"), ND)
                for ft in range(ND):
                    if ft >= 2:
                        e.wait_ge(em.s("q3_t"), ft - 1)
                    e.dma_start(oatt[ft % 2][:], oat_img[ft]
                                ).then_inc(em.s(f"q3_ld{2 + ft % 2}"), 16)

            @blk.vector
            def _(e):
                e.wait_ge(em.s("q3_bc"), 1)
                for ft in range(ND):
                    e.wait_ge(em.s(f"q3_ld{2 + ft % 2}"), 16 * (ft // 2 + 1))
                    if ft >= 2:
                        e.wait_ge(em.s("q3_ap"), ft - 1)
                    e.tensor_tensor(tmp[ft % 2][:], oatt[ft % 2][:], ps_R[:],
                                    ALU.mult).then_inc(em.s("q3_t"), 1)

            @blk.scalar
            def _(e):
                for ft in range(ND):
                    e.wait_ge(em.s("q3_t"), ft + 1)
                    if ft >= 2:
                        e.wait_ge(em.s(f"q3_st{ft % 2}"), 16 * (ft // 2))
                    e.activation(nrm[ft % 2][:], tmp[ft % 2][:], AF.Identity,
                                 scale=nsc[:, ft:ft + 1]).then_inc(em.s("q3_ap"), 1)

            @blk.gpsimd
            def _(e):
                for ft in range(ND):
                    e.wait_ge(em.s("q3_ap"), ft + 1)
                    e.dma_start(oat_img[ft], nrm[ft % 2][:]
                                ).then_inc(em.s(f"q3_st{ft % 2}"), 16)
                e.wait_ge(em.s("q3_st0"), 16 * (ND // 2))
                e.wait_ge(em.s("q3_st1"), 16 * (ND // 2))

            em.val["q3_bc"] = 1
            em.val["q3_t"] = ND
            em.val["q3_ap"] = ND
            em.val["q3_ld2"] = 16 * (ND // 2)
            em.val["q3_ld3"] = 16 * (ND // 2)
            em.val["q3_st0"] = 16 * (ND // 2)
            em.val["q3_st1"] = 16 * (ND // 2)


def phase_q4(nc, em, wo, f_img, bs_t, x1_img, xn2_img, oat_img, out, ident_f):
    em.roll()
    with ExitStack() as sp, nc.Block() as blk:
        for name in ["q4_xn", "q4_wl0", "q4_wl1", "q4_fl0", "q4_fl1",
                     "q4_x10", "q4_x11", "q4_mm", "q4_mx", "q4_cb", "q4_tp",
                     "q4_ev", "q4_st"]:
            em.new_sem(name)
        sb = lambda n_, s_, d_: sp.enter_context(nc.sbuf_tensor(n_, s_, d_))
        xn2w = sb("xn2w", [P, ND * TC], BF16)
        nrmw = sb("nrmw", [P, ND * TC], BF16)
        woi = [sb(f"woi{k}", [P, D], BF16) for k in range(2)]
        fii = [sb(f"fii{k}", [P, D], BF16) for k in range(2)]
        sx1 = [sb(f"sx1{k}", [P, TC], BF16) for k in range(2)]
        mxb = [sb(f"mxb{k}", [P, TC], F32) for k in range(2)]
        t4 = sb("t4", [P, TC], F32)
        ofb = [sb(f"ofb{k}", [P, TC], F32) for k in range(2)]
        stg = [sb(f"stg{k}", [P, D], F32) for k in range(4)]
        ps_wo = [sp.enter_context(nc.psum_tensor(f"ps_wo{k}", [P, TC], F32))
                 for k in range(2)]
        ps_F = [sp.enter_context(nc.psum_tensor(f"ps_F{k}", [P, TC], F32))
                for k in range(2)]
        ps_tr = sp.enter_context(nc.psum_tensor("ps_tr", [P, TC], F32))
        oat3 = nrmw[:].rearrange("p (c t) -> p c t", c=ND)
        xn2w3 = xn2w[:].rearrange("p (c t) -> p c t", c=ND)
        wo3 = wo.rearrange("(c p) e -> p c e", p=P)
        NSUB = TC // P

        @blk.sync
        def _(e):
            for tc in range(NTC):
                if tc >= 1:
                    e.wait_ge(em.s("q4_mm"), 2 * ND * tc)
                e.dma_start(xn2w3,
                            xn2_img.rearrange("p (c t) -> p c t", c=ND)
                            [:, :, tc * TC:(tc + 1) * TC]
                            ).then_inc(em.s("q4_xn"), 16)
                e.dma_start(oat3,
                            oat_img[:, :, tc * TC:(tc + 1) * TC]
                            .rearrange("c p t -> p c t")
                            ).then_inc(em.s("q4_xn"), 16)
                for et in range(ND):
                    m = tc * ND + et
                    if m >= 2:
                        e.wait_ge(em.s("q4_mm"), 2 * (m - 1))
                    e.dma_start(fii[et % 2][:].rearrange("p (c e) -> p c e", e=P),
                                f_img[:, :, et * P:(et + 1) * P]
                                .rearrange("c p e -> p c e")
                                ).then_inc(em.s(f"q4_fl{et % 2}"), 16)
                    if m >= 2:
                        e.wait_ge(em.s("q4_cb"), m - 1)
                    e.dma_start(sx1[et % 2][:],
                                x1_img.rearrange("p (c t) -> p c t", c=ND)
                                [:, et, tc * TC:(tc + 1) * TC]
                                ).then_inc(em.s(f"q4_x1{et % 2}"), 16)

        @blk.gpsimd
        def _(e):
            for tc in range(NTC):
                for et in range(ND):
                    m = tc * ND + et
                    if m >= 2:
                        e.wait_ge(em.s("q4_mm"), 2 * (m - 1))
                    e.dma_start(woi[et % 2][:].rearrange("p (c e) -> p c e", e=P),
                                wo3[:, :, et * P:(et + 1) * P]
                                ).then_inc(em.s(f"q4_wl{et % 2}"), 16)
                e.wait_ge(em.s("q4_ev"), (tc + 1) * ND * NSUB)
                for su in range(NSUB):
                    e.dma_start(out[(tc * NSUB + su) * P:(tc * NSUB + su + 1) * P, :],
                                stg[su][:]).then_inc(em.s("q4_st"), 16)
            e.wait_ge(em.s("q4_st"), 16 * NTC * NSUB)

        @blk.tensor
        def _(e):
            for tc in range(NTC):
                e.wait_ge(em.s("q4_xn"), 32 * (tc + 1))
                for et in range(ND):
                    m = tc * ND + et
                    if m >= 1:
                        mm = m - 1
                        e.wait_ge(em.s("q4_cb"), mm + 1)
                        if mm >= 1:
                            e.wait_ge(em.s("q4_ev"), mm * NSUB)
                        for su in range(NSUB):
                            inst = e.transpose(ps_tr[:, su * P:(su + 1) * P],
                                               ofb[mm % 2][:, su * P:(su + 1) * P],
                                               ident_f[:])
                        inst.then_inc(em.s("q4_tp"), 1)
                    e.wait_ge(em.s(f"q4_wl{et % 2}"), 16 * (m // 2 + 1))
                    e.wait_ge(em.s(f"q4_fl{et % 2}"), 16 * (m // 2 + 1))
                    if m >= 2:
                        e.wait_ge(em.s("q4_cb"), m - 1)
                    for fc in range(ND):
                        inst = e.matmul(ps_wo[et % 2][:],
                                        woi[et % 2][:, fc * P:(fc + 1) * P],
                                        oat3[:, fc, :],
                                        start=(fc == 0), stop=(fc == ND - 1))
                    inst.then_inc(em.s("q4_mm"), 1)
                    for dc in range(ND):
                        inst = e.matmul(ps_F[et % 2][:],
                                        fii[et % 2][:, dc * P:(dc + 1) * P],
                                        xn2w3[:, dc, :],
                                        start=(dc == 0), stop=(dc == ND - 1))
                    inst.then_inc(em.s("q4_mm"), 1)
            mm = NTC * ND - 1
            e.wait_ge(em.s("q4_cb"), mm + 1)
            e.wait_ge(em.s("q4_ev"), mm * NSUB)
            for su in range(NSUB):
                inst = e.transpose(ps_tr[:, su * P:(su + 1) * P],
                                   ofb[mm % 2][:, su * P:(su + 1) * P],
                                   ident_f[:])
            inst.then_inc(em.s("q4_tp"), 1)

        @blk.scalar
        def _(e):
            for tc in range(NTC):
                for et in range(ND):
                    m = tc * ND + et
                    e.wait_ge(em.s("q4_mm"), 2 * m + 2)
                    if m >= 2:
                        e.wait_ge(em.s("q4_cb"), m - 1)
                    e.activation(mxb[et % 2][:], ps_F[et % 2][:], AF.Identity,
                                 bias=bs_t[:, et:et + 1]).then_inc(em.s("q4_mx"), 1)

        @blk.vector
        def _(e):
            for tc in range(NTC):
                for et in range(ND):
                    m = tc * ND + et
                    if m >= 1:
                        mm = m - 1
                        tcc, ett = divmod(mm, ND)
                        e.wait_ge(em.s("q4_tp"), mm + 1)
                        if tcc >= 1 and ett == 0:
                            e.wait_ge(em.s("q4_st"), 16 * NSUB * tcc)
                        for su in range(NSUB):
                            inst = e.tensor_copy(
                                stg[su][:, ett * P:(ett + 1) * P],
                                ps_tr[:, su * P:(su + 1) * P])
                        inst.then_inc(em.s("q4_ev"), NSUB)
                    e.wait_ge(em.s("q4_mx"), m + 1)
                    e.wait_ge(em.s(f"q4_x1{et % 2}"), 16 * (m // 2 + 1))
                    if m >= 2:
                        e.wait_ge(em.s("q4_tp"), m - 1)
                    e.tensor_tensor(t4[:], ps_wo[et % 2][:], mxb[et % 2][:],
                                    ALU.add)
                    e.tensor_tensor(ofb[m % 2][:], t4[:], sx1[et % 2][:],
                                    ALU.add).then_inc(em.s("q4_cb"), 1)
            mm = NTC * ND - 1
            tcc, ett = divmod(mm, ND)
            e.wait_ge(em.s("q4_tp"), mm + 1)
            if ett == 0:
                e.wait_ge(em.s("q4_st"), 16 * NSUB * tcc)
            for su in range(NSUB):
                inst = e.tensor_copy(stg[su][:, ett * P:(ett + 1) * P],
                                     ps_tr[:, su * P:(su + 1) * P])
            inst.then_inc(em.s("q4_ev"), NSUB)

        em.val["q4_xn"] = 32 * NTC
        em.val["q4_wl0"] = 16 * NTC * ND // 2
        em.val["q4_wl1"] = 16 * NTC * ND // 2
        em.val["q4_fl0"] = 16 * NTC * ND // 2
        em.val["q4_fl1"] = 16 * NTC * ND // 2
        em.val["q4_x10"] = 16 * NTC * ND // 2
        em.val["q4_x11"] = 16 * NTC * ND // 2
        em.val["q4_mm"] = 2 * NTC * ND
        em.val["q4_mx"] = NTC * ND
        em.val["q4_cb"] = NTC * ND
        em.val["q4_tp"] = NTC * ND
        em.val["q4_ev"] = NSUB * NTC * ND
        em.val["q4_st"] = 16 * NTC * NSUB




def build_fused():
    nc = bass.Bass("TRN2", target_bir_lowering=False)

    x_in = nc.dram_tensor("x", [T, D], F32, kind="ExternalInput").ap()
    w_in = nc.dram_tensor("L0_attn_W_in", [D, D], F32, kind="ExternalInput").ap()
    b_in = nc.dram_tensor("L0_attn_b_in", [D], F32, kind="ExternalInput").ap()
    w_gate = nc.dram_tensor("L0_attn_W_gate", [D, D], F32, kind="ExternalInput").ap()
    b_gate = nc.dram_tensor("L0_attn_b_gate", [D], F32, kind="ExternalInput").ap()
    flam = nc.dram_tensor("L0_attn_forget_lambda", [D], F32, kind="ExternalInput").ap()
    gate_w = nc.dram_tensor("L0_ffn_gate_w", [IFF, D], I32, kind="ExternalInput").ap()
    gate_s = nc.dram_tensor("L0_ffn_gate_scale", [IFF], F32, kind="ExternalInput").ap()
    up_w = nc.dram_tensor("L0_ffn_up_w", [IFF, D], I32, kind="ExternalInput").ap()
    up_s = nc.dram_tensor("L0_ffn_up_scale", [IFF], F32, kind="ExternalInput").ap()
    down_w = nc.dram_tensor("L0_ffn_down_w", [D, IFF], I32, kind="ExternalInput").ap()
    down_s = nc.dram_tensor("L0_ffn_down_scale", [D], F32, kind="ExternalInput").ap()
    wq = nc.dram_tensor("L1_attn_Wq", [D, 2 * DH * NH], F32, kind="ExternalInput").ap()
    wk = nc.dram_tensor("L1_attn_Wk", [D, 2 * DH * NH], F32, kind="ExternalInput").ap()
    wv = nc.dram_tensor("L1_attn_Wv", [D, DH * NH], F32, kind="ExternalInput").ap()
    wo = nc.dram_tensor("L1_attn_Wo", [DH * NH, D], F32, kind="ExternalInput").ap()
    nsc_in = nc.dram_tensor("L1_attn_norm_scale", [D], F32, kind="ExternalInput").ap()
    lam_in = nc.dram_tensor("L1_attn_lam", [NH], F32, kind="ExternalInput").ap()
    factors = nc.dram_tensor("L1_ffn_factors", [8, D, D], F32, kind="ExternalInput").ap()
    bias_in = nc.dram_tensor("L1_ffn_bias", [D], F32, kind="ExternalInput").ap()

    out = nc.dram_tensor("out", [T, D], F32, kind="ExternalOutput").ap()
    x1_img = nc.dram_tensor("x1_img", [P, ND * T], BF16).ap()
    xn_img = nc.dram_tensor("xn_img", [P, ND * T], BF16).ap()
    st_img = nc.dram_tensor("st_img", [P, 2 * NTT], F32).ap()
    h_img = nc.dram_tensor("h_img", [ND, P, T], BF16).ap()
    cache_g = nc.dram_tensor("cache_g", [NIC, P, D], BF16).ap()
    cache_u = nc.dram_tensor("cache_u", [NIC, P, D], BF16).ap()
    cache_d = nc.dram_tensor("cache_d", [ND, P, IFF], BF16).ap()
    xn2_img = nc.dram_tensor("xn2_img", [P, ND * T], BF16).ap()
    oat_img = nc.dram_tensor("oat_img", [ND, P, T], BF16).ap()
    f_img = nc.dram_tensor("f_img", [ND, P, D], BF16).ap()

    with ExitStack() as st:
        em = Emit(nc, st)
        ident_f, ident_b, ones1_f, ones_bf, cst = make_identities(em, st, nc)

        with ExitStack() as st0:
            with nc.Block() as blk:
                ser = Ser(em, "p0")
                pst = st0.enter_context(nc.psum_tensor("p0ps", [P, P], F32))
                bh_gate = vec_table(ser, st, nc, "bh_gate", b_gate, ND, ident_f, pst, 0.5)
                bh_in = vec_table(ser, st, nc, "bh_in", b_in, ND, ident_f, pst, 0.5)
                lam_t = vec_table(ser, st, nc, "lam_t", flam, ND, ident_f, pst)
                gsc = vec_table(ser, st, nc, "gsc", gate_s, NIC, ident_f, pst)
                usc = vec_table(ser, st, nc, "usc", up_s, NIC, ident_f, pst)
                dsc = vec_table(ser, st, nc, "dsc", down_s, ND, ident_f, pst, 0.5)
                gbias = st.enter_context(nc.sbuf_tensor("gbias", [P, NIC], F32))
                ubias = st.enter_context(nc.sbuf_tensor("ubias", [P, NIC], F32))
                dbias = st.enter_context(nc.sbuf_tensor("dbias", [P, ND], F32))
                chalf = st.enter_context(nc.sbuf_tensor("chalf", [P, ND], F32))
                tmp_a = st0.enter_context(nc.sbuf_tensor("tmp_a", [P, ND], F32))
                tmp_b = st0.enter_context(nc.sbuf_tensor("tmp_b", [P, ND], F32))
                tmp_c = st0.enter_context(nc.sbuf_tensor("tmp_c", [P, ND], F32))
                ser.add("vector", lambda e: e.tensor_scalar(
                    gbias[:], gsc[:], -128.0, None, ALU.mult))
                ser.add("vector", lambda e: e.tensor_scalar(
                    ubias[:], usc[:], -128.0, None, ALU.mult))
                ser.add("vector", lambda e: e.tensor_scalar(
                    dbias[:], dsc[:], -128.0, None, ALU.mult))
                ser.add("scalar", lambda e: e.activation(tmp_a[:], lam_t[:], AF.Abs))
                ser.add("scalar", lambda e: e.activation(tmp_b[:], tmp_a[:], AF.Exp, scale=-1.0))
                ser.add("scalar", lambda e: e.activation(tmp_c[:], tmp_b[:], AF.Ln, bias=1.0))
                ser.add("scalar", lambda e: e.activation(tmp_a[:], lam_t[:], AF.Relu))
                ser.add("vector", lambda e: e.tensor_tensor(
                    tmp_b[:], tmp_c[:], tmp_a[:], ALU.add))
                ser.add("vector", lambda e: e.tensor_scalar(
                    chalf[:], tmp_b[:], -4.0, None, ALU.mult))
                ser.emit(blk, extra_wait=(em.s("s_id"), 2))

        with ExitStack() as st_xn:
            xn_big = st_xn.enter_context(nc.sbuf_tensor("xn_big", [P, ND * T], BF16))
            xn3 = xn_big[:].rearrange("p (c t) -> p c t", c=ND)
            with nc.named_scope("p1"):
                phase_p1(nc, em, x_in, xn_big, xn3, xn_img, ident_b, cst, st_img)
            with nc.named_scope("p2"):
                phase_p2(nc, em, w_in, w_gate, h_img, xn3, ident_b,
                         bh_in, bh_gate, chalf, cst)
        with nc.named_scope("prep"):
            phase_ffn_prep(nc, em, gate_w, up_w, down_w, cache_g, cache_u, cache_d,
                           gsc, gbias, usc, ubias, dsc, dbias, ident_b)
        with nc.named_scope("p3"):
            phase_p3(nc, em, x_in, xn_img, h_img, x1_img,
                     cache_g, cache_u, cache_d, ident_b)

    nc.reset()

    with ExitStack() as st:
        em = Emit(nc, st)
        ident_f, ident_b, ones1_f, ones_bf, cst = make_identities(em, st, nc, pfx="l1_")

        with ExitStack() as st0:
            with nc.Block() as blk:
                ser = Ser(em, "q0")
                pst = st0.enter_context(nc.psum_tensor("q0ps", [P, P], F32))
                nsc = vec_table(ser, st, nc, "nsc", nsc_in, ND, ident_f, pst)
                bs_t = vec_table(ser, st, nc, "bs_t", bias_in, ND, ident_f, pst)
                lam_row = st.enter_context(nc.sbuf_tensor("lam_row", [1, NH], F32))
                lam_b = st.enter_context(nc.sbuf_tensor("lam_b", [P, NH], F32))
                om = st.enter_context(nc.sbuf_tensor("om", [1, 1], F32))
                t_ml = st0.enter_context(nc.sbuf_tensor("t_ml", [1, 1], F32))
                t_ms = st0.enter_context(nc.sbuf_tensor("t_ms", [1, 1], F32))
                ser.add("sync", lambda e: e.dma_start(lam_row[:], lam_in.rearrange("(a b) -> a b", a=1)), dma=True)
                ser.add("tensor", lambda e: e.matmul(pst[:, 0:NH], ones1_f[0:1, :],
                                                     lam_row[:], start=True, stop=True))
                ser.add("vector", lambda e: e.tensor_copy(lam_b[:], pst[:, 0:NH]))
                ser.add("vector", lambda e: e.reduce_sum(t_ml[:], lam_row[:], axis=AX))
                ser.add("vector", lambda e: e.tensor_scalar(t_ms[:], t_ml[:], 1.0 / NH, None, ALU.mult))
                ser.add("vector", lambda e: e.tensor_scalar(om[:], t_ms[:], -1.0, 1.0, ALU.mult, ALU.add))
                ser.emit(blk, extra_wait=(em.s("s_id"), 2))

        em.roll()
        with ExitStack() as sf, nc.Block() as blk:
            for k in range(16):
                em.new_sem(f"q0f_ld{k}")
            for name in ["q0f_sum", "q0f_st"]:
                em.new_sem(name)
            fb = [sf.enter_context(nc.sbuf_tensor(f"fb{k}", [P, D], F32))
                  for k in range(16)]
            fta = sf.enter_context(nc.sbuf_tensor("fta", [P, D], F32))
            ftb = sf.enter_context(nc.sbuf_tensor("ftb", [P, D], F32))
            sFb = [sf.enter_context(nc.sbuf_tensor(f"sFb{k}", [P, D], BF16))
                   for k in range(2)]

            @blk.sync
            def _(e):
                for dc in range(ND):
                    for r in range(8):
                        k = (dc % 2) * 8 + r
                        if dc >= 2:
                            e.wait_ge(em.s("q0f_sum"), dc - 1)
                        e.dma_start(fb[k][:], factors[r, dc * P:(dc + 1) * P, :]
                                    ).then_inc(em.s(f"q0f_ld{k}"), 16)

            @blk.vector
            def _(e):
                for dc in range(ND):
                    ks = (dc % 2) * 8
                    for r in range(8):
                        e.wait_ge(em.s(f"q0f_ld{ks + r}"), 16 * (dc // 2 + 1))
                    e.tensor_tensor(fta[:], fb[ks][:], fb[ks + 1][:], ALU.add)
                    e.tensor_tensor(ftb[:], fta[:], fb[ks + 2][:], ALU.add)
                    e.tensor_tensor(fta[:], ftb[:], fb[ks + 3][:], ALU.add)
                    e.tensor_tensor(ftb[:], fta[:], fb[ks + 4][:], ALU.add)
                    e.tensor_tensor(fta[:], ftb[:], fb[ks + 5][:], ALU.add)
                    e.tensor_tensor(ftb[:], fta[:], fb[ks + 6][:], ALU.add)
                    e.tensor_tensor(fta[:], ftb[:], fb[ks + 7][:], ALU.add)
                    if dc >= 2:
                        e.wait_ge(em.s("q0f_st"), 16 * (dc - 1))
                    e.tensor_scalar(sFb[dc % 2][:], fta[:], 8.0 ** -0.5, None,
                                    ALU.mult).then_inc(em.s("q0f_sum"), 1)

            @blk.gpsimd
            def _(e):
                for dc in range(ND):
                    e.wait_ge(em.s("q0f_sum"), dc + 1)
                    e.dma_start(f_img[dc], sFb[dc % 2][:]
                                ).then_inc(em.s("q0f_st"), 16)
                e.wait_ge(em.s("q0f_st"), 16 * ND)

            em.val["q0f_sum"] = ND
            em.val["q0f_st"] = 16 * ND
            for k in range(16):
                em.val[f"q0f_ld{k}"] = 16 * (ND // 2)

        xn2_big = st.enter_context(nc.sbuf_tensor("xn2_big", [P, ND * T], BF16))
        xn23 = xn2_big[:].rearrange("p (c t) -> p c t", c=ND)
        with nc.named_scope("q1"):
            phase_q1(nc, em, x1_img, xn2_big, xn23, xn2_img, ones_bf, ones1_f, cst)
        with nc.named_scope("q2"):
            phase_q2(nc, em, wq, wk, wv, xn23, oat_img, lam_b, ident_b)
        with nc.named_scope("q3"):
            phase_q3(nc, em, oat_img, ones_bf, ones1_f, cst, nsc, om)
        with nc.named_scope("q4"):
            phase_q4(nc, em, wo, f_img, bs_t, x1_img, xn2_img, oat_img, out,
                     ident_f)

    return nc

def _split_inputs(inputs):
    keys = ["L0_attn_W_in", "L0_attn_b_in", "L0_attn_W_gate", "L0_attn_b_gate",
            "L0_attn_forget_lambda", "L0_ffn_gate_w", "L0_ffn_gate_scale",
            "L0_ffn_up_w", "L0_ffn_up_scale", "L0_ffn_down_w",
            "L0_ffn_down_scale"]
    shared = {k: np.ascontiguousarray(inputs[k]) for k in keys}
    x = inputs["L0_attn_x"]
    return [dict(shared, x=np.ascontiguousarray(x[b])) for b in range(8)]


def kernel(**inputs):
    nc = build_fused()
    keys = ["L0_attn_W_in", "L0_attn_b_in", "L0_attn_W_gate", "L0_attn_b_gate",
            "L0_attn_forget_lambda", "L0_ffn_gate_w", "L0_ffn_gate_scale",
            "L0_ffn_up_w", "L0_ffn_up_scale", "L0_ffn_down_w",
            "L0_ffn_down_scale", "L1_attn_Wq", "L1_attn_Wk", "L1_attn_Wv",
            "L1_attn_Wo", "L1_attn_norm_scale", "L1_attn_lam",
            "L1_ffn_factors", "L1_ffn_bias"]
    shared = {k: np.ascontiguousarray(inputs[k]) for k in keys}
    x = inputs["L0_attn_x"]
    in_maps = [dict(shared, x=np.ascontiguousarray(x[b])) for b in range(8)]
    res = run_bass_kernel_spmd(nc, in_maps, core_ids=list(range(8)))
    return np.stack([np.asarray(res.results[b]["out"]) for b in range(8)],
                    axis=0)


# version 08
# speedup vs baseline: 1.1911x; 1.1346x over previous
import numpy as np
import ml_dtypes
from contextlib import ExitStack
import concourse.bass as bass
import concourse.mybir as mybir
from concourse.bass_utils import run_bass_kernel_spmd

F32 = mybir.dt.float32
BF16 = mybir.dt.bfloat16
I32 = mybir.dt.int32
AF = mybir.ActivationFunctionType
ALU = mybir.AluOpType
AX = mybir.AxisListType.X

P = 128
D = 2048
T = 2048
ND = D // P
NTT = T // P
TC = 512
NTC = T // TC
IFF = 8192
NIC = IFF // P
WIN = 512
NW = T // WIN
NH = 16
DH = 128
EPS_LN = 1e-5
EPS_RMS = 1e-6
SCALE_QK = 1.0 / (DH ** 0.5)


class Emit:

    def __init__(self, nc, stack):
        self.nc = nc
        self.stack = stack
        self.sem = {}
        self.val = {}

    def new_sem(self, name):
        h = self.stack.enter_context(self.nc.semaphore(name))
        self.sem[name] = h
        self.val[name] = 0
        return h

    def roll(self):
        pass

    def s(self, name):
        return self.sem[name]


class Ser:

    def __init__(self, em, tag):
        self.em = em
        self.sc = em.new_sem(tag + "_c")
        self.sd = em.new_sem(tag + "_d")
        self.cn = tag + "_c"
        self.dn = tag + "_d"
        self.ops = []

    def add(self, eng, fn, dma=False):
        self.ops.append((eng, dma, fn))

    def emit(self, blk, extra_wait=None):
        plan = {}
        ncc, ndd = 0, 0
        for eng, dma, fn in self.ops:
            plan.setdefault(eng, []).append((ncc, ndd, dma, fn))
            if dma:
                ndd += 1
            else:
                ncc += 1
        self.em.val[self.cn] += ncc
        self.em.val[self.dn] += 16 * ndd

        def mk(items):
            def body(eng):
                if extra_wait is not None:
                    eng.wait_ge(extra_wait[0], extra_wait[1])
                for (c_b, d_b, dma, fn) in items:
                    eng.wait_ge(self.sc, c_b)
                    eng.wait_ge(self.sd, 16 * d_b)
                    inst = fn(eng)
                    inst.then_inc(self.sd if dma else self.sc, 16 if dma else 1)
            return body

        for eng_name, items in plan.items():
            getattr(blk, eng_name)(mk(items))


def make_identities(em, st, nc, pfx=""):
    io_a = st.enter_context(nc.sbuf_tensor(pfx + "io_a", [P, P], I32))
    io_b = st.enter_context(nc.sbuf_tensor(pfx + "io_b", [P, P], I32))
    ident_f = st.enter_context(nc.sbuf_tensor(pfx + "ident_f", [P, P], F32))
    ident_b = st.enter_context(nc.sbuf_tensor(pfx + "ident_b", [P, P], BF16))
    ones1_f = st.enter_context(nc.sbuf_tensor(pfx + "ones1_f", [1, P], F32))
    ones_bf = st.enter_context(nc.sbuf_tensor(pfx + "ones_bf", [P, 1], BF16))
    cst = st.enter_context(nc.sbuf_tensor(pfx + "cst", [P, 4], F32))
    s_id = em.new_sem(pfx + "s_id")
    em.sem["s_id"] = s_id
    em.val["s_id"] = 0
    with nc.Block() as blk:
        @blk.gpsimd
        def _(g):
            g.iota(io_a[:], [[1, P]], channel_multiplier=0)
            g.iota(io_b[:], [[0, P]], channel_multiplier=1)
            g.memset(ones1_f[:], 1.0)
            g.memset(cst[:, 0:1], EPS_LN)
            g.memset(cst[:, 1:2], 0.25 * (1.0 + 1e-6))
            g.memset(cst[:, 2:3], EPS_RMS)
            g.memset(ones_bf[:], 1.0).then_inc(s_id, 1)

        @blk.vector
        def _(v):
            v.wait_ge(s_id, 1)
            v.tensor_tensor(ident_f[:], io_a[:], io_b[:], ALU.is_equal)
            v.tensor_copy(ident_b[:], ident_f[:]).then_inc(s_id, 1)
    em.val["s_id"] = 2
    return ident_f, ident_b, ones1_f, ones_bf, cst


def vec_table(ser, st, nc, name, src_ap, n, ident_f, pst, scale=None):
    raw = st.enter_context(nc.sbuf_tensor(name + "_r", [n, P], F32))
    tbl = st.enter_context(nc.sbuf_tensor(name, [P, n], F32))
    ser.add("sync", lambda e: e.dma_start(
        raw[:], src_ap.rearrange("(j p) -> j p", j=n)), dma=True)
    ser.add("tensor", lambda e: e.transpose(
        pst[:, 0:n], raw[:], ident_f[0:n, 0:n]))
    if scale is None:
        ser.add("vector", lambda e: e.tensor_copy(tbl[:], pst[:, 0:n]))
    else:
        ser.add("vector", lambda e: e.tensor_scalar(
            tbl[:], pst[:, 0:n], scale, None, ALU.mult))
    return tbl



def build_l0(dbg=False):
    nc = bass.Bass("TRN2", target_bir_lowering=False)

    x_in = nc.dram_tensor("x", [T, D], F32, kind="ExternalInput").ap()
    w_in = nc.dram_tensor("L0_attn_W_in", [D, D], F32, kind="ExternalInput").ap()
    b_in = nc.dram_tensor("L0_attn_b_in", [D], F32, kind="ExternalInput").ap()
    w_gate = nc.dram_tensor("L0_attn_W_gate", [D, D], F32, kind="ExternalInput").ap()
    b_gate = nc.dram_tensor("L0_attn_b_gate", [D], F32, kind="ExternalInput").ap()
    flam = nc.dram_tensor("L0_attn_forget_lambda", [D], F32, kind="ExternalInput").ap()
    gate_w = nc.dram_tensor("L0_ffn_gate_w", [IFF, D], I32, kind="ExternalInput").ap()
    gate_s = nc.dram_tensor("L0_ffn_gate_scale", [IFF], F32, kind="ExternalInput").ap()
    up_w = nc.dram_tensor("L0_ffn_up_w", [IFF, D], I32, kind="ExternalInput").ap()
    up_s = nc.dram_tensor("L0_ffn_up_scale", [IFF], F32, kind="ExternalInput").ap()
    down_w = nc.dram_tensor("L0_ffn_down_w", [D, IFF], I32, kind="ExternalInput").ap()
    down_s = nc.dram_tensor("L0_ffn_down_scale", [D], F32, kind="ExternalInput").ap()

    x1_img = nc.dram_tensor("x1_img", [P, ND * T], BF16, kind="ExternalOutput").ap()
    dk = "ExternalOutput" if dbg else "Internal"
    xn_img = nc.dram_tensor("xn_img", [P, ND * T], BF16, kind=dk).ap()
    st_img = nc.dram_tensor("st_img", [P, 2 * NTT], F32, kind=dk).ap()
    h_img = nc.dram_tensor("h_img", [ND, P, T], BF16, kind=dk).ap()
    cache_g = nc.dram_tensor("cache_g", [NIC, P, D], BF16).ap()
    cache_u = nc.dram_tensor("cache_u", [NIC, P, D], BF16).ap()
    cache_d = nc.dram_tensor("cache_d", [ND, P, IFF], BF16).ap()

    with ExitStack() as st:
        em = Emit(nc, st)
        ident_f, ident_b, ones1_f, ones_bf, cst = make_identities(em, st, nc)

        with ExitStack() as st0:
            with nc.Block() as blk:
                ser = Ser(em, "p0")
                pst = st0.enter_context(nc.psum_tensor("p0ps", [P, P], F32))
                bh_gate = vec_table(ser, st, nc, "bh_gate", b_gate, ND, ident_f, pst, 0.5)
                bh_in = vec_table(ser, st, nc, "bh_in", b_in, ND, ident_f, pst, 0.5)
                lam_t = vec_table(ser, st, nc, "lam_t", flam, ND, ident_f, pst)
                gsc = vec_table(ser, st, nc, "gsc", gate_s, NIC, ident_f, pst)
                usc = vec_table(ser, st, nc, "usc", up_s, NIC, ident_f, pst)
                dsc = vec_table(ser, st, nc, "dsc", down_s, ND, ident_f, pst, 0.5)
                gbias = st.enter_context(nc.sbuf_tensor("gbias", [P, NIC], F32))
                ubias = st.enter_context(nc.sbuf_tensor("ubias", [P, NIC], F32))
                dbias = st.enter_context(nc.sbuf_tensor("dbias", [P, ND], F32))
                chalf = st.enter_context(nc.sbuf_tensor("chalf", [P, ND], F32))
                tmp_a = st0.enter_context(nc.sbuf_tensor("tmp_a", [P, ND], F32))
                tmp_b = st0.enter_context(nc.sbuf_tensor("tmp_b", [P, ND], F32))
                ser.add("vector", lambda e: e.tensor_scalar(
                    gbias[:], gsc[:], -128.0, None, ALU.mult))
                ser.add("vector", lambda e: e.tensor_scalar(
                    ubias[:], usc[:], -128.0, None, ALU.mult))
                ser.add("vector", lambda e: e.tensor_scalar(
                    dbias[:], dsc[:], -128.0, None, ALU.mult))
                tmp_c = st0.enter_context(nc.sbuf_tensor("tmp_c", [P, ND], F32))
                ser.add("scalar", lambda e: e.activation(tmp_a[:], lam_t[:], AF.Abs))
                ser.add("scalar", lambda e: e.activation(tmp_b[:], tmp_a[:], AF.Exp, scale=-1.0))
                ser.add("scalar", lambda e: e.activation(tmp_c[:], tmp_b[:], AF.Ln, bias=1.0))
                ser.add("scalar", lambda e: e.activation(tmp_a[:], lam_t[:], AF.Relu))
                ser.add("vector", lambda e: e.tensor_tensor(
                    tmp_b[:], tmp_c[:], tmp_a[:], ALU.add))
                ser.add("vector", lambda e: e.tensor_scalar(
                    chalf[:], tmp_b[:], -4.0, None, ALU.mult))
                ser.emit(blk, extra_wait=(em.s("s_id"), 2))

        with ExitStack() as st_xn:
            xn_big = st_xn.enter_context(nc.sbuf_tensor("xn_big", [P, ND * T], BF16))
            xn3 = xn_big[:].rearrange("p (c t) -> p c t", c=ND)
            phase_p1(nc, em, x_in, xn_big, xn3, xn_img, ident_b, cst, st_img)
            phase_p2(nc, em, w_in, w_gate, h_img, xn3, ident_b,
                     bh_in, bh_gate, chalf, cst)
        phase_ffn_prep(nc, em, gate_w, up_w, down_w, cache_g, cache_u, cache_d,
                       gsc, gbias, usc, ubias, dsc, dbias, ident_b)
        phase_p3(nc, em, x_in, xn_img, h_img, x1_img,
                 cache_g, cache_u, cache_d, ident_b)

    return nc


def phase_p1(nc, em, x_in, xn_big, xn3, xn_img, ident_b, cst, st_img):
    st = em.stack
    em.roll()
    with ExitStack() as sp, nc.Block() as blk:
        for name in ["p1_ld0", "p1_ld1", "p1_sqr", "p1_var", "p1_ap", "p1_tp",
                     "p1_ev", "p1_st", "p1_rr", "p1_fs"]:
            em.new_sem(name)
        sb = lambda n_, s_, d_: sp.enter_context(nc.sbuf_tensor(n_, s_, d_))
        pt = lambda n_, s_, d_: sp.enter_context(nc.psum_tensor(n_, s_, d_))
        bx = [sb(f"bx{k}", [P, D], F32) for k in range(2)]
        sqb = [sb(f"sqb{k}", [P, D], BF16) for k in range(2)]
        xntm = [sb(f"xntm{k}", [P, D], BF16) for k in range(2)]
        tfb = sb("tfb", [P, D], F32)
        s1t = [sb(f"s1t{k}", [P, 1], F32) for k in range(2)]
        s2t = sb("s2t", [P, 1], F32)
        nmean = [sb(f"nmean{k}", [P, 1], F32) for k in range(2)]
        varb = [sb(f"varb{k}", [P, 1], F32) for k in range(2)]
        rstdb = sb("rstdb", [P, 1], F32)
        lnb = sb("lnb", [P, 1], F32)
        varc = sb("varc", [P, NTT], F32)
        rstdc = sb("rstdc", [P, NTT], F32)
        s2m = sb("s2m", [P, 1], F32)
        m2t = sb("m2t", [P, 1], F32)
        pstr = [pt(f"pstr{k}", [P, D], BF16) for k in range(2)]

        @blk.sync
        def _(e):
            for i in range(NTT):
                if i >= 2:
                    e.wait_ge(em.s("p1_ap"), i - 1)
                e.dma_start(bx[i % 2][:], x_in[i * P:(i + 1) * P, :]
                            ).then_inc(em.s(f"p1_ld{i % 2}"), 16)

        @blk.scalar
        def _(e):
            for i in range(NTT):
                e.wait_ge(em.s(f"p1_ld{i % 2}"), 16 * (i // 2 + 1))
                if i >= 2:
                    e.wait_ge(em.s("p1_var"), i - 1)
                e.activation(sqb[i % 2][:], bx[i % 2][:], AF.Square
                             ).then_inc(em.s("p1_sqr"), 1)
                e.wait_ge(em.s("p1_var"), i + 1)
                e.activation(lnb[:], varb[i % 2][:], AF.Ln, bias=cst[:, 0:1]
                             ).then_inc(em.s("p1_fs"), 1)
                e.wait_ge(em.s("p1_fs"), 2 * i + 1)
                e.activation(rstdb[:], lnb[:], AF.Exp, scale=-0.5
                             ).then_inc(em.s("p1_fs"), 1)
                e.wait_ge(em.s("p1_fs"), 2 * i + 2)
                e.activation(rstdc[:, i:i + 1], varb[i % 2][:], AF.Copy)
                e.activation(tfb[:], bx[i % 2][:], AF.Identity,
                             bias=nmean[i % 2][:, 0:1])
                if i >= 2:
                    e.wait_ge(em.s("p1_tp"), i - 1)
                e.activation(xntm[i % 2][:], tfb[:], AF.Identity,
                             scale=rstdb[:, 0:1]).then_inc(em.s("p1_ap"), 1)

        @blk.vector
        def _(e):
            for i in range(NTT):
                e.wait_ge(em.s(f"p1_ld{i % 2}"), 16 * (i // 2 + 1))
                if i >= 2:
                    e.wait_ge(em.s("p1_ap"), i - 1)
                fv = 6 * i
                e.reduce_sum(s1t[i % 2][:], bx[i % 2][:], axis=AX
                             ).then_inc(em.s("p1_rr"), 1)
                e.wait_ge(em.s("p1_rr"), fv + 1)
                e.tensor_scalar(nmean[i % 2][:], s1t[i % 2][:],
                                -1.0 / D, None, ALU.mult
                                ).then_inc(em.s("p1_rr"), 1)
                e.wait_ge(em.s("p1_sqr"), i + 1)
                e.reduce_sum(s2t[:], sqb[i % 2][:], axis=AX
                             ).then_inc(em.s("p1_rr"), 1)
                e.wait_ge(em.s("p1_rr"), fv + 3)
                e.tensor_scalar(s2m[:], s2t[:], 1.0 / D, None, ALU.mult
                                ).then_inc(em.s("p1_rr"), 1)
                e.tensor_tensor(m2t[:], nmean[i % 2][:],
                                nmean[i % 2][:], ALU.mult
                                ).then_inc(em.s("p1_rr"), 1)
                e.wait_ge(em.s("p1_rr"), fv + 5)
                e.tensor_tensor(varb[i % 2][:], s2m[:], m2t[:],
                                ALU.subtract).then_inc(em.s("p1_rr"), 1)
                e.wait_ge(em.s("p1_rr"), fv + 6)
                e.tensor_copy(varc[:, i:i + 1], varb[i % 2][:]
                              ).then_inc(em.s("p1_var"), 1)
                e.wait_ge(em.s("p1_tp"), i + 1)
                e.tensor_copy(xn3[:, :, i * P:(i + 1) * P],
                              pstr[i % 2][:].rearrange("p (c t) -> p c t", c=ND)
                              ).then_inc(em.s("p1_ev"), 1)

        @blk.tensor
        def _(e):
            for i in range(NTT):
                e.wait_ge(em.s("p1_ap"), i + 1)
                if i >= 2:
                    e.wait_ge(em.s("p1_ev"), i - 1)
                for dc in range(ND):
                    inst = e.transpose(pstr[i % 2][:, dc * P:(dc + 1) * P],
                                       xntm[i % 2][:, dc * P:(dc + 1) * P],
                                       ident_b[:])
                inst.then_inc(em.s("p1_tp"), 1)

        @blk.gpsimd
        def _(e):
            e.wait_ge(em.s("p1_ev"), NTT)
            e.dma_start(xn_img, xn_big[:]).then_inc(em.s("p1_st"), 16)
            e.dma_start(st_img[:, 0:NTT], varc[:]).then_inc(em.s("p1_st"), 16)
            e.dma_start(st_img[:, NTT:2 * NTT], rstdc[:]).then_inc(em.s("p1_st"), 16)
            e.wait_ge(em.s("p1_st"), 48)

        em.val["p1_ld0"] = 16 * (NTT // 2)
        em.val["p1_ld1"] = 16 * (NTT // 2)
        em.val["p1_sqr"] = NTT
        em.val["p1_rr"] = 6 * NTT
        em.val["p1_var"] = NTT
        em.val["p1_fs"] = 2 * NTT
        em.val["p1_ap"] = NTT
        em.val["p1_tp"] = NTT
        em.val["p1_ev"] = NTT
        em.val["p1_st"] = 48


def phase_p2(nc, em, w_in, w_gate, h_img, xn3, ident_b, bh_in, bh_gate, chalf, cst):
    em.roll()
    with ExitStack() as sp, nc.Block() as blk:
        for name in ["p2_ld0", "p2_ld1", "p2_tp", "p2_ev", "p2_mm", "p2_th",
                     "p2_al", "p2_a2", "p2_sq", "p2_xb", "p2_hst0", "p2_hst1"]:
            em.new_sem(name)
        sb = lambda n_, s_, d_: sp.enter_context(nc.sbuf_tensor(n_, s_, d_))
        pt = lambda n_, s_, d_: sp.enter_context(nc.psum_tensor(n_, s_, d_))
        sWn = [sb(f"sWn{k}", [P, 2 * D], BF16) for k in range(2)]
        sWT = [sb(f"sWT{k}", [P, 2 * D], BF16) for k in range(2)]
        pswt = [pt(f"pswt{k}", [P, D], BF16) for k in range(2)]
        psg = [pt(f"psg{k}", [P, TC], F32) for k in range(4)]
        th_r = [sb(f"th_r{k}", [P, T], F32) for k in range(2)]
        th_i = [sb(f"th_i{k}", [P, T], F32) for k in range(2)]
        al_b = [sb(f"al_b{k}", [P, T], F32) for k in range(2)]
        a2_b = [sb(f"a2_b{k}", [P, T], F32) for k in range(2)]
        u_b = sb("u_b", [P, T], F32)
        xb_b = sb("xb_b", [P, T], F32)
        hf_b = sb("hf_b", [P, T], F32)
        hb_b = [sb(f"hb_b{k}", [P, T], BF16) for k in range(2)]

        def mmc(e_, g_, tc_):
            return 8 * e_ + 4 * g_ + tc_ + 1

        @blk.gpsimd
        def _(e):
            for et in range(ND):
                if et >= 2:
                    e.wait_ge(em.s("p2_tp"), et - 1)
                e.dma_start(sWn[et % 2][:, 0:D], w_gate[et * P:(et + 1) * P, :]
                            ).then_inc(em.s(f"p2_ld{et % 2}"), 16)
                e.dma_start(sWn[et % 2][:, D:2 * D], w_in[et * P:(et + 1) * P, :]
                            ).then_inc(em.s(f"p2_ld{et % 2}"), 16)
                if et >= 1:
                    e.wait_ge(em.s("p2_xb"), et)
                    e.dma_start(h_img[et - 1], hb_b[(et - 1) % 2][:]
                                ).then_inc(em.s(f"p2_hst{(et - 1) % 2}"), 16)
            e.wait_ge(em.s("p2_xb"), ND)
            e.dma_start(h_img[ND - 1], hb_b[(ND - 1) % 2][:]
                        ).then_inc(em.s(f"p2_hst{(ND - 1) % 2}"), 16)
            e.wait_ge(em.s("p2_hst0"), 16 * (ND // 2))
            e.wait_ge(em.s("p2_hst1"), 16 * (ND // 2))

        @blk.tensor
        def _(e):
            for et in range(ND):
                e.wait_ge(em.s(f"p2_ld{et % 2}"), 32 * (et // 2 + 1))
                if et >= 1:
                    e.wait_ge(em.s("p2_ev"), et)
                for w in range(2):
                    for dc in range(ND):
                        inst = e.transpose(
                            pswt[w][:, dc * P:(dc + 1) * P],
                            sWn[et % 2][:, w * D + dc * P:w * D + (dc + 1) * P],
                            ident_b[:])
                inst.then_inc(em.s("p2_tp"), 1)
                e.wait_ge(em.s("p2_ev"), et + 1)
                for g in range(2):
                    for tc in range(NTC):
                        if tc >= 2:
                            e.wait_ge(em.s("p2_th"), mmc(et, g, tc - 2))
                        elif et >= 1:
                            e.wait_ge(em.s("p2_th"), mmc(et - 1, g, tc + 2))
                        slot = 2 * g + (tc % 2)
                        for dc in range(ND):
                            inst = e.matmul(
                                psg[slot][:],
                                sWT[et % 2][:, g * D + dc * P:g * D + (dc + 1) * P],
                                xn3[:, dc, tc * TC:(tc + 1) * TC],
                                start=(dc == 0), stop=(dc == ND - 1))
                        inst.then_inc(em.s("p2_mm"), 1)

        @blk.scalar
        def _(e):
            for et in range(ND):
                for g in range(2):
                    th = th_r if g == 0 else th_i
                    bh = bh_gate if g == 0 else bh_in
                    if et >= 2 and g == 0:
                        e.wait_ge(em.s("p2_xb"), et - 1)
                    for tc in range(NTC):
                        e.wait_ge(em.s("p2_mm"), mmc(et, g, tc))
                        slot = 2 * g + (tc % 2)
                        e.activation(th[et % 2][:, tc * TC:(tc + 1) * TC],
                                     psg[slot][:], AF.Tanh, scale=0.5,
                                     bias=bh[:, et:et + 1]
                                     ).then_inc(em.s("p2_th"), 1)
                e.activation(al_b[et % 2][:], th_r[et % 2][:], AF.Exp,
                             scale=chalf[:, et:et + 1],
                             bias=chalf[:, et:et + 1]).then_inc(em.s("p2_al"), 1)
                if et % 2 == 1:
                    for e2 in (et - 1, et):
                        e.wait_ge(em.s("p2_a2"), e2 + 1)
                        e.activation(th_r[e2 % 2][:], a2_b[e2 % 2][:], AF.Sqrt,
                                     scale=-0.25, bias=cst[:, 1:2]
                                     ).then_inc(em.s("p2_sq"), 1)

        def chain(e, et):
            k = et % 2
            e.wait_ge(em.s("p2_al"), et + 1)
            if et >= 2:
                e.wait_ge(em.s("p2_sq"), et - 1)
            e.tensor_tensor(a2_b[k][:], al_b[k][:], al_b[k][:], ALU.mult
                            ).then_inc(em.s("p2_a2"), 1)
            e.wait_ge(em.s("p2_sq"), et + 1)
            e.tensor_tensor(u_b[:], th_r[k][:], xn3[:, et, :], ALU.mult)
            e.scalar_tensor_tensor(xb_b[:], th_i[k][:], 1.0, u_b[:],
                                   ALU.add, ALU.mult)
            e.tensor_tensor_scan(hf_b[:], al_b[k][:], xb_b[:], 0.0,
                                 ALU.mult, ALU.add)
            if et >= 2:
                e.wait_ge(em.s(f"p2_hst{et % 2}"), 16 * (et // 2))
            e.tensor_copy(hb_b[k][:], hf_b[:]).then_inc(em.s("p2_xb"), 1)

        @blk.vector
        def _(e):
            for et in range(ND):
                e.wait_ge(em.s("p2_tp"), et + 1)
                if et >= 2:
                    e.wait_ge(em.s("p2_mm"), 8 * (et - 1))
                for w in range(2):
                    inst = e.tensor_copy(
                        sWT[et % 2][:, w * D:(w + 1) * D], pswt[w][:])
                inst.then_inc(em.s("p2_ev"), 1)
                if et >= 1:
                    chain(e, et - 1)
            chain(e, ND - 1)

        em.val["p2_ld0"] = 32 * (ND // 2)
        em.val["p2_ld1"] = 32 * (ND // 2)
        em.val["p2_tp"] = ND
        em.val["p2_ev"] = ND
        em.val["p2_mm"] = 8 * ND
        em.val["p2_th"] = 8 * ND
        em.val["p2_al"] = ND
        em.val["p2_a2"] = ND
        em.val["p2_sq"] = ND
        em.val["p2_xb"] = ND
        em.val["p2_hst0"] = 16 * (ND // 2)
        em.val["p2_hst1"] = 16 * (ND // 2)


def phase_ffn_prep(nc, em, gate_w, up_w, down_w, cache_g, cache_u, cache_d,
                   gsc, gbias, usc, ubias, dsc, dbias, ident_b):
    em.roll()
    with ExitStack() as sp, nc.Block() as blk:
        for name in (["fp_dq", "fp_tp", "fp_ev"]
                     + [f"fp_ld{k}" for k in range(4)]
                     + [f"fp_st{k}" for k in range(4)]):
            em.new_sem(name)
        sb = lambda n_, s_, d_: sp.enter_context(nc.sbuf_tensor(n_, s_, d_))
        pt = lambda n_, s_, d_: sp.enter_context(nc.psum_tensor(n_, s_, d_))
        raw = [sb(f"fpraw{k}", [P, D], I32) for k in range(4)]
        dq = [sb(f"fpdq{k}", [P, D], BF16) for k in range(4)]
        img = [sb(f"fpimg{k}", [P, D], BF16) for k in range(4)]
        pswt = [pt(f"fpps{k}", [P, D], BF16) for k in range(4)]

        jobs = []
        for ic in range(NIC):
            jobs.append(("g", gate_w[ic * P:(ic + 1) * P, :], gsc, gbias,
                         cache_g, ic))
            jobs.append(("u", up_w[ic * P:(ic + 1) * P, :], usc, ubias,
                         cache_u, ic))
        for hh in range(ND):
            for q in range(4):
                jobs.append(("d", down_w[hh * P:(hh + 1) * P,
                                         q * D:(q + 1) * D], dsc, dbias,
                             cache_d[hh].rearrange(
                                 "p (q f) -> q p f", q=4)[q], hh))
        NJ = len(jobs)

        @blk.sync
        def _(e):
            for j, (kind, src, s_t, b_t, dst, idx) in enumerate(jobs):
                if j >= 4:
                    e.wait_ge(em.s("fp_dq"), j - 3)
                e.dma_start(raw[j % 4][:], src).then_inc(em.s(f"fp_ld{j % 4}"), 16)

        @blk.scalar
        def _(e):
            for j, (kind, src, s_t, b_t, dst, idx) in enumerate(jobs):
                e.wait_ge(em.s(f"fp_ld{j % 4}"), 16 * (j // 4 + 1))
                if j >= 4:
                    e.wait_ge(em.s("fp_tp"), j - 3)
                e.activation(dq[j % 4][:], raw[j % 4][:], AF.Identity,
                             scale=s_t[:, idx:idx + 1],
                             bias=b_t[:, idx:idx + 1]).then_inc(em.s("fp_dq"), 1)

        @blk.tensor
        def _(e):
            for j in range(NJ):
                e.wait_ge(em.s("fp_dq"), j + 1)
                if j >= 4:
                    e.wait_ge(em.s("fp_ev"), j - 3)
                for dc in range(ND):
                    inst = e.transpose(pswt[j % 4][:, dc * P:(dc + 1) * P],
                                       dq[j % 4][:, dc * P:(dc + 1) * P],
                                       ident_b[:])
                inst.then_inc(em.s("fp_tp"), 1)

        @blk.vector
        def _(e):
            for j in range(NJ):
                e.wait_ge(em.s("fp_tp"), j + 1)
                if j >= 4:
                    e.wait_ge(em.s(f"fp_st{j % 4}"), 16 * (j // 4))
                e.tensor_copy(img[j % 4][:], pswt[j % 4][:]
                              ).then_inc(em.s("fp_ev"), 1)

        @blk.gpsimd
        def _(e):
            for j, (kind, src, s_t, b_t, dst, idx) in enumerate(jobs):
                e.wait_ge(em.s("fp_ev"), j + 1)
                e.dma_start(dst if kind == "d" else dst[idx], img[j % 4][:]
                            ).then_inc(em.s(f"fp_st{j % 4}"), 16)
            for k in range(4):
                e.wait_ge(em.s(f"fp_st{k}"), 16 * (NJ // 4 + (1 if NJ % 4 > k else 0)))

        for k in range(4):
            em.val[f"fp_ld{k}"] = 16 * (NJ // 4 + (1 if NJ % 4 > k else 0))
        em.val["fp_dq"] = NJ
        em.val["fp_tp"] = NJ
        em.val["fp_ev"] = NJ
        for k in range(4):
            em.val[f"fp_st{k}"] = 16 * (NJ // 4 + (1 if NJ % 4 > k else 0))


def phase_p3(nc, em, x_in, xn_img, h_img, x1_img, cache_g, cache_u, cache_d,
             ident_b):
    em.roll()
    with ExitStack() as sp, nc.Block() as blk:
        for name in ["p3_xld", "p3_wld0", "p3_wld1", "p3_mm", "p3_tg", "p3_hc",
                     "p3_dld0", "p3_dld1", "p3_dn", "p3_cb", "p3_st0", "p3_st1",
                     "p3_xtp", "p3_xev", "p3_hld", "p3_xw0", "p3_xw1"]:
            em.new_sem(name)
        sb = lambda n_, s_, d_: sp.enter_context(nc.sbuf_tensor(n_, s_, d_))
        pt = lambda n_, s_, d_: sp.enter_context(nc.psum_tensor(n_, s_, d_))
        xnw = sb("xnw", [P, ND * WIN], BF16)
        hact = sb("hact", [P, NIC * WIN], BF16)
        img_g = [sb(f"img_g{k}", [P, D], BF16) for k in range(2)]
        img_u = [sb(f"img_u{k}", [P, D], BF16) for k in range(2)]
        dimg = [sb(f"dimg{k}", [P, IFF], BF16) for k in range(2)]
        su_b = sb("su_b", [P, WIN], F32)
        tg_b = [sb(f"tg_b{k}", [P, WIN], F32) for k in range(2)]
        pg_b = sb("pg_b", [P, WIN], F32)
        sxtm = [sb(f"sxtm{k}", [P, D], BF16) for k in range(2)]
        xfm = sb("xfm", [P, ND * WIN], BF16)
        hwb = sb("hwb", [P, ND * WIN], BF16)
        t1b = sb("t1b", [P, WIN], F32)
        x1b = [sb(f"x1b{k}", [P, WIN], BF16) for k in range(2)]
        psG = [pt(f"psG{k}", [P, WIN], F32) for k in range(2)]
        psU = [pt(f"psU{k}", [P, WIN], F32) for k in range(2)]
        psD = [pt(f"psD{k}", [P, WIN], F32) for k in range(2)]
        psX = pt("psX", [P, D], BF16)
        xnw3 = xnw[:].rearrange("p (c t) -> p c t", c=ND)
        xfm3 = xfm[:].rearrange("p (c t) -> p c t", c=ND)
        hwb3 = hwb[:].rearrange("p (c t) -> p c t", c=ND)
        NSUB = WIN // P

        @blk.sync
        def _(e):
            for w in range(NW):
                if w >= 1:
                    e.wait_ge(em.s("p3_mm"), 2 * NIC * w)
                e.dma_start(
                    xnw3, xn_img.rearrange("p (c t) -> p c t", c=ND)
                    [:, :, w * WIN:(w + 1) * WIN]).then_inc(em.s("p3_xld"), 16)
                for ic in range(NIC):
                    n = w * NIC + ic
                    if n >= 2:
                        e.wait_ge(em.s("p3_hc"), n - 1)
                    e.dma_start(img_g[ic % 2][:], cache_g[ic]
                                ).then_inc(em.s(f"p3_wld{ic % 2}"), 16)
                    e.dma_start(img_u[ic % 2][:], cache_u[ic]
                                ).then_inc(em.s(f"p3_wld{ic % 2}"), 16)
                for hh in range(ND):
                    m = w * ND + hh
                    if m >= 2:
                        e.wait_ge(em.s("p3_dn"), m - 1)
                    e.dma_start(dimg[hh % 2][:], cache_d[hh]
                                ).then_inc(em.s(f"p3_dld{hh % 2}"), 16)

        @blk.gpsimd
        def _(e):
            for w in range(NW):
                for s in range(NSUB):
                    i = w * NSUB + s
                    if i >= 2:
                        e.wait_ge(em.s("p3_xtp"), i - 1)
                    e.dma_start(sxtm[i % 2][:],
                                x_in[(w * WIN + s * P):(w * WIN + (s + 1) * P), :]
                                ).then_inc(em.s(f"p3_xw{i % 2}"), 16)
                if w >= 1:
                    e.wait_ge(em.s("p3_cb"), w * ND)
                e.dma_start(hwb3,
                            h_img[:, :, w * WIN:(w + 1) * WIN]
                            .rearrange("c p t -> p c t")
                            ).then_inc(em.s("p3_hld"), 16)
                for hh in range(ND):
                    m = w * ND + hh
                    e.wait_ge(em.s("p3_cb"), m + 1)
                    e.dma_start(x1_img.rearrange("p (c t) -> p c t", c=ND)
                                [:, hh, w * WIN:(w + 1) * WIN],
                                x1b[hh % 2][:]).then_inc(em.s(f"p3_st{hh % 2}"), 16)
            e.wait_ge(em.s("p3_st0"), 16 * NW * ND // 2)
            e.wait_ge(em.s("p3_st1"), 16 * NW * ND // 2)

        @blk.tensor
        def _(e):
            for w in range(NW):
                for s in range(NSUB):
                    i = w * NSUB + s
                    e.wait_ge(em.s(f"p3_xw{i % 2}"), 16 * (i // 2 + 1))
                    if i >= 1:
                        e.wait_ge(em.s("p3_xev"), i)
                    for dc in range(ND):
                        inst = e.transpose(psX[:, dc * P:(dc + 1) * P],
                                           sxtm[i % 2][:, dc * P:(dc + 1) * P],
                                           ident_b[:])
                    inst.then_inc(em.s("p3_xtp"), 1)
                for ic in range(NIC):
                    n = w * NIC + ic
                    e.wait_ge(em.s(f"p3_wld{ic % 2}"), 32 * ((w * NIC + ic) // 2 + 1))
                    e.wait_ge(em.s("p3_xld"), 16 * (w + 1))
                    if n >= 2:
                        e.wait_ge(em.s("p3_hc"), n - 1)
                    for dc in range(ND):
                        inst = e.matmul(psG[ic % 2][:],
                                        img_g[ic % 2][:, dc * P:(dc + 1) * P],
                                        xnw3[:, dc, :],
                                        start=(dc == 0), stop=(dc == ND - 1))
                    inst.then_inc(em.s("p3_mm"), 1)
                    for dc in range(ND):
                        inst = e.matmul(psU[ic % 2][:],
                                        img_u[ic % 2][:, dc * P:(dc + 1) * P],
                                        xnw3[:, dc, :],
                                        start=(dc == 0), stop=(dc == ND - 1))
                    inst.then_inc(em.s("p3_mm"), 1)
                for hh in range(ND):
                    m = w * ND + hh
                    e.wait_ge(em.s(f"p3_dld{hh % 2}"), 16 * (m // 2 + 1))
                    e.wait_ge(em.s("p3_hc"), (w + 1) * NIC)
                    if m >= 2:
                        e.wait_ge(em.s("p3_cb"), m - 1)
                    for ic in range(NIC):
                        inst = e.matmul(psD[hh % 2][:],
                                        dimg[hh % 2][:, ic * P:(ic + 1) * P],
                                        hact[:, ic * WIN:(ic + 1) * WIN],
                                        start=(ic == 0), stop=(ic == NIC - 1))
                    inst.then_inc(em.s("p3_dn"), 1)

        @blk.scalar
        def _(e):
            for w in range(NW):
                for ic in range(NIC):
                    n = w * NIC + ic
                    e.wait_ge(em.s("p3_mm"), 2 * n + 1)
                    if n >= 2:
                        e.wait_ge(em.s("p3_hc"), n - 1)
                    e.activation(tg_b[ic % 2][:], psG[ic % 2][:], AF.Tanh,
                                 scale=0.5).then_inc(em.s("p3_tg"), 1)

        @blk.vector
        def _(e):
            for w in range(NW):
                for s in range(NSUB):
                    i = w * NSUB + s
                    e.wait_ge(em.s("p3_xtp"), i + 1)
                    if w >= 1 and s == 0:
                        e.wait_ge(em.s("p3_cb"), w * ND)
                    e.tensor_copy(xfm3[:, :, s * P:(s + 1) * P],
                                  psX[:].rearrange("p (c t) -> p c t", c=ND)
                                  ).then_inc(em.s("p3_xev"), 1)
                for ic in range(NIC):
                    n = w * NIC + ic
                    e.wait_ge(em.s("p3_mm"), 2 * n + 2)
                    e.tensor_copy(su_b[:], psU[ic % 2][:])
                    e.tensor_tensor(pg_b[:], psG[ic % 2][:], su_b[:], ALU.mult)
                    e.wait_ge(em.s("p3_tg"), n + 1)
                    if w >= 1:
                        e.wait_ge(em.s("p3_dn"), w * ND)
                    e.scalar_tensor_tensor(hact[:, ic * WIN:(ic + 1) * WIN],
                                           tg_b[ic % 2][:], 1.0, pg_b[:],
                                           ALU.add, ALU.mult
                                           ).then_inc(em.s("p3_hc"), 1)
                for hh in range(ND):
                    m = w * ND + hh
                    e.wait_ge(em.s("p3_dn"), m + 1)
                    if hh == 0:
                        e.wait_ge(em.s("p3_xev"), (w + 1) * NSUB)
                        e.wait_ge(em.s("p3_hld"), 16 * (w + 1))
                    if m >= 2:
                        e.wait_ge(em.s(f"p3_st{hh % 2}"), 16 * (m // 2))
                    e.tensor_tensor(t1b[:], psD[hh % 2][:], xfm3[:, hh, :],
                                    ALU.add)
                    e.tensor_tensor(x1b[hh % 2][:], t1b[:], hwb3[:, hh, :],
                                    ALU.add).then_inc(em.s("p3_cb"), 1)

        em.val["p3_xld"] = 16 * NW
        em.val["p3_wld0"] = 32 * NW * NIC // 2
        em.val["p3_wld1"] = 32 * NW * NIC // 2
        em.val["p3_dld0"] = 16 * NW * ND // 2
        em.val["p3_dld1"] = 16 * NW * ND // 2
        em.val["p3_mm"] = 2 * NW * NIC
        em.val["p3_tg"] = NW * NIC
        em.val["p3_hc"] = NW * NIC
        em.val["p3_dn"] = NW * ND
        em.val["p3_cb"] = NW * ND
        em.val["p3_st0"] = 16 * NW * ND // 2
        em.val["p3_st1"] = 16 * NW * ND // 2
        em.val["p3_xw0"] = 16 * NW * NSUB // 2
        em.val["p3_xw1"] = 16 * NW * NSUB // 2
        em.val["p3_xtp"] = NW * NSUB
        em.val["p3_xev"] = NW * NSUB
        em.val["p3_hld"] = 16 * NW




def build_l1(dbg=False):
    nc = bass.Bass("TRN2", target_bir_lowering=False)

    x1_img = nc.dram_tensor("x1_img", [P, ND * T], BF16, kind="ExternalInput").ap()
    wq = nc.dram_tensor("L1_attn_Wq", [D, 2 * DH * NH], F32, kind="ExternalInput").ap()
    wk = nc.dram_tensor("L1_attn_Wk", [D, 2 * DH * NH], F32, kind="ExternalInput").ap()
    wv = nc.dram_tensor("L1_attn_Wv", [D, DH * NH], F32, kind="ExternalInput").ap()
    wo = nc.dram_tensor("L1_attn_Wo", [DH * NH, D], F32, kind="ExternalInput").ap()
    nsc_in = nc.dram_tensor("L1_attn_norm_scale", [D], F32, kind="ExternalInput").ap()
    lam_in = nc.dram_tensor("L1_attn_lam", [NH], F32, kind="ExternalInput").ap()
    factors = nc.dram_tensor("L1_ffn_factors", [8, D, D], F32, kind="ExternalInput").ap()
    bias_in = nc.dram_tensor("L1_ffn_bias", [D], F32, kind="ExternalInput").ap()

    out = nc.dram_tensor("out", [T, D], F32, kind="ExternalOutput").ap()
    dk = "ExternalOutput" if dbg else "Internal"
    xn2_img = nc.dram_tensor("xn2_img", [P, ND * T], BF16, kind=dk).ap()
    oat_img = nc.dram_tensor("oat_img", [ND, P, T], BF16, kind=dk).ap()
    f_img = nc.dram_tensor("f_img", [ND, P, D], BF16).ap()

    with ExitStack() as st:
        em = Emit(nc, st)
        ident_f, ident_b, ones1_f, ones_bf, cst = make_identities(em, st, nc)

        with ExitStack() as st0:
            with nc.Block() as blk:
                ser = Ser(em, "q0")
                pst = st0.enter_context(nc.psum_tensor("q0ps", [P, P], F32))
                nsc = vec_table(ser, st, nc, "nsc", nsc_in, ND, ident_f, pst)
                bs_t = vec_table(ser, st, nc, "bs_t", bias_in, ND, ident_f, pst)
                lam_row = st.enter_context(nc.sbuf_tensor("lam_row", [1, NH], F32))
                lam_b = st.enter_context(nc.sbuf_tensor("lam_b", [P, NH], F32))
                om = st.enter_context(nc.sbuf_tensor("om", [1, 1], F32))
                t_ml = st0.enter_context(nc.sbuf_tensor("t_ml", [1, 1], F32))
                t_ms = st0.enter_context(nc.sbuf_tensor("t_ms", [1, 1], F32))
                ser.add("sync", lambda e: e.dma_start(lam_row[:], lam_in.rearrange("(a b) -> a b", a=1)), dma=True)
                ser.add("tensor", lambda e: e.matmul(pst[:, 0:NH], ones1_f[0:1, :],
                                                     lam_row[:], start=True, stop=True))
                ser.add("vector", lambda e: e.tensor_copy(lam_b[:], pst[:, 0:NH]))
                ser.add("vector", lambda e: e.reduce_sum(t_ml[:], lam_row[:], axis=AX))
                ser.add("vector", lambda e: e.tensor_scalar(t_ms[:], t_ml[:], 1.0 / NH, None, ALU.mult))
                ser.add("vector", lambda e: e.tensor_scalar(om[:], t_ms[:], -1.0, 1.0, ALU.mult, ALU.add))
                ser.emit(blk, extra_wait=(em.s("s_id"), 2))

        em.roll()
        with ExitStack() as sf, nc.Block() as blk:
            for k in range(8):
                em.new_sem(f"q0f_ld{k}")
            for name in ["q0f_sum", "q0f_st"]:
                em.new_sem(name)
            fb = [sf.enter_context(nc.sbuf_tensor(f"fb{k}", [P, D], F32))
                  for k in range(8)]
            fta = sf.enter_context(nc.sbuf_tensor("fta", [P, D], F32))
            ftb = sf.enter_context(nc.sbuf_tensor("ftb", [P, D], F32))
            sFb = sf.enter_context(nc.sbuf_tensor("sFb", [P, D], BF16))

            @blk.sync
            def _(e):
                for dc in range(ND):
                    for r in range(8):
                        if dc >= 1:
                            e.wait_ge(em.s("q0f_sum"), dc)
                        e.dma_start(fb[r][:], factors[r, dc * P:(dc + 1) * P, :]
                                    ).then_inc(em.s(f"q0f_ld{r}"), 16)

            @blk.vector
            def _(e):
                for dc in range(ND):
                    for r in range(8):
                        e.wait_ge(em.s(f"q0f_ld{r}"), 16 * (dc + 1))
                    e.tensor_tensor(fta[:], fb[0][:], fb[1][:], ALU.add)
                    e.tensor_tensor(ftb[:], fta[:], fb[2][:], ALU.add)
                    e.tensor_tensor(fta[:], ftb[:], fb[3][:], ALU.add)
                    e.tensor_tensor(ftb[:], fta[:], fb[4][:], ALU.add)
                    e.tensor_tensor(fta[:], ftb[:], fb[5][:], ALU.add)
                    e.tensor_tensor(ftb[:], fta[:], fb[6][:], ALU.add)
                    e.tensor_tensor(fta[:], ftb[:], fb[7][:], ALU.add)
                    if dc >= 1:
                        e.wait_ge(em.s("q0f_st"), 16 * dc)
                    e.tensor_scalar(sFb[:], fta[:], 8.0 ** -0.5, None, ALU.mult
                                    ).then_inc(em.s("q0f_sum"), 1)

            @blk.gpsimd
            def _(e):
                for dc in range(ND):
                    e.wait_ge(em.s("q0f_sum"), dc + 1)
                    e.dma_start(f_img[dc], sFb[:]).then_inc(em.s("q0f_st"), 16)
                e.wait_ge(em.s("q0f_st"), 16 * ND)

            em.val["q0f_sum"] = ND
            em.val["q0f_st"] = 16 * ND
            for r in range(8):
                em.val[f"q0f_ld{r}"] = 16 * ND

        xn2_big = st.enter_context(nc.sbuf_tensor("xn2_big", [P, ND * T], BF16))
        xn23 = xn2_big[:].rearrange("p (c t) -> p c t", c=ND)

        phase_q1(nc, em, x1_img, xn2_big, xn23, xn2_img, ones_bf, ones1_f, cst)
        phase_q2(nc, em, wq, wk, wv, xn23, oat_img, lam_b, ident_b)
        phase_q3(nc, em, oat_img, ones_bf, ones1_f, cst, nsc, om)
        phase_q4(nc, em, wo, f_img, bs_t, x1_img, xn2_img, oat_img, out,
                 ident_f)

    return nc


def phase_q1(nc, em, x1_img, xn2_big, xn23, xn2_img, ones_bf, ones1_f, cst):
    em.roll()
    with ExitStack() as sp, nc.Block() as blk:
        for name in ["q1_ld0", "q1_ld1", "q1_sq", "q1_mm", "q1_row", "q1_bc",
                     "q1_ap", "q1_st", "q1_fs", "q1_ld2", "q1_ld3"]:
            em.new_sem(name)
        sb = lambda n_, s_, d_: sp.enter_context(nc.sbuf_tensor(n_, s_, d_))
        x1t = [sb(f"q1x{k}", [P, D], BF16) for k in range(2)]
        sqt = [sb(f"q1s{k}", [P, D], BF16) for k in range(2)]
        rows_c = sb("rows_c", [1, D], F32)
        rows_d = sb("rows_d", [1, D], F32)
        mrow = sb("mrow", [1, D], F32)[:]
        s2row = sb("s2row", [1, D], F32)[:]
        m2row = sb("m2row", [1, D], F32)[:]
        varrow = sb("varrow", [1, D], F32)[:]
        lnrow = sb("lnrow", [1, D], F32)[:]
        arow = rows_c[:]
        brow = rows_d[:]
        tmp = sb("q1tmp", [P, D], F32)
        x13 = x1_img.rearrange("p (c t) -> p c t", c=ND)

        with ExitStack() as sps:
            ps_s1 = sps.enter_context(nc.psum_tensor("ps_s1", [1, D], F32))
            ps_s2 = sps.enter_context(nc.psum_tensor("ps_s2", [1, D], F32))

            @blk.sync
            def _(e):
                for et in range(ND):
                    if et >= 2:
                        e.wait_ge(em.s("q1_sq"), et - 1)
                        e.wait_ge(em.s("q1_mm"), 8 * (et - 1))
                    e.dma_start(x1t[et % 2][:], x13[:, et, :]
                                ).then_inc(em.s(f"q1_ld{et % 2}"), 16)

            @blk.scalar
            def _(e):
                for et in range(ND):
                    e.wait_ge(em.s(f"q1_ld{et % 2}"), 16 * (et // 2 + 1))
                    if et >= 2:
                        e.wait_ge(em.s("q1_mm"), 8 * (et - 1))
                    e.activation(sqt[et % 2][:], x1t[et % 2][:], AF.Square
                                 ).then_inc(em.s("q1_sq"), 1)
                e.wait_ge(em.s("q1_row"), 3)
                e.activation(lnrow, varrow, AF.Ln, bias=cst[0:1, 0:1]
                             ).then_inc(em.s("q1_fs"), 1)
                e.wait_ge(em.s("q1_fs"), 1)
                e.activation(arow, lnrow, AF.Exp, scale=-0.5
                             ).then_inc(em.s("q1_fs"), 1)
                e.wait_ge(em.s("q1_fs"), 2)
                e.nop().then_inc(em.s("q1_row"), 1)

            @blk.tensor
            def _(e):
                for et in range(ND):
                    e.wait_ge(em.s("q1_sq"), et + 1)
                    for tc in range(NTC):
                        e.matmul(ps_s1[:, tc * TC:(tc + 1) * TC], ones_bf[:],
                                 x1t[et % 2][:, tc * TC:(tc + 1) * TC],
                                 start=(et == 0), stop=(et == ND - 1))
                        inst = e.matmul(ps_s2[:, tc * TC:(tc + 1) * TC],
                                        ones_bf[:],
                                        sqt[et % 2][:, tc * TC:(tc + 1) * TC],
                                        start=(et == 0), stop=(et == ND - 1))
                    inst.then_inc(em.s("q1_mm"), 8)

            @blk.vector
            def _(e):
                e.wait_ge(em.s("q1_mm"), 8 * ND)
                e.tensor_scalar(mrow, ps_s1[:], 1.0 / D, None, ALU.mult)
                e.tensor_scalar(s2row, ps_s2[:], 1.0 / D, None, ALU.mult
                                ).then_inc(em.s("q1_row"), 1)
                e.wait_ge(em.s("q1_row"), 1)
                e.tensor_tensor(m2row, mrow, mrow, ALU.mult
                                ).then_inc(em.s("q1_row"), 1)
                e.wait_ge(em.s("q1_row"), 2)
                e.tensor_tensor(varrow, s2row, m2row, ALU.subtract
                                ).then_inc(em.s("q1_row"), 1)
                e.wait_ge(em.s("q1_row"), 4)
                e.scalar_tensor_tensor(brow, mrow, -1.0, arow,
                                       ALU.mult, ALU.mult
                                       ).then_inc(em.s("q1_row"), 1)

            em.val["q1_sq"] = ND
            em.val["q1_mm"] = 8 * ND
            em.val["q1_row"] = 5
            em.val["q1_fs"] = 2
            em.val["q1_ld0"] = 16 * (ND // 2)
            em.val["q1_ld1"] = 16 * (ND // 2)

        with ExitStack() as sps2:
            ps_A = sps2.enter_context(nc.psum_tensor("ps_A", [P, D], F32))
            ps_B = sps2.enter_context(nc.psum_tensor("ps_B", [P, D], F32))

            @blk.tensor
            def _(e):
                e.wait_ge(em.s("q1_row"), 5)
                for tc in range(NTC):
                    e.matmul(ps_A[:, tc * TC:(tc + 1) * TC], ones1_f[0:1, :],
                             rows_c[0:1, tc * TC:(tc + 1) * TC], start=True,
                             stop=True)
                    inst = e.matmul(ps_B[:, tc * TC:(tc + 1) * TC],
                                    ones1_f[0:1, :],
                                    rows_d[0:1, tc * TC:(tc + 1) * TC],
                                    start=True, stop=True)
                inst.then_inc(em.s("q1_bc"), 1)

            @blk.sync
            def _(e):
                e.wait_ge(em.s("q1_mm"), 8 * ND)
                for et in range(ND):
                    if et >= 2:
                        e.wait_ge(em.s("q1_ap"), et - 1)
                    e.dma_start(x1t[et % 2][:], x13[:, et, :]
                                ).then_inc(em.s(f"q1_ld{2 + et % 2}"), 16)

            @blk.vector
            def _(e):
                e.wait_ge(em.s("q1_bc"), 1)
                for et in range(ND):
                    e.wait_ge(em.s(f"q1_ld{2 + et % 2}"), 16 * (et // 2 + 1))
                    e.tensor_tensor(tmp[:], x1t[et % 2][:], ps_A[:], ALU.mult)
                    e.tensor_tensor(xn23[:, et, :], tmp[:], ps_B[:], ALU.add
                                    ).then_inc(em.s("q1_ap"), 1)

            @blk.gpsimd
            def _(e):
                e.wait_ge(em.s("q1_ap"), ND)
                e.dma_start(xn2_img, xn2_big[:]).then_inc(em.s("q1_st"), 16)
                e.wait_ge(em.s("q1_st"), 16)

            em.val["q1_bc"] = 1
            em.val["q1_ap"] = ND
            em.val["q1_st"] = 16
            em.val["q1_ld2"] = 16 * (ND // 2)
            em.val["q1_ld3"] = 16 * (ND // 2)


def phase_q2(nc, em, wq, wk, wv, xn23, oat_img, lam_b, ident_b):
    em.roll()
    with ExitStack() as sp, nc.Block() as blk:
        for name in ["q2_wl0", "q2_wl1", "q2_qg", "q2_qe", "q2_vt", "q2_ve",
                     "q2_sc", "q2_e", "q2_d", "q2_ed", "q2_aw", "q2_tp",
                     "q2_awe", "q2_av", "q2_oev", "q2_fv", "q2_ost0",
                     "q2_ost1"]:
            em.new_sem(name)
        sb = lambda n_, s_, d_: sp.enter_context(nc.sbuf_tensor(n_, s_, d_))
        wqh = [sb(f"wqh{k}", [P, ND * 256], BF16) for k in range(2)]
        wkh = [sb(f"wkh{k}", [P, ND * 256], BF16) for k in range(2)]
        wvh = [sb(f"wvh{k}", [P, ND * P], BF16) for k in range(2)]
        q1f = [sb(f"q1f{k}", [P, T], BF16) for k in range(2)]
        q2f = [sb(f"q2f{k}", [P, T], BF16) for k in range(2)]
        k1f = [sb(f"k1f{k}", [P, T], BF16) for k in range(2)]
        k2f = [sb(f"k2f{k}", [P, T], BF16) for k in range(2)]
        vf = [sb(f"vf{k}", [P, T], BF16) for k in range(2)]
        vt = sb("vt", [P, T], BF16)
        e1b = [sb(f"e1b{k}", [P, T], BF16) for k in range(2)]
        e2b = sb("e2b", [P, T], BF16)
        w2b = sb("w2b", [P, T], BF16)
        db = [sb(f"db{k}", [P, T], BF16) for k in range(2)]
        edb = [sb(f"edb{k}", [P, T], BF16) for k in range(2)]
        awb = [sb(f"awb{k}", [P, T], BF16) for k in range(2)]
        awsb = [sb(f"awsb{k}", [P, T], BF16) for k in range(2)]
        oath = sb("oath", [P, T], BF16)
        sum1 = [sb(f"sum1{k}", [P, 1], F32) for k in range(2)]
        sum2 = [sb(f"sum2{k}", [P, 1], F32) for k in range(2)]
        sumd = [sb(f"sumd{k}", [P, 1], F32) for k in range(2)]
        inv1 = sb("inv1", [P, 1], F32)
        inv2v = sb("inv2v", [P, 1], F32)
        invd = sb("invd", [P, 1], F32)
        l2s = sb("l2s", [P, 1], F32)
        ps_sc_t = sp.enter_context(nc.psum_tensor("ps_sc", [P, T], F32))
        ps_tp_t = sp.enter_context(nc.psum_tensor("ps_tp", [P, T // 2], F32))
        ps_av_t = sp.enter_context(nc.psum_tensor("ps_av", [P, P], F32))
        ps_qkv_t = sp.enter_context(nc.psum_tensor("ps_qkv", [P, TC], F32))
        ps_sc = ps_sc_t[:]
        ps_av = ps_av_t[:]
        ps_qkv = ps_qkv_t[:]
        pstp_bf = ps_tp_t.bitcast(BF16)
        wq3 = wq.rearrange("(c p) e -> p c e", p=P)
        wk3 = wk.rearrange("(c p) e -> p c e", p=P)
        wv3 = wv.rearrange("(c p) e -> p c e", p=P)

        def qkv_lhsT(h, oi, dc):
            if oi == 0:
                return wqh[h % 2][:, dc * 256:dc * 256 + 128]
            if oi == 1:
                return wqh[h % 2][:, dc * 256 + 128:dc * 256 + 256]
            if oi == 2:
                return wkh[h % 2][:, dc * 256:dc * 256 + 128]
            if oi == 3:
                return wkh[h % 2][:, dc * 256 + 128:dc * 256 + 256]
            return wvh[h % 2][:, dc * 128:dc * 128 + 128]

        def qkv_dst(h, oi):
            return [q1f, q2f, k1f, k2f, vf][oi][h % 2]

        PREF = {qt: ([2 * qt, 2 * qt + 1] if qt < 4 else [qt + 4])
                for qt in range(NTT)}

        @blk.gpsimd
        def _(e):
            def load(h):
                if h >= 2:
                    e.wait_ge(em.s("q2_qg"), 20 * (h - 1))
                e.dma_start(wqh[h % 2][:].rearrange("p (c e) -> p c e", e=256),
                            wq3[:, :, h * 256:(h + 1) * 256]
                            ).then_inc(em.s(f"q2_wl{h % 2}"), 16)
                e.dma_start(wkh[h % 2][:].rearrange("p (c e) -> p c e", e=256),
                            wk3[:, :, h * 256:(h + 1) * 256]
                            ).then_inc(em.s(f"q2_wl{h % 2}"), 16)
                e.dma_start(wvh[h % 2][:].rearrange("p (c e) -> p c e", e=128),
                            wv3[:, :, h * 128:(h + 1) * 128]
                            ).then_inc(em.s(f"q2_wl{h % 2}"), 16)

            load(0)
            load(1)
            for h in range(NH):
                if h + 2 < NH:
                    load(h + 2)
                e.wait_ge(em.s("q2_oev"), NH * (h + 1))
                e.dma_start(oat_img[h], oath[:]).then_inc(em.s("q2_ost0"), 16)
            e.wait_ge(em.s("q2_ost0"), 16 * NH)

        def pe_group(e, h, g):
            oi, tc = divmod((g - 1) % 20, NTC)
            if g >= 2:
                e.wait_ge(em.s("q2_qe"), g - 1)
            for dc in range(ND):
                inst = e.matmul(ps_qkv, qkv_lhsT(h, oi, dc),
                                xn23[:, dc, tc * TC:(tc + 1) * TC],
                                start=(dc == 0), stop=(dc == ND - 1))
            inst.then_inc(em.s("q2_qg"), 1)

        def pe_tp_av(e, m):
            hh, qq = divmod(m, NTT)
            e.wait_ge(em.s("q2_aw"), m + 1)
            e.wait_ge(em.s("q2_awe"), m)
            if qq == 0:
                e.wait_ge(em.s("q2_ve"), hh + 1)
            for kt in range(NTT):
                inst = e.transpose(pstp_bf[:, kt * P:(kt + 1) * P],
                                   awb[m % 2][:, kt * P:(kt + 1) * P],
                                   ident_b[:])
            inst.then_inc(em.s("q2_tp"), 1)
            e.wait_ge(em.s("q2_awe"), m + 1)
            e.wait_ge(em.s("q2_oev"), m)
            for kt in range(NTT):
                inst = e.matmul(ps_av, vt[:, kt * P:(kt + 1) * P],
                                awsb[m % 2][:, kt * P:(kt + 1) * P],
                                start=(kt == 0), stop=(kt == NTT - 1))
            inst.then_inc(em.s("q2_av"), 1)

        @blk.tensor
        def _(e):
            e.wait_ge(em.s("q2_wl0"), 48)
            for g in range(1, 21):
                pe_group(e, 0, g)
            e.wait_ge(em.s("q2_qe"), 20)
            for kt in range(NTT):
                inst = e.transpose(pstp_bf[:, kt * P:(kt + 1) * P],
                                   vf[0][:, kt * P:(kt + 1) * P], ident_b[:])
            inst.then_inc(em.s("q2_vt"), 1)
            for n in range(NH * NTT):
                h, qt = divmod(n, NTT)
                pre = PREF[qt] if h < NH - 1 else []
                for j in pre:
                    if j == 0:
                        e.wait_ge(em.s(f"q2_wl{(h + 1) % 2}"),
                                  48 * ((h + 1) // 2 + 1))
                    pe_group(e, h + 1, 20 * (h + 1) + j + 1)
                if n >= 1:
                    e.wait_ge(em.s("q2_e"), 2 * n)
                for tc in range(NTC):
                    inst = e.matmul(ps_sc[:, tc * TC:(tc + 1) * TC],
                                    q1f[h % 2][:, qt * P:(qt + 1) * P],
                                    k1f[h % 2][:, tc * TC:(tc + 1) * TC],
                                    start=True, stop=True)
                inst.then_inc(em.s("q2_sc"), 1)
                e.wait_ge(em.s("q2_e"), 2 * n + 1)
                for tc in range(NTC):
                    inst = e.matmul(ps_sc[:, tc * TC:(tc + 1) * TC],
                                    q2f[h % 2][:, qt * P:(qt + 1) * P],
                                    k2f[h % 2][:, tc * TC:(tc + 1) * TC],
                                    start=True, stop=True)
                inst.then_inc(em.s("q2_sc"), 1)
                if n >= 1:
                    pe_tp_av(e, n - 1)
                    hh, qq = divmod(n - 1, NTT)
                    if qq == NTT - 1 and hh < NH - 1:
                        e.wait_ge(em.s("q2_qe"), 20 * (hh + 2))
                        for kt in range(NTT):
                            inst = e.transpose(
                                pstp_bf[:, kt * P:(kt + 1) * P],
                                vf[(hh + 1) % 2][:, kt * P:(kt + 1) * P],
                                ident_b[:])
                        inst.then_inc(em.s("q2_vt"), 1)
            pe_tp_av(e, NH * NTT - 1)

        @blk.scalar
        def _(e):
            for h in range(NH):
                for qt in range(NTT):
                    n = NH * h + qt
                    e.wait_ge(em.s("q2_sc"), 2 * n + 1)
                    if n >= 2:
                        e.wait_ge(em.s("q2_d"), n - 1)
                    e.activation(e1b[n % 2][:], ps_sc, AF.Exp, scale=SCALE_QK,
                                 accum_out=sum1[n % 2][:]).then_inc(em.s("q2_e"), 1)
                    e.wait_ge(em.s("q2_sc"), 2 * n + 2)
                    if n >= 1:
                        e.wait_ge(em.s("q2_d"), n)
                    e.activation(e2b[:], ps_sc, AF.Exp, scale=SCALE_QK,
                                 accum_out=sum2[n % 2][:]).then_inc(em.s("q2_e"), 1)
                    e.wait_ge(em.s("q2_d"), n + 1)
                    if n >= 2:
                        e.wait_ge(em.s("q2_aw"), n - 1)
                    e.activation(edb[n % 2][:], db[n % 2][:], AF.Exp,
                                 accum_out=sumd[n % 2][:]).then_inc(em.s("q2_ed"), 1)

        @blk.vector
        def _(e):
            fv = [0]

            def fence(inst):
                fv[0] += 1
                inst.then_inc(em.s("q2_fv"), 1)
                e.wait_ge(em.s("q2_fv"), fv[0])

            def qkv_evac(h, g):
                oi, tc = divmod((g - 1) % 20, NTC)
                e.wait_ge(em.s("q2_qg"), g)
                e.tensor_copy(qkv_dst(h, oi)[:, tc * TC:(tc + 1) * TC],
                              ps_qkv).then_inc(em.s("q2_qe"), 1)

            def pref_evacs(n):
                h, qt = divmod(n, NTT)
                if h >= NH - 1:
                    return
                if qt == 0:
                    e.wait_ge(em.s("q2_sc"), 32 * h)
                for j in PREF[qt]:
                    qkv_evac(h + 1, 20 * (h + 1) + j + 1)

            for g in range(1, 21):
                qkv_evac(0, g)
            e.wait_ge(em.s("q2_vt"), 1)
            e.tensor_copy(vt[:], pstp_bf[:, 0:T]).then_inc(em.s("q2_ve"), 1)
            pref_evacs(0)
            for h in range(NH):
                for qt in range(NTT):
                    n = NH * h + qt
                    e.wait_ge(em.s("q2_e"), 2 * n + 1)
                    fence(e.reciprocal(inv1[:], sum1[n % 2][:]))
                    e.tensor_scalar(w2b[:], e1b[n % 2][:], inv1[:, 0:1], None,
                                    ALU.mult)
                    e.wait_ge(em.s("q2_e"), 2 * n + 2)
                    fence(e.reciprocal(inv2v[:], sum2[n % 2][:]))
                    fence(e.tensor_scalar(l2s[:], inv2v[:], lam_b[:, h:h + 1],
                                          -1.0, ALU.mult, ALU.mult))
                    if n >= 2:
                        e.wait_ge(em.s("q2_ed"), n - 1)
                    e.scalar_tensor_tensor(db[n % 2][:], e2b[:],
                                           l2s[:, 0:1], w2b[:], ALU.mult,
                                           ALU.add).then_inc(em.s("q2_d"), 1)
                    pref_evacs(n + 1)
                    e.wait_ge(em.s("q2_ed"), n + 1)
                    fence(e.reciprocal(invd[:], sumd[n % 2][:]))
                    if n >= 2:
                        e.wait_ge(em.s("q2_tp"), n - 1)
                    e.tensor_scalar(awb[n % 2][:], edb[n % 2][:], invd[:, 0:1],
                                    None, ALU.mult).then_inc(em.s("q2_aw"), 1)
                    e.wait_ge(em.s("q2_tp"), n + 1)
                    if n >= 2:
                        e.wait_ge(em.s("q2_av"), n - 1)
                    e.tensor_copy(awsb[n % 2][:], pstp_bf[:, 0:T]
                                  ).then_inc(em.s("q2_awe"), 1)
                    e.wait_ge(em.s("q2_av"), n + 1)
                    if qt == 0 and h >= 1:
                        e.wait_ge(em.s("q2_ost0"), 16 * h)
                    e.tensor_copy(oath[:, qt * P:(qt + 1) * P],
                                  ps_av).then_inc(em.s("q2_oev"), 1)
                    if qt == NTT - 1 and h < NH - 1:
                        e.wait_ge(em.s("q2_vt"), h + 2)
                        e.tensor_copy(vt[:], pstp_bf[:, 0:T]
                                      ).then_inc(em.s("q2_ve"), 1)


        em.val["q2_wl0"] = 48 * (NH // 2)
        em.val["q2_wl1"] = 48 * (NH // 2)
        em.val["q2_qg"] = 20 * NH
        em.val["q2_qe"] = 20 * NH
        em.val["q2_vt"] = NH
        em.val["q2_ve"] = NH
        em.val["q2_sc"] = 2 * NH * NTT
        em.val["q2_e"] = 2 * NH * NTT
        em.val["q2_d"] = NH * NTT
        em.val["q2_ed"] = NH * NTT
        em.val["q2_aw"] = NH * NTT
        em.val["q2_tp"] = NH * NTT
        em.val["q2_awe"] = NH * NTT
        em.val["q2_av"] = NH * NTT
        em.val["q2_oev"] = NH * NTT
        em.val["q2_fv"] = 4 * NH * NTT
        em.val["q2_ost0"] = 16 * NH
        em.val["q2_ost1"] = 0


def phase_q3(nc, em, oat_img, ones_bf, ones1_f, cst, nsc, om):
    em.roll()
    with ExitStack() as sp, nc.Block() as blk:
        for name in ["q3_ld0", "q3_ld1", "q3_ld2", "q3_ld3", "q3_sq", "q3_mm",
                     "q3_row", "q3_fs", "q3_bc", "q3_t", "q3_ap", "q3_st0",
                     "q3_st1"]:
            em.new_sem(name)
        sb = lambda n_, s_, d_: sp.enter_context(nc.sbuf_tensor(n_, s_, d_))
        oatt = [sb(f"q3o{k}", [P, T], BF16) for k in range(2)]
        sqa = [sb(f"q3sq{k}", [P, T], BF16) for k in range(2)]
        nrm = [sb(f"q3n{k}", [P, T], BF16) for k in range(2)]
        msrow = sb("msrow", [1, D], F32)[:]
        lnr = sb("lnr", [1, D], F32)[:]
        rstdr = sb("rstdr", [1, D], F32)[:]
        rows2 = sb("q3rows2", [1, D], F32)
        rstdr2 = rows2[:]
        tmp = [sb(f"q3tmp{k}", [P, T], F32) for k in range(2)]

        with ExitStack() as sps:
            ps_ms = sps.enter_context(nc.psum_tensor("ps_ms", [1, D], F32))

            @blk.sync
            def _(e):
                for ft in range(ND):
                    if ft >= 2:
                        e.wait_ge(em.s("q3_sq"), ft - 1)
                    e.dma_start(oatt[ft % 2][:], oat_img[ft]
                                ).then_inc(em.s(f"q3_ld{ft % 2}"), 16)

            @blk.scalar
            def _(e):
                for ft in range(ND):
                    e.wait_ge(em.s(f"q3_ld{ft % 2}"), 16 * (ft // 2 + 1))
                    if ft >= 2:
                        e.wait_ge(em.s("q3_mm"), ft - 1)
                    e.activation(sqa[ft % 2][:], oatt[ft % 2][:], AF.Square
                                 ).then_inc(em.s("q3_sq"), 1)
                e.wait_ge(em.s("q3_row"), 1)
                e.activation(lnr, msrow, AF.Ln, bias=cst[0:1, 2:3]
                             ).then_inc(em.s("q3_fs"), 1)
                e.wait_ge(em.s("q3_fs"), 1)
                e.activation(rstdr, lnr, AF.Exp, scale=-0.5
                             ).then_inc(em.s("q3_fs"), 1)
                e.wait_ge(em.s("q3_fs"), 2)
                e.nop().then_inc(em.s("q3_row"), 1)

            @blk.tensor
            def _(e):
                for ft in range(ND):
                    e.wait_ge(em.s("q3_sq"), ft + 1)
                    for tc in range(NTC):
                        inst = e.matmul(ps_ms[:, tc * TC:(tc + 1) * TC],
                                        ones_bf[:],
                                        sqa[ft % 2][:, tc * TC:(tc + 1) * TC],
                                        start=(ft == 0), stop=(ft == ND - 1))
                    inst.then_inc(em.s("q3_mm"), 1)

            @blk.vector
            def _(e):
                e.wait_ge(em.s("q3_mm"), ND)
                e.tensor_scalar(msrow, ps_ms[:], 1.0 / D, None, ALU.mult
                                ).then_inc(em.s("q3_row"), 1)
                e.wait_ge(em.s("q3_row"), 2)
                e.tensor_scalar(rstdr2, rstdr, om[0:1, 0:1], None,
                                ALU.mult).then_inc(em.s("q3_row"), 1)

            em.val["q3_sq"] = ND
            em.val["q3_mm"] = ND
            em.val["q3_row"] = 3
            em.val["q3_fs"] = 2
            em.val["q3_ld0"] = 16 * (ND // 2)
            em.val["q3_ld1"] = 16 * (ND // 2)

        with ExitStack() as sps2:
            ps_R = sps2.enter_context(nc.psum_tensor("ps_R", [P, D], F32))

            @blk.tensor
            def _(e):
                e.wait_ge(em.s("q3_row"), 3)
                for tc in range(NTC):
                    inst = e.matmul(ps_R[:, tc * TC:(tc + 1) * TC],
                                    ones1_f[0:1, :],
                                    rows2[0:1, tc * TC:(tc + 1) * TC],
                                    start=True, stop=True)
                inst.then_inc(em.s("q3_bc"), 1)

            @blk.sync
            def _(e):
                e.wait_ge(em.s("q3_mm"), ND)
                for ft in range(ND):
                    if ft >= 2:
                        e.wait_ge(em.s("q3_t"), ft - 1)
                    e.dma_start(oatt[ft % 2][:], oat_img[ft]
                                ).then_inc(em.s(f"q3_ld{2 + ft % 2}"), 16)

            @blk.vector
            def _(e):
                e.wait_ge(em.s("q3_bc"), 1)
                for ft in range(ND):
                    e.wait_ge(em.s(f"q3_ld{2 + ft % 2}"), 16 * (ft // 2 + 1))
                    if ft >= 2:
                        e.wait_ge(em.s("q3_ap"), ft - 1)
                    e.tensor_tensor(tmp[ft % 2][:], oatt[ft % 2][:], ps_R[:],
                                    ALU.mult).then_inc(em.s("q3_t"), 1)

            @blk.scalar
            def _(e):
                for ft in range(ND):
                    e.wait_ge(em.s("q3_t"), ft + 1)
                    if ft >= 2:
                        e.wait_ge(em.s(f"q3_st{ft % 2}"), 16 * (ft // 2))
                    e.activation(nrm[ft % 2][:], tmp[ft % 2][:], AF.Identity,
                                 scale=nsc[:, ft:ft + 1]).then_inc(em.s("q3_ap"), 1)

            @blk.gpsimd
            def _(e):
                for ft in range(ND):
                    e.wait_ge(em.s("q3_ap"), ft + 1)
                    e.dma_start(oat_img[ft], nrm[ft % 2][:]
                                ).then_inc(em.s(f"q3_st{ft % 2}"), 16)
                e.wait_ge(em.s("q3_st0"), 16 * (ND // 2))
                e.wait_ge(em.s("q3_st1"), 16 * (ND // 2))

            em.val["q3_bc"] = 1
            em.val["q3_t"] = ND
            em.val["q3_ap"] = ND
            em.val["q3_ld2"] = 16 * (ND // 2)
            em.val["q3_ld3"] = 16 * (ND // 2)
            em.val["q3_st0"] = 16 * (ND // 2)
            em.val["q3_st1"] = 16 * (ND // 2)


def phase_q4(nc, em, wo, f_img, bs_t, x1_img, xn2_img, oat_img, out, ident_f):
    em.roll()
    with ExitStack() as sp, nc.Block() as blk:
        for name in ["q4_xn", "q4_wl0", "q4_wl1", "q4_fl0", "q4_fl1",
                     "q4_x10", "q4_x11", "q4_mm", "q4_mx", "q4_cb", "q4_tp",
                     "q4_ev", "q4_st"]:
            em.new_sem(name)
        sb = lambda n_, s_, d_: sp.enter_context(nc.sbuf_tensor(n_, s_, d_))
        xn2w = sb("xn2w", [P, ND * TC], BF16)
        nrmw = sb("nrmw", [P, ND * TC], BF16)
        woi = [sb(f"woi{k}", [P, D], BF16) for k in range(2)]
        fii = [sb(f"fii{k}", [P, D], BF16) for k in range(2)]
        sx1 = [sb(f"sx1{k}", [P, TC], BF16) for k in range(2)]
        mxb = [sb(f"mxb{k}", [P, TC], F32) for k in range(2)]
        t4 = sb("t4", [P, TC], F32)
        ofb = [sb(f"ofb{k}", [P, TC], F32) for k in range(2)]
        stg = [sb(f"stg{k}", [P, D], F32) for k in range(4)]
        ps_wo = [sp.enter_context(nc.psum_tensor(f"ps_wo{k}", [P, TC], F32))
                 for k in range(2)]
        ps_F = [sp.enter_context(nc.psum_tensor(f"ps_F{k}", [P, TC], F32))
                for k in range(2)]
        ps_tr = sp.enter_context(nc.psum_tensor("ps_tr", [P, TC], F32))
        oat3 = nrmw[:].rearrange("p (c t) -> p c t", c=ND)
        xn2w3 = xn2w[:].rearrange("p (c t) -> p c t", c=ND)
        wo3 = wo.rearrange("(c p) e -> p c e", p=P)
        NSUB = TC // P

        @blk.sync
        def _(e):
            for tc in range(NTC):
                if tc >= 1:
                    e.wait_ge(em.s("q4_mm"), 2 * ND * tc)
                e.dma_start(xn2w3,
                            xn2_img.rearrange("p (c t) -> p c t", c=ND)
                            [:, :, tc * TC:(tc + 1) * TC]
                            ).then_inc(em.s("q4_xn"), 16)
                e.dma_start(oat3,
                            oat_img[:, :, tc * TC:(tc + 1) * TC]
                            .rearrange("c p t -> p c t")
                            ).then_inc(em.s("q4_xn"), 16)
                for et in range(ND):
                    m = tc * ND + et
                    if m >= 2:
                        e.wait_ge(em.s("q4_mm"), 2 * (m - 1))
                    e.dma_start(fii[et % 2][:].rearrange("p (c e) -> p c e", e=P),
                                f_img[:, :, et * P:(et + 1) * P]
                                .rearrange("c p e -> p c e")
                                ).then_inc(em.s(f"q4_fl{et % 2}"), 16)
                    if m >= 2:
                        e.wait_ge(em.s("q4_cb"), m - 1)
                    e.dma_start(sx1[et % 2][:],
                                x1_img.rearrange("p (c t) -> p c t", c=ND)
                                [:, et, tc * TC:(tc + 1) * TC]
                                ).then_inc(em.s(f"q4_x1{et % 2}"), 16)

        @blk.gpsimd
        def _(e):
            for tc in range(NTC):
                for et in range(ND):
                    m = tc * ND + et
                    if m >= 2:
                        e.wait_ge(em.s("q4_mm"), 2 * (m - 1))
                    e.dma_start(woi[et % 2][:].rearrange("p (c e) -> p c e", e=P),
                                wo3[:, :, et * P:(et + 1) * P]
                                ).then_inc(em.s(f"q4_wl{et % 2}"), 16)
                e.wait_ge(em.s("q4_ev"), (tc + 1) * ND * NSUB)
                for su in range(NSUB):
                    e.dma_start(out[(tc * NSUB + su) * P:(tc * NSUB + su + 1) * P, :],
                                stg[su][:]).then_inc(em.s("q4_st"), 16)
            e.wait_ge(em.s("q4_st"), 16 * NTC * NSUB)

        @blk.tensor
        def _(e):
            for tc in range(NTC):
                e.wait_ge(em.s("q4_xn"), 32 * (tc + 1))
                for et in range(ND):
                    m = tc * ND + et
                    if m >= 1:
                        mm = m - 1
                        e.wait_ge(em.s("q4_cb"), mm + 1)
                        if mm >= 1:
                            e.wait_ge(em.s("q4_ev"), mm * NSUB)
                        for su in range(NSUB):
                            inst = e.transpose(ps_tr[:, su * P:(su + 1) * P],
                                               ofb[mm % 2][:, su * P:(su + 1) * P],
                                               ident_f[:])
                        inst.then_inc(em.s("q4_tp"), 1)
                    e.wait_ge(em.s(f"q4_wl{et % 2}"), 16 * (m // 2 + 1))
                    e.wait_ge(em.s(f"q4_fl{et % 2}"), 16 * (m // 2 + 1))
                    if m >= 2:
                        e.wait_ge(em.s("q4_cb"), m - 1)
                    for fc in range(ND):
                        inst = e.matmul(ps_wo[et % 2][:],
                                        woi[et % 2][:, fc * P:(fc + 1) * P],
                                        oat3[:, fc, :],
                                        start=(fc == 0), stop=(fc == ND - 1))
                    inst.then_inc(em.s("q4_mm"), 1)
                    for dc in range(ND):
                        inst = e.matmul(ps_F[et % 2][:],
                                        fii[et % 2][:, dc * P:(dc + 1) * P],
                                        xn2w3[:, dc, :],
                                        start=(dc == 0), stop=(dc == ND - 1))
                    inst.then_inc(em.s("q4_mm"), 1)
            mm = NTC * ND - 1
            e.wait_ge(em.s("q4_cb"), mm + 1)
            e.wait_ge(em.s("q4_ev"), mm * NSUB)
            for su in range(NSUB):
                inst = e.transpose(ps_tr[:, su * P:(su + 1) * P],
                                   ofb[mm % 2][:, su * P:(su + 1) * P],
                                   ident_f[:])
            inst.then_inc(em.s("q4_tp"), 1)

        @blk.scalar
        def _(e):
            for tc in range(NTC):
                for et in range(ND):
                    m = tc * ND + et
                    e.wait_ge(em.s("q4_mm"), 2 * m + 2)
                    if m >= 2:
                        e.wait_ge(em.s("q4_cb"), m - 1)
                    e.activation(mxb[et % 2][:], ps_F[et % 2][:], AF.Identity,
                                 bias=bs_t[:, et:et + 1]).then_inc(em.s("q4_mx"), 1)

        @blk.vector
        def _(e):
            for tc in range(NTC):
                for et in range(ND):
                    m = tc * ND + et
                    if m >= 1:
                        mm = m - 1
                        tcc, ett = divmod(mm, ND)
                        e.wait_ge(em.s("q4_tp"), mm + 1)
                        if tcc >= 1 and ett == 0:
                            e.wait_ge(em.s("q4_st"), 16 * NSUB * tcc)
                        for su in range(NSUB):
                            inst = e.tensor_copy(
                                stg[su][:, ett * P:(ett + 1) * P],
                                ps_tr[:, su * P:(su + 1) * P])
                        inst.then_inc(em.s("q4_ev"), NSUB)
                    e.wait_ge(em.s("q4_mx"), m + 1)
                    e.wait_ge(em.s(f"q4_x1{et % 2}"), 16 * (m // 2 + 1))
                    if m >= 2:
                        e.wait_ge(em.s("q4_tp"), m - 1)
                    e.tensor_tensor(t4[:], ps_wo[et % 2][:], mxb[et % 2][:],
                                    ALU.add)
                    e.tensor_tensor(ofb[m % 2][:], t4[:], sx1[et % 2][:],
                                    ALU.add).then_inc(em.s("q4_cb"), 1)
            mm = NTC * ND - 1
            tcc, ett = divmod(mm, ND)
            e.wait_ge(em.s("q4_tp"), mm + 1)
            if ett == 0:
                e.wait_ge(em.s("q4_st"), 16 * NSUB * tcc)
            for su in range(NSUB):
                inst = e.tensor_copy(stg[su][:, ett * P:(ett + 1) * P],
                                     ps_tr[:, su * P:(su + 1) * P])
            inst.then_inc(em.s("q4_ev"), NSUB)

        em.val["q4_xn"] = 32 * NTC
        em.val["q4_wl0"] = 16 * NTC * ND // 2
        em.val["q4_wl1"] = 16 * NTC * ND // 2
        em.val["q4_fl0"] = 16 * NTC * ND // 2
        em.val["q4_fl1"] = 16 * NTC * ND // 2
        em.val["q4_x10"] = 16 * NTC * ND // 2
        em.val["q4_x11"] = 16 * NTC * ND // 2
        em.val["q4_mm"] = 2 * NTC * ND
        em.val["q4_mx"] = NTC * ND
        em.val["q4_cb"] = NTC * ND
        em.val["q4_tp"] = NTC * ND
        em.val["q4_ev"] = NSUB * NTC * ND
        em.val["q4_st"] = 16 * NTC * NSUB




def build_fused():
    nc = bass.Bass("TRN2", target_bir_lowering=False)

    x_in = nc.dram_tensor("x", [T, D], F32, kind="ExternalInput").ap()
    w_in = nc.dram_tensor("L0_attn_W_in", [D, D], F32, kind="ExternalInput").ap()
    b_in = nc.dram_tensor("L0_attn_b_in", [D], F32, kind="ExternalInput").ap()
    w_gate = nc.dram_tensor("L0_attn_W_gate", [D, D], F32, kind="ExternalInput").ap()
    b_gate = nc.dram_tensor("L0_attn_b_gate", [D], F32, kind="ExternalInput").ap()
    flam = nc.dram_tensor("L0_attn_forget_lambda", [D], F32, kind="ExternalInput").ap()
    gate_w = nc.dram_tensor("L0_ffn_gate_w", [IFF, D], I32, kind="ExternalInput").ap()
    gate_s = nc.dram_tensor("L0_ffn_gate_scale", [IFF], F32, kind="ExternalInput").ap()
    up_w = nc.dram_tensor("L0_ffn_up_w", [IFF, D], I32, kind="ExternalInput").ap()
    up_s = nc.dram_tensor("L0_ffn_up_scale", [IFF], F32, kind="ExternalInput").ap()
    down_w = nc.dram_tensor("L0_ffn_down_w", [D, IFF], I32, kind="ExternalInput").ap()
    down_s = nc.dram_tensor("L0_ffn_down_scale", [D], F32, kind="ExternalInput").ap()
    wq = nc.dram_tensor("L1_attn_Wq", [D, 2 * DH * NH], F32, kind="ExternalInput").ap()
    wk = nc.dram_tensor("L1_attn_Wk", [D, 2 * DH * NH], F32, kind="ExternalInput").ap()
    wv = nc.dram_tensor("L1_attn_Wv", [D, DH * NH], F32, kind="ExternalInput").ap()
    wo = nc.dram_tensor("L1_attn_Wo", [DH * NH, D], F32, kind="ExternalInput").ap()
    nsc_in = nc.dram_tensor("L1_attn_norm_scale", [D], F32, kind="ExternalInput").ap()
    lam_in = nc.dram_tensor("L1_attn_lam", [NH], F32, kind="ExternalInput").ap()
    factors = nc.dram_tensor("L1_ffn_factors", [8, D, D], F32, kind="ExternalInput").ap()
    bias_in = nc.dram_tensor("L1_ffn_bias", [D], F32, kind="ExternalInput").ap()

    out = nc.dram_tensor("out", [T, D], F32, kind="ExternalOutput").ap()
    x1_img = nc.dram_tensor("x1_img", [P, ND * T], BF16).ap()
    xn_img = nc.dram_tensor("xn_img", [P, ND * T], BF16).ap()
    st_img = nc.dram_tensor("st_img", [P, 2 * NTT], F32).ap()
    h_img = nc.dram_tensor("h_img", [ND, P, T], BF16).ap()
    cache_g = nc.dram_tensor("cache_g", [NIC, P, D], BF16).ap()
    cache_u = nc.dram_tensor("cache_u", [NIC, P, D], BF16).ap()
    cache_d = nc.dram_tensor("cache_d", [ND, P, IFF], BF16).ap()
    xn2_img = nc.dram_tensor("xn2_img", [P, ND * T], BF16).ap()
    oat_img = nc.dram_tensor("oat_img", [ND, P, T], BF16).ap()
    f_img = nc.dram_tensor("f_img", [ND, P, D], BF16).ap()

    with ExitStack() as st:
        em = Emit(nc, st)
        ident_f, ident_b, ones1_f, ones_bf, cst = make_identities(em, st, nc)

        with ExitStack() as st0:
            with nc.Block() as blk:
                ser = Ser(em, "p0")
                pst = st0.enter_context(nc.psum_tensor("p0ps", [P, P], F32))
                bh_gate = vec_table(ser, st, nc, "bh_gate", b_gate, ND, ident_f, pst, 0.5)
                bh_in = vec_table(ser, st, nc, "bh_in", b_in, ND, ident_f, pst, 0.5)
                lam_t = vec_table(ser, st, nc, "lam_t", flam, ND, ident_f, pst)
                gsc = vec_table(ser, st, nc, "gsc", gate_s, NIC, ident_f, pst)
                usc = vec_table(ser, st, nc, "usc", up_s, NIC, ident_f, pst)
                dsc = vec_table(ser, st, nc, "dsc", down_s, ND, ident_f, pst, 0.5)
                gbias = st.enter_context(nc.sbuf_tensor("gbias", [P, NIC], F32))
                ubias = st.enter_context(nc.sbuf_tensor("ubias", [P, NIC], F32))
                dbias = st.enter_context(nc.sbuf_tensor("dbias", [P, ND], F32))
                chalf = st.enter_context(nc.sbuf_tensor("chalf", [P, ND], F32))
                tmp_a = st0.enter_context(nc.sbuf_tensor("tmp_a", [P, ND], F32))
                tmp_b = st0.enter_context(nc.sbuf_tensor("tmp_b", [P, ND], F32))
                tmp_c = st0.enter_context(nc.sbuf_tensor("tmp_c", [P, ND], F32))
                ser.add("vector", lambda e: e.tensor_scalar(
                    gbias[:], gsc[:], -128.0, None, ALU.mult))
                ser.add("vector", lambda e: e.tensor_scalar(
                    ubias[:], usc[:], -128.0, None, ALU.mult))
                ser.add("vector", lambda e: e.tensor_scalar(
                    dbias[:], dsc[:], -128.0, None, ALU.mult))
                ser.add("scalar", lambda e: e.activation(tmp_a[:], lam_t[:], AF.Abs))
                ser.add("scalar", lambda e: e.activation(tmp_b[:], tmp_a[:], AF.Exp, scale=-1.0))
                ser.add("scalar", lambda e: e.activation(tmp_c[:], tmp_b[:], AF.Ln, bias=1.0))
                ser.add("scalar", lambda e: e.activation(tmp_a[:], lam_t[:], AF.Relu))
                ser.add("vector", lambda e: e.tensor_tensor(
                    tmp_b[:], tmp_c[:], tmp_a[:], ALU.add))
                ser.add("vector", lambda e: e.tensor_scalar(
                    chalf[:], tmp_b[:], -4.0, None, ALU.mult))
                ser.emit(blk, extra_wait=(em.s("s_id"), 2))

        with ExitStack() as st_xn:
            xn_big = st_xn.enter_context(nc.sbuf_tensor("xn_big", [P, ND * T], BF16))
            xn3 = xn_big[:].rearrange("p (c t) -> p c t", c=ND)
            with nc.named_scope("p1"):
                phase_p1(nc, em, x_in, xn_big, xn3, xn_img, ident_b, cst, st_img)
            with nc.named_scope("p2"):
                phase_p2(nc, em, w_in, w_gate, h_img, xn3, ident_b,
                         bh_in, bh_gate, chalf, cst)
        with nc.named_scope("prep"):
            phase_ffn_prep(nc, em, gate_w, up_w, down_w, cache_g, cache_u, cache_d,
                           gsc, gbias, usc, ubias, dsc, dbias, ident_b)
        with nc.named_scope("p3"):
            phase_p3(nc, em, x_in, xn_img, h_img, x1_img,
                     cache_g, cache_u, cache_d, ident_b)

    nc.reset()

    with ExitStack() as st:
        em = Emit(nc, st)
        ident_f, ident_b, ones1_f, ones_bf, cst = make_identities(em, st, nc, pfx="l1_")

        with ExitStack() as st0:
            with nc.Block() as blk:
                ser = Ser(em, "q0")
                pst = st0.enter_context(nc.psum_tensor("q0ps", [P, P], F32))
                nsc = vec_table(ser, st, nc, "nsc", nsc_in, ND, ident_f, pst)
                bs_t = vec_table(ser, st, nc, "bs_t", bias_in, ND, ident_f, pst)
                lam_row = st.enter_context(nc.sbuf_tensor("lam_row", [1, NH], F32))
                lam_b = st.enter_context(nc.sbuf_tensor("lam_b", [P, NH], F32))
                om = st.enter_context(nc.sbuf_tensor("om", [1, 1], F32))
                t_ml = st0.enter_context(nc.sbuf_tensor("t_ml", [1, 1], F32))
                t_ms = st0.enter_context(nc.sbuf_tensor("t_ms", [1, 1], F32))
                ser.add("sync", lambda e: e.dma_start(lam_row[:], lam_in.rearrange("(a b) -> a b", a=1)), dma=True)
                ser.add("tensor", lambda e: e.matmul(pst[:, 0:NH], ones1_f[0:1, :],
                                                     lam_row[:], start=True, stop=True))
                ser.add("vector", lambda e: e.tensor_copy(lam_b[:], pst[:, 0:NH]))
                ser.add("vector", lambda e: e.reduce_sum(t_ml[:], lam_row[:], axis=AX))
                ser.add("vector", lambda e: e.tensor_scalar(t_ms[:], t_ml[:], 1.0 / NH, None, ALU.mult))
                ser.add("vector", lambda e: e.tensor_scalar(om[:], t_ms[:], -1.0, 1.0, ALU.mult, ALU.add))
                ser.emit(blk, extra_wait=(em.s("s_id"), 2))

        em.roll()
        with ExitStack() as sf, nc.Block() as blk:
            for k in range(16):
                em.new_sem(f"q0f_ld{k}")
            for name in ["q0f_sum", "q0f_st"]:
                em.new_sem(name)
            fb = [sf.enter_context(nc.sbuf_tensor(f"fb{k}", [P, D], F32))
                  for k in range(16)]
            fta = sf.enter_context(nc.sbuf_tensor("fta", [P, D], F32))
            ftb = sf.enter_context(nc.sbuf_tensor("ftb", [P, D], F32))
            sFb = [sf.enter_context(nc.sbuf_tensor(f"sFb{k}", [P, D], BF16))
                   for k in range(2)]

            @blk.sync
            def _(e):
                for dc in range(ND):
                    for r in range(8):
                        k = (dc % 2) * 8 + r
                        if dc >= 2:
                            e.wait_ge(em.s("q0f_sum"), dc - 1)
                        e.dma_start(fb[k][:], factors[r, dc * P:(dc + 1) * P, :]
                                    ).then_inc(em.s(f"q0f_ld{k}"), 16)

            @blk.vector
            def _(e):
                for dc in range(ND):
                    ks = (dc % 2) * 8
                    for r in range(8):
                        e.wait_ge(em.s(f"q0f_ld{ks + r}"), 16 * (dc // 2 + 1))
                    e.tensor_tensor(fta[:], fb[ks][:], fb[ks + 1][:], ALU.add)
                    e.tensor_tensor(ftb[:], fta[:], fb[ks + 2][:], ALU.add)
                    e.tensor_tensor(fta[:], ftb[:], fb[ks + 3][:], ALU.add)
                    e.tensor_tensor(ftb[:], fta[:], fb[ks + 4][:], ALU.add)
                    e.tensor_tensor(fta[:], ftb[:], fb[ks + 5][:], ALU.add)
                    e.tensor_tensor(ftb[:], fta[:], fb[ks + 6][:], ALU.add)
                    e.tensor_tensor(fta[:], ftb[:], fb[ks + 7][:], ALU.add)
                    if dc >= 2:
                        e.wait_ge(em.s("q0f_st"), 16 * (dc - 1))
                    e.tensor_scalar(sFb[dc % 2][:], fta[:], 8.0 ** -0.5, None,
                                    ALU.mult).then_inc(em.s("q0f_sum"), 1)

            @blk.gpsimd
            def _(e):
                for dc in range(ND):
                    e.wait_ge(em.s("q0f_sum"), dc + 1)
                    e.dma_start(f_img[dc], sFb[dc % 2][:]
                                ).then_inc(em.s("q0f_st"), 16)
                e.wait_ge(em.s("q0f_st"), 16 * ND)

            em.val["q0f_sum"] = ND
            em.val["q0f_st"] = 16 * ND
            for k in range(16):
                em.val[f"q0f_ld{k}"] = 16 * (ND // 2)

        xn2_big = st.enter_context(nc.sbuf_tensor("xn2_big", [P, ND * T], BF16))
        xn23 = xn2_big[:].rearrange("p (c t) -> p c t", c=ND)
        with nc.named_scope("q1"):
            phase_q1(nc, em, x1_img, xn2_big, xn23, xn2_img, ones_bf, ones1_f, cst)
        with nc.named_scope("q2"):
            phase_q2(nc, em, wq, wk, wv, xn23, oat_img, lam_b, ident_b)
        with nc.named_scope("q3"):
            phase_q3(nc, em, oat_img, ones_bf, ones1_f, cst, nsc, om)
        with nc.named_scope("q4"):
            phase_q4(nc, em, wo, f_img, bs_t, x1_img, xn2_img, oat_img, out,
                     ident_f)

    return nc

def _split_inputs(inputs):
    keys = ["L0_attn_W_in", "L0_attn_b_in", "L0_attn_W_gate", "L0_attn_b_gate",
            "L0_attn_forget_lambda", "L0_ffn_gate_w", "L0_ffn_gate_scale",
            "L0_ffn_up_w", "L0_ffn_up_scale", "L0_ffn_down_w",
            "L0_ffn_down_scale"]
    shared = {k: np.ascontiguousarray(inputs[k]) for k in keys}
    x = inputs["L0_attn_x"]
    return [dict(shared, x=np.ascontiguousarray(x[b])) for b in range(8)]


def kernel(**inputs):
    nc = build_fused()
    keys = ["L0_attn_W_in", "L0_attn_b_in", "L0_attn_W_gate", "L0_attn_b_gate",
            "L0_attn_forget_lambda", "L0_ffn_gate_w", "L0_ffn_gate_scale",
            "L0_ffn_up_w", "L0_ffn_up_scale", "L0_ffn_down_w",
            "L0_ffn_down_scale", "L1_attn_Wq", "L1_attn_Wk", "L1_attn_Wv",
            "L1_attn_Wo", "L1_attn_norm_scale", "L1_attn_lam",
            "L1_ffn_factors", "L1_ffn_bias"]
    shared = {k: np.ascontiguousarray(inputs[k]) for k in keys}
    x = inputs["L0_attn_x"]
    in_maps = [dict(shared, x=np.ascontiguousarray(x[b])) for b in range(8)]
    res = run_bass_kernel_spmd(nc, in_maps, core_ids=list(range(8)))
    return np.stack([np.asarray(res.results[b]["out"]) for b in range(8)],
                    axis=0)
